# version 22
# baseline (speedup 1.0000x reference)
import numpy as np
import concourse.bass as bass
import concourse.mybir as mybir
from concourse.bass_utils import run_bass_kernel_spmd
from concourse.bass_core_rust import IndirectOffsetOnAxis

F32 = mybir.dt.float32
BF16 = mybir.dt.bfloat16
I32 = mybir.dt.int32
I16 = mybir.dt.int16
U32 = mybir.dt.uint32
A = mybir.AluOpType
AF = mybir.ActivationFunctionType
AX = mybir.AxisListType

NCORE = 8
B, L, D = 4, 4096, 1024
T = B * L
NT = T // NCORE
TT = NT // 128
DC = D // 128
E, KTOP, CAP = 64, 8, 1024
EPC = E // NCORE
FME = 512
FI = 2816
NH, DK = 4, 256
NCH = L // 128
LN_EPS, ATT_EPS, RMS_EPS = 1e-5, 1e-6, 1e-5
NEWTON = 6
TRASH = E * CAP

ENGINES = ("tensor", "vector", "scalar", "gpsimd", "sync")
DMA_ENGINES = ("sync", "scalar", "gpsimd", "vector", "tensor")


class Trk:
    def __init__(self, nc):
        self.nc = nc
        self._ctx = []
        self.sem = {}
        self.cnt = {}
        for e in ENGINES:
            self.sem[e] = self._enter(nc.semaphore(f"s_{e}"))
            self.cnt[e] = 0
        self.DMA_POOL = 12
        self.dma_slot = {}
        for e in DMA_ENGINES:
            for i in range(self.DMA_POOL):
                self.sem[f"dma_{e}{i}"] = self._enter(nc.semaphore(f"d_{e}{i}"))
                self.cnt[f"dma_{e}{i}"] = 0
            self.dma_slot[e] = 0
        self.water = {e: {} for e in ENGINES}
        self.writer = {}
        self.readers = {}
        self.prog = {e: [] for e in ENGINES}
        self.n_inst = 0
        import os
        self.no_self_wait = os.environ.get("KB_NOSELFWAIT", "0") == "1"

    def sbuf(self, name, shape, dtype):
        return self._enter(self.nc.sbuf_tensor(name, shape, dtype))

    def psum(self, name, shape, dtype=F32):
        return self._enter(self.nc.psum_tensor(name, shape, dtype))

    def _enter(self, cm):
        v = cm.__enter__()
        self._ctx.append(cm)
        return v

    @staticmethod
    def _span(x):
        if not hasattr(x, "tensor"):
            try:
                es = mybir.dt.size(x.dtype)
                import math as _m
                tot = es
                for d in x.shape[1:]:
                    tot *= d
                return (x.name, 0, max(tot, es * int(np.prod(x.shape))))
            except Exception:
                return (x.name, 0, 1 << 60)
        tensor = x.tensor
        name = tensor.name
        try:
            es = mybir.dt.size(x.dtype)
            rowlen = 1
            for d in tensor.shape[1:]:
                rowlen *= d
            from concourse.bass_primitives import MemorySpace as _MS
            flat = x.space != _MS.SBUF and x.space != _MS.PSUM
            dims = list(x.ap)
            if flat:
                lo = x.offset
                ext = 0
                for st, ct in dims:
                    ext += abs(int(st)) * (int(ct) - 1)
                return (name, lo * es, (lo + ext + 1) * es)
            lo = x.offset % rowlen if rowlen else 0
            ext = 0
            for st, ct in dims[1:]:
                ext += abs(int(st)) * (int(ct) - 1)
            return (name, lo * es, (lo + ext + 1) * es)
        except Exception:
            return (name, 0, 1 << 60)

    def _wait(self, eng, dep):
        sname, val = dep
        if val <= 0 or self.water[eng].get(sname, 0) >= val:
            return
        if self.no_self_wait and sname == eng:
            self.water[eng][sname] = val
            return
        self.water[eng][sname] = val
        sem = self.sem[sname]
        self.prog[eng].append(lambda EE, sem=sem, val=val: EE.wait_ge(sem, val))

    def op(self, eng, fn, reads=(), writes=(), dma=False):
        rsp = [self._span(x) for x in reads]
        wsp = [self._span(x) for x in writes]
        for name, lo, hi in rsp:
            for wlo, whi, dep in self.writer.get(name, ()):
                if wlo < hi and lo < whi:
                    self._wait(eng, dep)
        for name, lo, hi in wsp:
            for wlo, whi, dep in self.writer.get(name, ()):
                if wlo < hi and lo < whi:
                    self._wait(eng, dep)
            for rlo, rhi, dep in self.readers.get(name, ()):
                if rlo < hi and lo < rhi:
                    self._wait(eng, dep)
        if dma:
            i = self.dma_slot[eng]
            self.dma_slot[eng] = (i + 1) % self.DMA_POOL
            sname = f"dma_{eng}{i}"
            self._wait(eng, (sname, self.cnt[sname]))
            self.cnt[sname] += 16
            val = self.cnt[sname]
            inc = 16
        else:
            sname = eng
            self.cnt[sname] += 1
            val = self.cnt[sname]
            inc = 1
        sem = self.sem[sname]

        def run(EE, fn=fn, sem=sem, inc=inc):
            fn(EE).then_inc(sem, inc)

        self.prog[eng].append(run)
        dep = (sname, val)
        for name, lo, hi in rsp:
            self.readers.setdefault(name, []).append((lo, hi, dep))
        for name, lo, hi in wsp:
            def upd(entries):
                out = []
                for elo, ehi, edep in entries:
                    if elo >= hi or ehi <= lo:
                        out.append((elo, ehi, edep))
                        continue
                    if elo < lo:
                        out.append((elo, lo, edep))
                    if ehi > hi:
                        out.append((hi, ehi, edep))
                return out
            self.writer[name] = upd(self.writer.get(name, []))
            self.writer[name].append((lo, hi, dep))
            self.readers[name] = upd(self.readers.get(name, []))
        self.n_inst += 1
        return dep

    def wait_all_dma(self, eng):
        def w(EE):
            for i in range(self.DMA_POOL):
                sn = f"dma_{eng}{i}"
                if self.cnt[sn] > 0:
                    EE.wait_ge(self.sem[sn], self.cnt[sn])

        self.prog[eng].append(w)

    def finalize(self, block):
        for e in ENGINES:
            seq = self.prog[e]
            if not seq:
                continue

            def make(seq):
                def body(EE):
                    for f in seq:
                        f(EE)

                return body

            getattr(block, e)(make(seq))

    def close(self):
        for cm in reversed(self._ctx):
            cm.__exit__(None, None, None)


class KB:
    def __init__(self, debug_taps=()):
        self.nc = bass.Bass("TRN2", target_bir_lowering=False, debug=False)
        self.debug_taps = dict(debug_taps)
        self.taps = {}
        self.declare_io()
        self.t = Trk(self.nc)
        self.alloc()
        with self.nc.Block() as block:
            self.emit()
            for e in DMA_ENGINES:
                self.t.wait_all_dma(e)
            self.t.finalize(block)
        self.t.close()

    def declare_io(self):
        nc = self.nc

        def din(name, shape, dtype=F32):
            ap = nc.dram_tensor(name, shape, dtype, kind="ExternalInput").ap()
            setattr(self, name, ap)

        din("x_tm", [NT, D])
        din("w0", [D, 3 * D])
        din("gwt", [D, E])
        din("ebb", [128, E])
        din("w1t", [EPC, D, FME])
        din("w3t", [EPC, D, FME])
        din("w2t", [EPC, FME, D])
        din("sgt", [D, D])
        din("sut", [D, D])
        din("sdt", [D, D])
        din("wq", [D, D])
        din("wk", [D, D])
        din("wv", [D, D])
        din("wbeta", [D, NH])
        din("wo", [D, D])
        din("gnb", [128, D])
        din("wg_nat", [FI, 128])
        din("wu_nat", [FI, 128])
        din("wd_nat", [D, FI // 8])
        din("identf", [128, 128])
        din("identb", [128, 128], BF16)
        din("ident2b", [128, 128], BF16)
        din("strictU", [128, 128])
        din("trilxb", [128, 128], BF16)
        din("triuib", [128, 128], BF16)
        din("iotap", [128, 1])
        din("iotae", [128, E])
        din("iotag", [128, 8])
        din("onesc", [128, 1])
        din("onescb", [128, 1], BF16)
        din("ones1", [1, 128])
        din("prevm", [NCORE, 1])
        din("tokoff", [128, 1])
        din("actc", [128, 8])
        din("qkvidx", [128, 96], U32)
        din("betaidx", [128, 32], U32)
        din("oidx", [128, 32], U32)

        self.y = nc.dram_tensor("y", [NT, D], F32, kind="ExternalOutput").ap()

        def dint(name, shape, dtype=F32, space="Local"):
            ap = nc.dram_tensor(name, shape, dtype, kind="Internal", addr_space=space).ap()
            setattr(self, name, ap)

        dint("agx_in", [NT, D], BF16)
        dint("agx_out", [T, D], BF16, "Shared")
        dint("kv_in", [D + 1, D])
        dint("kv_out", [D + 1, D])
        dint("cnt_in", [1, E])
        dint("cnt_out", [NCORE, E])
        dint("tab_in", [E * CAP + 128, 2])
        dint("tab_out", [E * CAP + 128, 2], F32, "Shared")
        dint("ye_in", [EPC * CAP, D], BF16)
        dint("ye_out", [E * CAP + 128, D], BF16, "Shared")
        dint("x1_dram", [NT, D])
        dint("s_dram", [NT, D], BF16)
        dint("s1_dram", [NT, D], BF16)
        dint("xn1f_dram", [NT, D])
        dint("xn1b_dram", [NT, D], BF16)
        dint("qkvb_in", [NT, 3 * D], BF16)
        dint("qkvb_out", [2 * NT, 3 * D], BF16)
        dint("bb_in", [NT, 16], BF16)
        dint("bb_out", [2 * NT, 16], BF16)
        dint("o_in", [L, 2 * DK], BF16)
        dint("o_out", [2 * L, 2 * DK], BF16)
        dint("wgq_nat", [FI, 128], BF16)
        dint("wuq_nat", [FI, 128], BF16)
        dint("wdq_a", [D, 128], BF16)
        dint("wdq_b", [D, 128], BF16)
        dint("wdq_c", [D, 128], BF16)
        dint("wgu_in", [128, 2 * FI], BF16)
        dint("wgu_out", [D, 2 * FI], BF16, "Shared")
        dint("wd_in", [FI // 8, D], BF16)
        dint("wd_out", [FI, D], BF16, "Shared")
        dint("xq_dram", [NT, D], BF16)
        dint("xe_nat", [CAP, D], BF16)
        dint("xe_nat2", [CAP, D], BF16)
        dint("qkv_my", [L, 3 * 512], BF16)
        dint("beta_my", [L, 8], BF16)
        dint("dsc", [64 * 128, 1280], BF16)
        dint("o_my", [NT, D], BF16)
        dint("hq_dram", [NT, FI], BF16)

        for name, (shape, dtype) in self.debug_taps.items():
            self.taps[name] = nc.dram_tensor("tap_" + name, shape, dtype, kind="ExternalOutput").ap()

    def tap(self, name, src_ap, reads):
        if name in self.taps:
            dst = self.taps[name]
            self.t.op("sync", lambda EE: EE.dma_start(out=dst, in_=src_ap),
                      reads=reads, writes=[dst], dma=True)

    def tap_tiles(self, name, tiled_ap, reads, rows=128):
        if name in self.taps:
            dst = self.taps[name]
            ntile = dst.shape[0] // rows
            for i in range(ntile):
                self.t.op("sync", (lambda i: lambda EE: EE.dma_start(
                    out=dst[i * rows:(i + 1) * rows, :], in_=tiled_ap[:, i, :]))(i),
                    reads=reads, writes=[dst], dma=True)

    def alloc(self):
        t = self.t
        self.ps = [t.psum(f"ps{i}", [128, 512]) for i in range(6)]
        self.pw = t.psum("pw", [128, 1024])
        for nm in ("identf", "identb", "ident2b", "strictU", "trilxb", "triuib",
                   "iotap", "iotae", "iotag", "onesc", "onescb", "ones1", "prevm", "ebb",
                   "tokoff", "actc", "gnb"):
            ap = getattr(self, nm)
            setattr(self, "c_" + nm, t.sbuf("c_" + nm, list(ap.shape), ap.dtype))
        self.arA = t.sbuf("arA", [128, 16384], BF16)
        self.arB = t.sbuf("arB", [128, 16384], BF16)
        self.arC = t.sbuf("arC", [128, 16384], BF16)
        self.arD = t.sbuf("arD", [128, 16384], BF16)
        self.arW = t.sbuf("arW", [128, 8192], BF16)
        self.arE = t.sbuf("arE", [128, 8192], BF16)
        self.arF = t.sbuf("arF", [128, 4096], F32)
        self.arG = t.sbuf("arG", [128, 2048], F32)
        self.mmean = t.sbuf("mmean", [128, TT], F32)
        self.rstd = t.sbuf("rstd", [128, TT], F32)
        self.sm1 = t.sbuf("sm1", [128, TT], F32)
        self.sm2 = t.sbuf("sm2", [128, TT], F32)
        self.zcol = t.sbuf("zcol", [128, TT], F32)
        self.small = t.sbuf("small", [128, 2048], F32)
        self.smi = t.sbuf("smi", [128, 128], U32)
        self.smh = t.sbuf("smh", [128, 64], I16)

    def rsqrt_newton(self, v, out, eps, vt, ot, n):
        t = self.t
        tmp = self.sm2 if ot is not self.sm2 and vt is not self.sm2 else self.sm1
        tt = tmp
        t.op("vector", lambda EE: EE.tensor_scalar(v, v, eps, None, A.add), reads=[vt], writes=[vt])
        t.op("vector", lambda EE: EE.reciprocal(out, v), reads=[vt], writes=[ot])
        t.op("scalar", lambda EE: EE.activation(out=out, in_=out, func=AF.Sqrt), reads=[ot], writes=[ot])
        t.op("vector", lambda EE: EE.tensor_tensor(tt[:, 0:n], out, out, A.mult), reads=[ot], writes=[tt])
        t.op("vector", lambda EE: EE.tensor_tensor(tt[:, 0:n], tt[:, 0:n], v, A.mult), reads=[tt, vt], writes=[tt])
        t.op("vector", lambda EE: EE.tensor_scalar(tt[:, 0:n], tt[:, 0:n], -0.5, 1.5, A.mult, A.add), reads=[tt], writes=[tt])
        t.op("vector", lambda EE: EE.tensor_tensor(out, out, tt[:, 0:n], A.mult), reads=[tt, ot], writes=[ot])

    PHASES = ("consts", "ln0", "l0_attn", "routing", "moe", "l0_epilogue",
              "ln1_proj", "delta", "s1", "l1_ffn")

    def emit(self):
        import os
        upto = int(os.environ.get("KB_PHASES", "99"))
        self.emit_consts()
        self.emit_ln0()
        if upto >= 2:
            self.emit_l0_attn()
        if upto >= 3:
            self.emit_routing()
        if upto >= 4:
            self.emit_moe()
        if upto >= 5:
            self.emit_l0_epilogue()
        if upto >= 6:
            self.emit_ln1_proj()
        if upto >= 7:
            self.emit_wquant()
            self.emit_delta()
        if upto >= 8:
            self.emit_s1_gu()
        self.emit_l1_down()

    def emit_consts(self):
        t = self.t
        for nm in ("identf", "identb", "ident2b", "strictU", "trilxb", "triuib",
                   "iotap", "iotae", "iotag", "onesc", "onescb", "ones1", "prevm", "ebb",
                   "tokoff", "actc", "gnb"):
            ap = getattr(self, nm)
            sb = getattr(self, "c_" + nm)
            t.op("sync", (lambda sb, ap: lambda EE: EE.dma_start(out=sb[:], in_=ap))(sb, ap),
                 reads=[ap], writes=[sb], dma=True)

    def emit_ln0(self):
        t = self.t
        xtile = self.arF
        msum, sqs = self.sm1, self.sm2
        scr = self.arG
        xview = xtile.ap().rearrange("p (k d) -> p k d", d=1024)
        for i in range(TT):
            xa = xview[:, i % 4, :]
            t.op("sync", (lambda i, xa: lambda EE: EE.dma_start(
                out=xa, in_=self.x_tm[i * 128:(i + 1) * 128, :]))(i, xa),
                reads=[self.x_tm], writes=[self.arF], dma=True)
            t.op("scalar", (lambda i, xa: lambda EE: EE.activation(
                out=scr[:, 0:1024], in_=xa, func=AF.Identity, accum_out=msum[:, i:i + 1]))(i, xa),
                reads=[self.arF], writes=[self.arG, msum])
            t.op("scalar", (lambda i, xa: lambda EE: EE.activation(
                out=scr[:, 1024:2048], in_=xa, func=AF.Square, accum_out=sqs[:, i:i + 1]))(i, xa),
                reads=[self.arF], writes=[self.arG, sqs])
        mmean, rstd = self.mmean, self.rstd
        t.op("vector", lambda EE: EE.tensor_scalar(mmean[:], msum[:], 1.0 / D, None, A.mult),
             reads=[msum], writes=[mmean])
        t.op("vector", lambda EE: EE.tensor_tensor(self.zcol[:], mmean[:], mmean[:], A.mult),
             reads=[mmean], writes=[self.zcol])
        t.op("vector", lambda EE: EE.scalar_tensor_tensor(sqs[:], sqs[:], 1.0 / D, self.zcol[:], A.mult, A.subtract),
             reads=[sqs, self.zcol], writes=[sqs])
        self.rsqrt_newton(sqs[:], rstd[:], LN_EPS, sqs, rstd, TT)
        xn_tm = self.arD.ap().rearrange("p (i d) -> p i d", d=1024)
        self.xn_tm = xn_tm
        for i in range(TT):
            xa = xview[:, i % 4, :]
            t.op("sync", (lambda i, xa: lambda EE: EE.dma_start(
                out=xa, in_=self.x_tm[i * 128:(i + 1) * 128, :]))(i, xa),
                reads=[self.x_tm], writes=[self.arF], dma=True)
            t.op("vector", (lambda i, xa: lambda EE: EE.tensor_scalar(
                xn_tm[:, i, :], xa, mmean[:, i:i + 1], rstd[:, i:i + 1], A.subtract, A.mult))(i, xa),
                reads=[self.arF, mmean, rstd], writes=[self.arD])
            t.op("gpsimd", (lambda i: lambda EE: EE.dma_start(
                out=self.agx_in[i * 128:(i + 1) * 128, :], in_=xn_tm[:, i, :]))(i),
                reads=[self.arD], writes=[self.agx_in], dma=True)
        t.op("gpsimd", lambda EE: EE.collective_compute(
            "AllGather", A.bypass, replica_groups=[list(range(NCORE))],
            ins=[self.agx_in.opt()], outs=[self.agx_out.opt()]),
            reads=[self.agx_in], writes=[self.agx_out])
        xn_fm = self.arB.ap().rearrange("p (c n) -> p c n", n=NT)
        self.xn_fm = xn_fm
        for c in range(DC):
            t.op("scalar", (lambda c: lambda EE: EE.dma_start_transpose(
                xn_fm[:, c, :], self.agx_in[:, c * 128:(c + 1) * 128]))(c),
                reads=[self.agx_in], writes=[self.arB], dma=True)
        self.tap_tiles("xn_tm", xn_tm, [self.arD])

    def emit_l0_attn(self):
        t = self.t
        xn_fm = self.xn_fm
        qp_fm = self.arA.ap().rearrange("p (c n) -> p c n", n=NT)
        kp_tm = self.arC.ap().rearrange("p (i d) -> p i d", d=1024)
        v_tm = self.arD.ap().rearrange("p (i d) -> p i d", d=1024)
        wbuf = self.arW.ap().rearrange("p (c n) -> p c n", n=1024)
        ps = self.ps
        scr = self.arG
        for c in range(DC):
            t.op("gpsimd", (lambda c: lambda EE: EE.dma_start(
                out=wbuf[:, c, :], in_=self.w0[c * 128:(c + 1) * 128, 0:1024]))(c),
                reads=[self.w0], writes=[self.arW], dma=True)
        for qc in range(DC):
            for n in range(4):
                p = ps[n % 2]
                sl = slice(n * 512, (n + 1) * 512)
                sa = scr[:, (n % 2) * 512:(n % 2) * 512 + 512]
                sb = scr[:, 1024 + (n % 2) * 512:1024 + (n % 2) * 512 + 512]
                for c in range(DC):
                    t.op("tensor", (lambda qc, c, sl, p: lambda EE: EE.matmul(
                        p[:], wbuf[:, c, qc * 128:(qc + 1) * 128], xn_fm[:, c, sl],
                        start=(c == 0), stop=(c == DC - 1)))(qc, c, sl, p),
                        reads=[wbuf[:, c, qc * 128:(qc + 1) * 128], xn_fm[:, c, sl]], writes=[p])
                t.op("vector", (lambda p, sa: lambda EE: EE.tensor_scalar(
                    sa, p[:], 0.0, None, A.min))(p, sa), reads=[p], writes=[sa])
                t.op("scalar", (lambda sa, sb: lambda EE: EE.activation(out=sb, in_=sa, func=AF.Exp))(sa, sb),
                     reads=[sa], writes=[sb])
                t.op("vector", (lambda qc, sl, p, sb: lambda EE: EE.scalar_tensor_tensor(
                    qp_fm[:, qc, sl], p[:], 0.0, sb, A.max, A.add))(qc, sl, p, sb),
                    reads=[p, sb], writes=[qp_fm[:, qc, sl]])
        for c in range(DC):
            t.op("gpsimd", (lambda c: lambda EE: EE.dma_start(
                out=wbuf[:, c, :], in_=self.w0[c * 128:(c + 1) * 128, 1024:2048]))(c),
                reads=[self.w0, self.arW], writes=[self.arW], dma=True)
        for i in range(TT):
            for n in range(2):
                p = ps[2 + n]
                sl = slice(n * 512, (n + 1) * 512)
                sa = scr[:, n * 512:n * 512 + 512]
                sb = scr[:, 1024 + n * 512:1024 + n * 512 + 512]
                for c in range(DC):
                    t.op("tensor", (lambda i, c, sl, p: lambda EE: EE.matmul(
                        p[:], xn_fm[:, c, i * 128:(i + 1) * 128], wbuf[:, c, sl],
                        start=(c == 0), stop=(c == DC - 1)))(i, c, sl, p),
                        reads=[xn_fm[:, c, i * 128:(i + 1) * 128], wbuf[:, c, sl]], writes=[p])
                t.op("vector", (lambda p, sa: lambda EE: EE.tensor_scalar(
                    sa, p[:], 0.0, None, A.min))(p, sa), reads=[p], writes=[sa])
                t.op("scalar", (lambda sa, sb: lambda EE: EE.activation(out=sb, in_=sa, func=AF.Exp))(sa, sb),
                     reads=[sa], writes=[sb])
                t.op("vector", (lambda i, n, p, sb: lambda EE: EE.scalar_tensor_tensor(
                    kp_tm[:, i, n * 512:(n + 1) * 512], p[:], 0.0, sb, A.max, A.add))(i, n, p, sb),
                    reads=[p, sb], writes=[kp_tm[:, i, n * 512:(n + 1) * 512]])
        for c in range(DC):
            t.op("gpsimd", (lambda c: lambda EE: EE.dma_start(
                out=wbuf[:, c, :], in_=self.w0[c * 128:(c + 1) * 128, 2048:3072]))(c),
                reads=[self.w0, self.arW], writes=[self.arW], dma=True)
        for i in range(TT):
            for n in range(2):
                p = ps[2 + n]
                sl = slice(n * 512, (n + 1) * 512)
                for c in range(DC):
                    t.op("tensor", (lambda i, c, sl, p: lambda EE: EE.matmul(
                        p[:], xn_fm[:, c, i * 128:(i + 1) * 128], wbuf[:, c, sl],
                        start=(c == 0), stop=(c == DC - 1)))(i, c, sl, p),
                        reads=[xn_fm[:, c, i * 128:(i + 1) * 128], wbuf[:, c, sl]], writes=[p])
                t.op("vector", (lambda i, n, p: lambda EE: EE.tensor_copy(
                    v_tm[:, i, n * 512:(n + 1) * 512], p[:]))(i, n, p),
                    reads=[p], writes=[v_tm[:, i, n * 512:(n + 1) * 512]])
        pw = self.pw
        p5 = ps[5]
        for kc in range(DC):
            for nh in range(2):
                sl = slice(nh * 512, (nh + 1) * 512)
                for i in range(TT):
                    t.op("tensor", (lambda kc, i, sl: lambda EE: EE.matmul(
                        pw[:, sl], kp_tm[:, i, kc * 128:(kc + 1) * 128], v_tm[:, i, sl],
                        start=(i == 0), stop=(i == TT - 1)))(kc, i, sl),
                        reads=[self.arC, self.arD], writes=[pw])
            for i in range(TT):
                t.op("tensor", (lambda kc, i: lambda EE: EE.matmul(
                    p5[:, 0:1], kp_tm[:, i, kc * 128:(kc + 1) * 128], self.c_onescb[:],
                    start=(i == 0), stop=(i == TT - 1)))(kc, i),
                    reads=[kp_tm[:, i, kc * 128:(kc + 1) * 128], self.c_onescb], writes=[p5])
            sa = scr[:, (kc % 2) * 1024:(kc % 2) * 1024 + 1024]
            sb = scr[:, 2046 + (kc % 2):2047 + (kc % 2)]
            t.op("vector", (lambda kc, sa: lambda EE: EE.tensor_copy(sa, pw[:]))(kc, sa),
                 reads=[pw], writes=[sa])
            t.op("vector", (lambda kc, sb: lambda EE: EE.tensor_copy(sb, p5[:, 0:1]))(kc, sb),
                 reads=[p5], writes=[sb])
            t.op("sync", (lambda kc, sa: lambda EE: EE.dma_start(
                out=self.kv_in[kc * 128:(kc + 1) * 128, :], in_=sa))(kc, sa),
                reads=[sa], writes=[self.kv_in[kc * 128:(kc + 1) * 128, :]], dma=True)
            t.op("sync", (lambda kc, sb: lambda EE: EE.dma_start(
                out=self.kv_in[D:D + 1, kc * 128:(kc + 1) * 128].rearrange("one p -> p one"),
                in_=sb))(kc, sb),
                reads=[sb], writes=[self.kv_in[D:D + 1, kc * 128:(kc + 1) * 128]], dma=True)
        t.op("gpsimd", lambda EE: EE.collective_compute(
            "AllReduce", A.add, replica_groups=[[0, 1], [2, 3], [4, 5], [6, 7]],
            ins=[self.kv_in.opt()], outs=[self.kv_out.opt()]),
            reads=[self.kv_in], writes=[self.kv_out])
        self.tap("kvout", self.kv_out, [self.kv_out])
        self.tap("kvin", self.kv_in, [self.kv_in])
        kvb = self.arC.ap().rearrange("p (c n) -> p c n", n=1024)
        ksc = self.sm1
        for kc in range(DC):
            t.op("gpsimd", (lambda kc: lambda EE: EE.dma_start(
                out=kvb[:, kc, :], in_=self.kv_out[kc * 128:(kc + 1) * 128, :]))(kc),
                reads=[self.kv_out, self.arC], writes=[self.arC], dma=True)
            t.op("gpsimd", (lambda kc: lambda EE: EE.dma_start(
                out=ksc[:, kc:kc + 1],
                in_=self.kv_out[D:D + 1, kc * 128:(kc + 1) * 128].rearrange("one p -> p one")))(kc),
                reads=[self.kv_out], writes=[ksc], dma=True)
        ksb = self.smh.ap().bitcast(BF16)
        t.op("vector", lambda EE: EE.tensor_copy(ksb[:, 0:DC], ksc[:, 0:DC]),
             reads=[ksc], writes=[self.smh])
        zrow = self.arG
        for n in range(4):
            for c in range(DC):
                t.op("tensor", (lambda n, c: lambda EE: EE.matmul(
                    p5[0:1, 0:512], ksb[:, c:c + 1], qp_fm[:, c, n * 512:(n + 1) * 512],
                    start=(c == 0), stop=(c == DC - 1)))(n, c),
                    reads=[self.smh, self.arA], writes=[p5])
            t.op("vector", (lambda n: lambda EE: EE.tensor_copy(zrow[0:1, n * 512:(n + 1) * 512], p5[0:1, 0:512]))(n),
                 reads=[p5], writes=[self.arG])
        zcol = self.zcol
        for i in range(TT):
            t.op("tensor", (lambda i: lambda EE: EE.transpose(
                self.ps[4][:, 0:1], zrow[0:1, i * 128:(i + 1) * 128], self.c_identf[0:1, 0:1]))(i),
                reads=[self.arG, self.c_identf], writes=[self.ps[4]])
            t.op("vector", (lambda i: lambda EE: EE.tensor_copy(zcol[:, i:i + 1], self.ps[4][:, 0:1]))(i),
                 reads=[self.ps[4]], writes=[zcol])
        t.op("vector", lambda EE: EE.tensor_scalar(zcol[:], zcol[:], ATT_EPS, None, A.add),
             reads=[zcol], writes=[zcol])
        t.op("vector", lambda EE: EE.reciprocal(zcol[:], zcol[:]), reads=[zcol], writes=[zcol])
        sstage = self.arW.ap().bitcast(BF16)
        for i in range(TT):
            sst = sstage[:, 6144 + (i % 2) * 1024:6144 + (i % 2) * 1024 + 1024]
            for n in range(2):
                p = ps[n]
                sl = slice(n * 512, (n + 1) * 512)
                for c in range(DC):
                    t.op("tensor", (lambda i, c, sl, p: lambda EE: EE.matmul(
                        p[:], qp_fm[:, c, i * 128:(i + 1) * 128], kvb[:, c, sl],
                        start=(c == 0), stop=(c == DC - 1)))(i, c, sl, p),
                        reads=[qp_fm[:, c, i * 128:(i + 1) * 128], kvb[:, c, sl]], writes=[p])
                t.op("vector", (lambda i, sl, p, sst: lambda EE: EE.tensor_scalar(
                    sst[:, sl], p[:], zcol[:, i:i + 1], None, A.mult))(i, sl, p, sst),
                    reads=[p, zcol], writes=[sst[:, sl]])
            t.op("sync", (lambda i, sst: lambda EE: EE.dma_start(
                out=self.s_dram[i * 128:(i + 1) * 128, :], in_=sst))(i, sst),
                reads=[sst], writes=[self.s_dram[i * 128:(i + 1) * 128, :]], dma=True)
        self.tap("s0", self.s_dram, [self.s_dram])

    def emit_routing(self):
        t = self.t
        aF = self.arA.ap().bitcast(F32)
        scores = aF.rearrange("p (i e) -> p i e", e=E)
        sc0 = 0
        ss0 = 1024
        pf0 = 2048
        gwsb = self.arF
        for c in range(DC):
            t.op("sync", (lambda c: lambda EE: EE.dma_start(
                out=gwsb[:, c * 64:(c + 1) * 64], in_=self.gwt[c * 128:(c + 1) * 128, :]))(c),
                reads=[self.gwt], writes=[self.arF], dma=True)
        p5 = self.ps[5]
        for c in range(DC):
            t.op("tensor", (lambda c: lambda EE: EE.matmul(
                p5[0:1, 0:64], self.c_onesc[:], gwsb[:, c * 64:(c + 1) * 64],
                start=(c == 0), stop=(c == DC - 1)))(c),
                reads=[self.c_onesc, self.arF], writes=[p5])
        gwcol = self.arG
        t.op("vector", lambda EE: EE.tensor_copy(gwcol[0:1, 0:E], p5[0:1, 0:64]), reads=[p5], writes=[self.arG])
        t.op("tensor", lambda EE: EE.matmul(p5[:, 0:64], self.c_ones1[:], gwcol[0:1, 0:E], start=True, stop=True),
             reads=[self.c_ones1, self.arG], writes=[p5])
        gwcolb = self.arG
        t.op("vector", lambda EE: EE.tensor_copy(gwcolb[:, 64:64 + E], p5[:, 0:64]), reads=[p5], writes=[self.arG])
        mr = self.sm2
        t.op("vector", lambda EE: EE.tensor_tensor(mr[:], self.mmean[:], self.rstd[:], A.mult),
             reads=[self.mmean, self.rstd], writes=[mr])
        xfm0 = 1024
        p0 = self.ps[0]
        small = self.small
        for i in range(TT):
            t.op("sync", (lambda i: lambda EE: EE.dma_start(
                out=self.arF[:, 1024:2048], in_=self.x_tm[i * 128:(i + 1) * 128, :]))(i),
                reads=[self.x_tm], writes=[self.arF], dma=True)
            for c in range(DC):
                t.op("tensor", (lambda c: lambda EE: EE.transpose(
                    self.ps[4][:, 0:128], self.arF[:, 1024 + c * 128:1024 + (c + 1) * 128], self.c_identf[:]))(c),
                    reads=[self.arF, self.c_identf], writes=[self.ps[4]])
                t.op("vector", (lambda c: lambda EE: EE.tensor_copy(
                    self.arG[:, xfm0 + c * 128:xfm0 + (c + 1) * 128], self.ps[4][:, 0:128]))(c),
                    reads=[self.ps[4]], writes=[self.arG])
            for c in range(DC):
                t.op("tensor", (lambda i, c: lambda EE: EE.matmul(
                    p0[:, 0:64], self.arG[:, xfm0 + c * 128:xfm0 + (c + 1) * 128],
                    gwsb[:, c * 64:(c + 1) * 64],
                    start=(c == 0), stop=(c == DC - 1)))(i, c),
                    reads=[self.arG, self.arF], writes=[p0])
            t.op("vector", (lambda i: lambda EE: EE.tensor_scalar(
                small[:, 1024:1024 + E], gwcolb[:, 64:64 + E], mr[:, i:i + 1], None, A.mult))(i),
                reads=[self.arG, mr], writes=[small])
            t.op("vector", (lambda i: lambda EE: EE.scalar_tensor_tensor(
                small[:, 1088:1088 + E], p0[:, 0:64], self.rstd[:, i:i + 1], small[:, 1024:1024 + E],
                A.mult, A.subtract))(i),
                reads=[p0, self.rstd, small], writes=[small])
            t.op("scalar", (lambda i: lambda EE: EE.activation(
                out=aF[:, sc0 + i * E:sc0 + (i + 1) * E], in_=small[:, 1088:1088 + E], func=AF.Sigmoid))(i),
                reads=[small], writes=[self.arA])
            sfc = small[:, 1152:1152 + E]
            t.op("vector", (lambda i, sfc: lambda EE: EE.tensor_tensor(
                sfc, aF[:, sc0 + i * E:sc0 + (i + 1) * E], self.c_ebb[:], A.add))(i, sfc),
                reads=[self.arA, self.c_ebb], writes=[small])
            g3 = sfc.rearrange("p (g e) -> p g e", e=8)
            l1m = small[:, 1216:1248].rearrange("p (g e) -> p g e", e=4)
            l1s = small[:, 1248:1280].rearrange("p (g e) -> p g e", e=4)
            t.op("vector", (lambda g3, l1m: lambda EE: EE.tensor_tensor(l1m, g3[:, :, 0::2], g3[:, :, 1::2], A.max))(g3, l1m),
                 reads=[small], writes=[small])
            t.op("vector", (lambda g3, l1s: lambda EE: EE.tensor_tensor(l1s, g3[:, :, 0::2], g3[:, :, 1::2], A.add))(g3, l1s),
                 reads=[small], writes=[small])
            l2m = small[:, 1280:1296].rearrange("p (g e) -> p g e", e=2)
            l2s = small[:, 1296:1312].rearrange("p (g e) -> p g e", e=2)
            l2c = small[:, 1312:1328].rearrange("p (g e) -> p g e", e=2)
            t.op("vector", (lambda l1m, l2m: lambda EE: EE.tensor_tensor(l2m, l1m[:, :, 0::2], l1m[:, :, 1::2], A.max))(l1m, l2m),
                 reads=[small], writes=[small])
            t.op("vector", (lambda l1m, l2c: lambda EE: EE.tensor_tensor(l2c, l1m[:, :, 0::2], l1m[:, :, 1::2], A.add))(l1m, l2c),
                 reads=[small], writes=[small])
            t.op("vector", (lambda l1s, l2s: lambda EE: EE.tensor_tensor(l2s, l1s[:, :, 0::2], l1s[:, :, 1::2], A.max))(l1s, l2s),
                 reads=[small], writes=[small])
            t.op("vector", (lambda l2s, l2c: lambda EE: EE.tensor_tensor(l2s, l2s, l2c, A.max))(l2s, l2c),
                 reads=[small], writes=[small])
            gsum = small[:, 1328:1336]
            t.op("vector", (lambda l2m, gsum: lambda EE: EE.tensor_tensor(
                gsum, l2m[:, :, 0], l2m[:, :, 1], A.add))(l2m, gsum),
                reads=[small], writes=[small])
            t.op("vector", (lambda l2s, gsum: lambda EE: EE.tensor_tensor(
                gsum, gsum, l2s[:, :, 0], A.max))(l2s, gsum),
                reads=[small], writes=[small])
            t.op("vector", (lambda l2s, gsum: lambda EE: EE.tensor_tensor(
                gsum, gsum, l2s[:, :, 1], A.max))(l2s, gsum),
                reads=[small], writes=[small])
            gm8 = small[:, 1336:1344]
            gi8 = self.smi[:, 0:8]
            t.op("vector", (lambda gsum, gm8: lambda EE: EE.max(gm8, gsum))(gsum, gm8),
                 reads=[small], writes=[small])
            t.op("vector", (lambda gm8, gsum, gi8: lambda EE: EE.max_index(gi8, gm8, gsum))(gm8, gsum, gi8),
                 reads=[small], writes=[self.smi])
            gif = small[:, 1344:1352]
            t.op("vector", (lambda gi8, gif: lambda EE: EE.tensor_copy(gif, gi8))(gi8, gif),
                 reads=[self.smi], writes=[small])
            gmask = small[:, 1352:1360]
            for j in range(4):
                if j == 0:
                    t.op("vector", (lambda gif, gmask: lambda EE: EE.tensor_tensor(
                        gmask, self.c_iotag[:], gif[:, 0:1].broadcast_to([128, 8]), A.is_equal))(gif, gmask),
                        reads=[small, self.c_iotag], writes=[small])
                else:
                    t.op("vector", (lambda gif, gmask, j: lambda EE: EE.scalar_tensor_tensor(
                        gmask, self.c_iotag[:], 0.0, gif[:, j:j + 1].broadcast_to([128, 8]), A.add, A.is_equal,
                        accum_out=None))(gif, gmask, j),
                        reads=[small, self.c_iotag], writes=[small]) if False else None
                    oh1 = small[:, 1360:1368]
                    t.op("vector", (lambda gif, oh1, j: lambda EE: EE.tensor_tensor(
                        oh1, self.c_iotag[:], gif[:, j:j + 1].broadcast_to([128, 8]), A.is_equal))(gif, oh1, j),
                        reads=[small, self.c_iotag], writes=[small])
                    t.op("vector", (lambda gmask, oh1: lambda EE: EE.tensor_tensor(
                        gmask, gmask, oh1, A.add))(gmask, oh1),
                        reads=[small], writes=[small])
            tmp = small[:, 1368:1368 + E]
            t.op("vector", (lambda sfc, gmask, tmp: lambda EE: EE.tensor_tensor(
                tmp.rearrange("p (g e) -> p g e", e=8), sfc.rearrange("p (g e) -> p g e", e=8),
                gmask.rearrange("p (g o) -> p g o", o=1).broadcast_to([128, 8, 8]), A.mult))(sfc, gmask, tmp),
                reads=[small], writes=[small])
            t8 = small[:, 1432:1440]
            ti8 = self.smi[:, 8:16]
            t.op("vector", (lambda tmp, t8: lambda EE: EE.max(t8, tmp))(tmp, t8),
                 reads=[small], writes=[small])
            t.op("vector", (lambda t8, tmp, ti8: lambda EE: EE.max_index(ti8, t8, tmp))(t8, tmp, ti8),
                 reads=[small], writes=[self.smi])
            tidxf = small[:, 0:128].rearrange("p (i j) -> p i j", j=8)[:, i, :]
            t.op("vector", (lambda ti8, tidxf: lambda EE: EE.tensor_copy(tidxf, ti8))(ti8, tidxf),
                 reads=[self.smi], writes=[small])
            ssel = aF[:, ss0 + i * E:ss0 + (i + 1) * E]
            wsum = small[:, 640:648]
            topw = small[:, 128:256].rearrange("p (i j) -> p i j", j=8)[:, i, :]
            oh = small[:, 1440:1440 + E]
            scsl = aF[:, sc0 + i * E:sc0 + (i + 1) * E]
            for j in range(KTOP):
                t.op("vector", (lambda tidxf, oh, j: lambda EE: EE.tensor_tensor(
                    oh, self.c_iotae[:], tidxf[:, j:j + 1].broadcast_to([128, E]), A.is_equal))(tidxf, oh, j),
                    reads=[small, self.c_iotae], writes=[small])
                t.op("vector", (lambda oh, scsl, topw, j: lambda EE: EE.scalar_tensor_tensor(
                    small[:, 1504:1504 + E], scsl, 0.0, oh, A.add, A.mult,
                    accum_out=topw[:, j:j + 1]))(oh, scsl, topw, j),
                    reads=[small, self.arA], writes=[small])
                if j == 0:
                    t.op("vector", (lambda oh, ssel: lambda EE: EE.tensor_copy(ssel, oh))(oh, ssel),
                         reads=[small], writes=[self.arA])
                else:
                    t.op("vector", (lambda oh, ssel: lambda EE: EE.tensor_tensor(ssel, ssel, oh, A.add))(oh, ssel),
                         reads=[small, self.arA], writes=[self.arA])
            t.op("vector", (lambda topw, wsum: lambda EE: EE.tensor_reduce(
                wsum[:, 0:1], topw, AX.X, A.add))(topw, wsum),
                reads=[small], writes=[small])
            t.op("vector", (lambda wsum: lambda EE: EE.tensor_scalar(
                wsum[:, 0:1], wsum[:, 0:1], 1e-20, None, A.add))(wsum),
                reads=[small], writes=[small])
            t.op("vector", (lambda wsum: lambda EE: EE.reciprocal(wsum[:, 1:2], wsum[:, 0:1]))(wsum),
                 reads=[small], writes=[small])
            t.op("vector", (lambda wsum: lambda EE: EE.tensor_scalar(
                wsum[:, 1:2], wsum[:, 1:2], 2.5, None, A.mult))(wsum),
                reads=[small], writes=[small])
            t.op("vector", (lambda topw, wsum: lambda EE: EE.tensor_scalar(
                topw, topw, wsum[:, 1:2], None, A.mult))(topw, wsum),
                reads=[small], writes=[small])
        for i in range(TT):
            t.op("tensor", (lambda i: lambda EE: EE.matmul(
                p5[0:1, 0:64], self.c_onesc[:], aF[:, ss0 + i * E:ss0 + (i + 1) * E],
                start=(i == 0), stop=(i == TT - 1)))(i),
                reads=[self.c_onesc, self.arA], writes=[p5])
        t.op("vector", lambda EE: EE.tensor_copy(self.arG[0:1, 128:128 + E], p5[0:1, 0:64]),
             reads=[p5], writes=[self.arG])
        t.op("sync", lambda EE: EE.dma_start(out=self.cnt_in, in_=self.arG[0:1, 128:128 + E]),
             reads=[self.arG], writes=[self.cnt_in], dma=True)
        t.op("gpsimd", lambda EE: EE.collective_compute(
            "AllGather", A.bypass, replica_groups=[list(range(NCORE))],
            ins=[self.cnt_in.opt()], outs=[self.cnt_out.opt()]),
            reads=[self.cnt_in], writes=[self.cnt_out])
        cnts = self.arG
        t.op("sync", lambda EE: EE.dma_start(out=cnts[0:NCORE, 256:256 + E], in_=self.cnt_out),
             reads=[self.cnt_out], writes=[self.arG], dma=True)
        t.op("tensor", lambda EE: EE.matmul(p5[0:1, 0:64], self.c_prevm[:], cnts[0:NCORE, 256:256 + E],
                                            start=True, stop=True),
             reads=[self.c_prevm, self.arG], writes=[p5])
        t.op("vector", lambda EE: EE.tensor_copy(self.arG[0:1, 384:384 + E], p5[0:1, 0:64]),
             reads=[p5], writes=[self.arG])
        carry = self.arG[0:1, 384:384 + E]
        small = self.small
        for i in range(TT):
            t.op("tensor", (lambda i: lambda EE: EE.matmul(
                p0[:, 0:64], self.c_strictU[:], aF[:, ss0 + i * E:ss0 + (i + 1) * E],
                start=True, stop=False))(i),
                reads=[self.c_strictU, self.arA], writes=[p0])
            t.op("tensor", (lambda i: lambda EE: EE.matmul(
                p0[:, 0:64], self.c_ones1[:], carry, start=False, stop=True))(i),
                reads=[self.c_ones1, self.arG], writes=[p0])
            t.op("vector", (lambda i: lambda EE: EE.tensor_copy(
                aF[:, pf0 + i * E:pf0 + (i + 1) * E], p0[:, 0:64]))(i),
                reads=[p0], writes=[self.arA])
            t.op("tensor", (lambda i: lambda EE: EE.matmul(
                p5[0:1, 0:64], self.c_onesc[:], aF[:, ss0 + i * E:ss0 + (i + 1) * E],
                start=True, stop=True))(i),
                reads=[self.c_onesc, self.arA], writes=[p5])
            t.op("vector", (lambda i: lambda EE: EE.tensor_tensor(
                carry, carry, p5[0:1, 0:64], A.add))(i),
                reads=[p5, self.arG], writes=[self.arG])
        zer2 = self.arG[:, 512:514]
        t.op("vector", lambda EE: EE.memset(zer2, 0.0), writes=[self.arG])
        t.op("sync", lambda EE: EE.dma_start(
            out=self.tab_in.rearrange("(p k) two -> p (k two)", p=128), in_=self.arG[:, 512:514].broadcast_to([128, 1026]) if False else self.arG[:, 512:514]),
            reads=[self.arG], writes=[self.tab_in], dma=True) if False else None
        t.op("vector", lambda EE: EE.memset(self.arF[:, 0:1026], 0.0), writes=[self.arF])
        t.op("sync", lambda EE: EE.dma_start(
            out=self.tab_in.rearrange("(p k) two -> p (k two)", p=128), in_=self.arF[:, 0:1026]),
            reads=[self.arF], writes=[self.tab_in], dma=True)
        tidxa = small[:, 0:128].rearrange("p (i j) -> p i j", j=8)
        topwa = small[:, 128:256].rearrange("p (i j) -> p i j", j=8)
        posg = small[:, 256:384].rearrange("p (i j) -> p i j", j=8)
        kept = small[:, 384:512].rearrange("p (i j) -> p i j", j=8)
        scat = small[:, 512:640].rearrange("p (i j) -> p i j", j=8)
        trash = self.arG[:, 520:521]
        t.op("vector", lambda EE: EE.tensor_scalar(trash, self.c_iotap[:], float(TRASH), None, A.add),
             reads=[self.c_iotap], writes=[self.arG])
        oh = small[:, 1440:1440 + E]
        ex = small[:, 1504:1504 + E]
        rec = self.arG[:, 521:523]
        sidx = self.smi[:, 16:17]
        for i in range(TT):
            for j in range(KTOP):
                t.op("vector", (lambda i, j: lambda EE: EE.tensor_tensor(
                    oh, self.c_iotae[:], tidxa[:, i, j:j + 1].broadcast_to([128, E]), A.is_equal))(i, j),
                    reads=[small, self.c_iotae], writes=[small])
                t.op("vector", (lambda i, j: lambda EE: EE.scalar_tensor_tensor(
                    ex, aF[:, pf0 + i * E:pf0 + (i + 1) * E], 0.0, oh, A.add, A.mult,
                    accum_out=posg[:, i, j:j + 1]))(i, j),
                    reads=[small, self.arA], writes=[small])
                t.op("vector", (lambda i, j: lambda EE: EE.tensor_scalar(
                    kept[:, i, j:j + 1], posg[:, i, j:j + 1], float(CAP), None, A.is_lt))(i, j),
                    reads=[small], writes=[small])
                t.op("vector", (lambda i, j: lambda EE: EE.scalar_tensor_tensor(
                    scat[:, i, j:j + 1], tidxa[:, i, j:j + 1], float(CAP), posg[:, i, j:j + 1],
                    A.mult, A.add))(i, j),
                    reads=[small], writes=[small])
                t.op("vector", (lambda i, j: lambda EE: EE.scalar_tensor_tensor(
                    scat[:, i, j:j + 1], scat[:, i, j:j + 1], 1.0, trash, A.mult, A.subtract))(i, j),
                    reads=[small, self.arG], writes=[small])
                t.op("vector", (lambda i, j: lambda EE: EE.tensor_tensor(
                    scat[:, i, j:j + 1], scat[:, i, j:j + 1], kept[:, i, j:j + 1], A.mult))(i, j),
                    reads=[small], writes=[small])
                t.op("vector", (lambda i, j: lambda EE: EE.scalar_tensor_tensor(
                    scat[:, i, j:j + 1], scat[:, i, j:j + 1], 1.0, trash, A.mult, A.add))(i, j),
                    reads=[small, self.arG], writes=[small])
                t.op("vector", (lambda i, j: lambda EE: EE.tensor_scalar(
                    rec[:, 0:1], self.c_tokoff[:], float(i * 128 + 1), None, A.add))(i, j),
                    reads=[self.c_tokoff], writes=[self.arG])
                t.op("vector", (lambda i, j: lambda EE: EE.tensor_tensor(
                    rec[:, 0:1], rec[:, 0:1], self.c_iotap[:], A.add))(i, j),
                    reads=[self.arG, self.c_iotap], writes=[self.arG])
                t.op("vector", (lambda i, j: lambda EE: EE.tensor_copy(
                    rec[:, 1:2], topwa[:, i, j:j + 1]))(i, j),
                    reads=[small], writes=[self.arG])
                t.op("vector", (lambda i, j: lambda EE: EE.tensor_copy(
                    sidx, scat[:, i, j:j + 1]))(i, j),
                    reads=[small], writes=[self.smi])
                t.op("gpsimd", (lambda i, j: lambda EE: EE.indirect_dma_start(
                    self.tab_in, IndirectOffsetOnAxis(sidx, 0), rec, None))(i, j),
                    reads=[self.arG, self.smi, self.tab_in], writes=[self.tab_in], dma=True)
        t.op("gpsimd", lambda EE: EE.collective_compute(
            "AllReduce", A.add, replica_groups=[list(range(NCORE))],
            ins=[self.tab_in.opt()], outs=[self.tab_out.opt()]),
            reads=[self.tab_in], writes=[self.tab_out])
        self.tap("posg", small[:, 256:384], [small])
        self.tap("scat", small[:, 512:640], [small])
        self.tap("topw", small[:, 128:256], [small])
        self.tap("tidx", small[:, 0:128], [small])

    def moe_gen(self, g):
        t = self.t
        ps = self.ps
        ar = self.arA if g == 0 else self.arC
        xe = ar.ap().rearrange("p (c n) -> p c n", n=1024)
        h_fm = xe
        w2b = xe
        war = self.arW if g == 0 else self.arE
        web = war.ap().rearrange("p (c n) -> p c n", n=512)
        xen = self.xe_nat if g == 0 else self.xe_nat2
        arD = self.arD.ap()
        arF = self.arF.ap()
        arG = self.arG.ap()
        pM0, pM1, pW = ps[2 * g], ps[2 * g + 1], ps[4 + g]
        tabf = self.small[:, g * 16:g * 16 + 8]
        idxf = self.small[:, g * 16 + 8:g * 16 + 16]
        wrow = arF[0:1, g * 2048:g * 2048 + 1024]
        wb = arF[:, g * 2048 + 1024:g * 2048 + 2048]
        sg = arG[:, g * 512:g * 512 + 512]
        tmp = arG[:, 1024 + g * 512:1024 + g * 512 + 512]
        yestg = arD[:, 6144 + g * 1024:6144 + g * 1024 + 1024]
        for el in range(g, EPC, 2):
            for c in range(DC):
                t.op("gpsimd", (lambda el, c: lambda EE: EE.dma_start(
                    out=web[:, c, :], in_=self.w1t[el, c * 128:(c + 1) * 128, :]))(el, c),
                    reads=[self.w1t], writes=[web[:, c, :]], dma=True)
                t.op("gpsimd", (lambda el, c: lambda EE: EE.dma_start(
                    out=web[:, 8 + c, :], in_=self.w3t[el, c * 128:(c + 1) * 128, :]))(el, c),
                    reads=[self.w3t], writes=[web[:, 8 + c, :]], dma=True)
                yield
            for fc in range(4):
                t.op("gpsimd", (lambda el, fc: lambda EE: EE.dma_start(
                    out=w2b[:, 12 + fc, :], in_=self.w2t[el, fc * 128:(fc + 1) * 128, :]))(el, fc),
                    reads=[self.w2t], writes=[w2b[:, 12 + fc, :]], dma=True)
            yield
            t.op("sync", (lambda el, tabf: lambda EE: EE.dma_start(
                out=tabf,
                in_=self.tab_out[el * CAP:(el + 1) * CAP, 0:1].rearrange("(k p) one -> p k", p=128),
                allow_slow_non_contiguous=True))(el, tabf),
                reads=[self.tab_out[el * CAP:(el + 1) * CAP, 0:1]], writes=[tabf], dma=True)
            t.op("sync", (lambda el, wrow: lambda EE: EE.dma_start(
                out=wrow,
                in_=self.tab_out[el * CAP:(el + 1) * CAP, 1:2].rearrange("s one -> one s"),
                allow_slow_non_contiguous=True))(el, wrow),
                reads=[self.tab_out[el * CAP:(el + 1) * CAP, 1:2]], writes=[wrow], dma=True)
            yield
            t.op("vector", (lambda tabf, idxf: lambda EE: EE.tensor_scalar(
                idxf, tabf, 1.0, 1.0, A.max, A.subtract))(tabf, idxf),
                reads=[tabf], writes=[idxf])
            t.op("vector", (lambda g, idxf: lambda EE: EE.tensor_copy(
                self.smi[:, g * 8:g * 8 + 8], idxf))(g, idxf),
                reads=[idxf], writes=[self.smi[:, g * 8:g * 8 + 8]])
            yield
            for n in range(2):
                t.op("tensor", (lambda n, wrow, pW: lambda EE: EE.matmul(
                    pW[:], self.c_ones1[:], wrow[:, n * 512:(n + 1) * 512],
                    start=True, stop=True))(n, wrow, pW),
                    reads=[self.c_ones1, wrow[:, n * 512:(n + 1) * 512]], writes=[pW])
                yield
                t.op("vector", (lambda n, wb, pW: lambda EE: EE.tensor_copy(
                    wb[:, n * 512:(n + 1) * 512], pW[:]))(n, wb, pW),
                    reads=[pW], writes=[wb[:, n * 512:(n + 1) * 512]])
                yield
            for st in range(8):
                xg = arD[:, (g * 2 + st % 2) * 1024:(g * 2 + st % 2) * 1024 + 1024].bitcast(BF16)
                t.op("gpsimd", (lambda el, st, g, xg: lambda EE: EE.indirect_dma_start(
                    xg, None, self.agx_out,
                    IndirectOffsetOnAxis(self.smi[:, g * 8 + st:g * 8 + st + 1], 0)))(el, st, g, xg),
                    reads=[self.agx_out, self.smi[:, g * 8 + st:g * 8 + st + 1], xg], writes=[xg], dma=True)
                yield
                t.op("gpsimd", (lambda el, st, xg: lambda EE: EE.dma_start(
                    out=xen[st * 128:(st + 1) * 128, :], in_=xg))(el, st, xg),
                    reads=[xg], writes=[xen[st * 128:(st + 1) * 128, :]], dma=True)
                yield
            for c in range(DC):
                t.op("scalar", (lambda el, c: lambda EE: EE.dma_start_transpose(
                    xe[:, c, :], xen[:, c * 128:(c + 1) * 128]))(el, c),
                    reads=[xen[:, c * 128:(c + 1) * 128]], writes=[xe[:, c, :]], dma=True)
                yield
            for fc in range(4):
                for n in range(2):
                    sl = slice(n * 512, (n + 1) * 512)
                    for c in range(DC):
                        t.op("tensor", (lambda fc, c, sl, pM0: lambda EE: EE.matmul(
                            pM0[:], web[:, c, fc * 128:(fc + 1) * 128], xe[:, c, sl],
                            start=(c == 0), stop=(c == DC - 1)))(fc, c, sl, pM0),
                            reads=[web[:, c, fc * 128:(fc + 1) * 128], xe[:, c, sl]], writes=[pM0])
                    yield
                    for c in range(DC):
                        t.op("tensor", (lambda fc, c, sl, pM1: lambda EE: EE.matmul(
                            pM1[:], web[:, 8 + c, fc * 128:(fc + 1) * 128], xe[:, c, sl],
                            start=(c == 0), stop=(c == DC - 1)))(fc, c, sl, pM1),
                            reads=[web[:, 8 + c, fc * 128:(fc + 1) * 128], xe[:, c, sl]], writes=[pM1])
                    yield
                    t.op("scalar", (lambda sg, pM0: lambda EE: EE.activation(
                        out=sg, in_=pM0[:], func=AF.Silu))(sg, pM0),
                        reads=[pM0], writes=[sg])
                    yield
                    t.op("vector", (lambda tmp, sg, pM1: lambda EE: EE.tensor_tensor(
                        tmp, sg, pM1[:], A.mult))(tmp, sg, pM1),
                        reads=[sg, pM1], writes=[tmp])
                    yield
                    t.op("vector", (lambda fc, sl, tmp, wb: lambda EE: EE.tensor_tensor(
                        h_fm[:, 8 + fc, sl], tmp, wb[:, sl], A.mult))(fc, sl, tmp, wb),
                        reads=[tmp, wb[:, sl]], writes=[h_fm[:, 8 + fc, sl]])
                    yield
            for st in range(8):
                for n in range(2):
                    p = pM0 if n == 0 else pM1
                    sl = slice(n * 512, (n + 1) * 512)
                    for fc in range(4):
                        t.op("tensor", (lambda st, fc, sl, p: lambda EE: EE.matmul(
                            p[:], h_fm[:, 8 + fc, st * 128:(st + 1) * 128], w2b[:, 12 + fc, sl],
                            start=(fc == 0), stop=(fc == 3)))(st, fc, sl, p),
                            reads=[h_fm[:, 8 + fc, st * 128:(st + 1) * 128], w2b[:, 12 + fc, sl]],
                            writes=[p])
                    yield
                    t.op("vector", (lambda n, p, yestg: lambda EE: EE.tensor_copy(
                        yestg[:, n * 512:(n + 1) * 512], p[:]))(n, p, yestg),
                        reads=[p], writes=[yestg[:, n * 512:(n + 1) * 512]])
                    yield
                t.op("sync", (lambda el, st, yestg: lambda EE: EE.dma_start(
                    out=self.ye_in[el * CAP + st * 128:el * CAP + (st + 1) * 128, :],
                    in_=yestg))(el, st, yestg),
                    reads=[yestg],
                    writes=[self.ye_in[el * CAP + st * 128:el * CAP + (st + 1) * 128, :]], dma=True)
                yield

    def emit_moe(self):
        t = self.t
        self._rr([self.moe_gen(0), self.moe_gen(1)])
        t.op("vector", lambda EE: EE.memset(self.arD.ap()[:, 6144:7168], 0.0),
             writes=[self.arD.ap()[:, 6144:7168]])
        t.op("sync", lambda EE: EE.dma_start(
            out=self.ye_out[E * CAP:E * CAP + 128, :], in_=self.arD.ap()[:, 6144:7168].bitcast(BF16)[:, 0:1024]),
            reads=[self.arD.ap()[:, 6144:7168]], writes=[self.ye_out[E * CAP:E * CAP + 128, :]], dma=True)
        t.op("gpsimd", lambda EE: EE.collective_compute(
            "AllGather", A.bypass, replica_groups=[list(range(NCORE))],
            ins=[self.ye_in.opt()], outs=[self.ye_out[0:E * CAP, :].opt()]),
            reads=[self.ye_in], writes=[self.ye_out])

    def emit_l0_epilogue(self):
        t = self.t
        xn_fm = self.xn_fm
        gsh = self.arC.ap().rearrange("p (c n) -> p c n", n=NT)
        ush = self.arD.ap().rearrange("p (c n) -> p c n", n=NT)
        wsg = self.arW.ap().rearrange("p (c n) -> p c n", n=1024)
        ps = self.ps
        for wnm, dst in (("sgt", gsh), ("sut", ush)):
            wap = getattr(self, wnm)
            dstar = self.arC if dst is gsh else self.arD
            for c in range(DC):
                t.op("gpsimd", (lambda c, wap: lambda EE: EE.dma_start(
                    out=wsg[:, c, :], in_=wap[c * 128:(c + 1) * 128, :]))(c, wap),
                    reads=[wap, self.arW], writes=[self.arW], dma=True)
            for oc in range(DC):
                for n in range(4):
                    p = ps[n % 2]
                    sl = slice(n * 512, (n + 1) * 512)
                    for c in range(DC):
                        t.op("tensor", (lambda oc, c, sl, p: lambda EE: EE.matmul(
                            p[:], wsg[:, c, oc * 128:(oc + 1) * 128], xn_fm[:, c, sl],
                            start=(c == 0), stop=(c == DC - 1)))(oc, c, sl, p),
                            reads=[self.arW, self.arB], writes=[p])
                    if dst is gsh:
                        t.op("scalar", (lambda oc, sl, p, dst, dstar: lambda EE: EE.activation(
                            out=dst[:, oc, sl], in_=p[:], func=AF.Silu))(oc, sl, p, dst, dstar),
                            reads=[p], writes=[dstar])
                    else:
                        t.op("vector", (lambda oc, sl, p, dst, dstar: lambda EE: EE.tensor_copy(
                            dst[:, oc, sl], p[:]))(oc, sl, p, dst, dstar),
                            reads=[p], writes=[dstar])
        for c in range(DC):
            t.op("vector", (lambda c: lambda EE: EE.tensor_tensor(
                gsh[:, c, :], gsh[:, c, :], ush[:, c, :], A.mult))(c),
                reads=[self.arC, self.arD], writes=[self.arC])
        wsd = self.arW.ap().rearrange("p (c n) -> p c n", n=1024)
        for c in range(DC):
            t.op("gpsimd", (lambda c: lambda EE: EE.dma_start(
                out=wsd[:, c, :], in_=self.sdt[c * 128:(c + 1) * 128, :]))(c),
                reads=[self.sdt, self.arW], writes=[self.arW], dma=True)
        small = self.small
        scat = small[:, 512:640].rearrange("p (i j) -> p i j", j=8)
        acc = self.arF
        scr = self.arG
        for i in range(TT):
            for j in range(KTOP):
                t.op("vector", (lambda i, j: lambda EE: EE.tensor_copy(
                    self.smi[:, 17:18], scat[:, i, j:j + 1]))(i, j),
                    reads=[small], writes=[self.smi])
                t.op("gpsimd", (lambda i, j: lambda EE: EE.indirect_dma_start(
                    acc[:, 0:1024], None, self.ye_out,
                    IndirectOffsetOnAxis(self.smi[:, 17:18], 0),
                    compute_op=(A.bypass if j == 0 else A.add)))(i, j),
                    reads=[self.ye_out, self.smi] + ([self.arF] if j > 0 else []),
                    writes=[self.arF], dma=True)
            for n in range(2):
                p = ps[2 + n]
                sl = slice(n * 512, (n + 1) * 512)
                for c in range(DC):
                    t.op("tensor", (lambda i, c, sl, p: lambda EE: EE.matmul(
                        p[:], gsh[:, c, i * 128:(i + 1) * 128], wsd[:, c, sl],
                        start=(c == 0), stop=(c == DC - 1)))(i, c, sl, p),
                        reads=[self.arC, self.arW], writes=[p])
                t.op("vector", (lambda i, n, sl, p: lambda EE: EE.tensor_tensor(
                    scr[:, sl] if False else scr[:, n * 512:(n + 1) * 512],
                    acc[:, n * 512:(n + 1) * 512], p[:], A.add))(i, n, sl, p),
                    reads=[self.arF, p], writes=[self.arG])
            t.op("scalar", (lambda i: lambda EE: EE.activation(
                out=scr[:, 1024:2048], in_=scr[:, 0:1024], func=AF.Silu))(i),
                reads=[self.arG], writes=[self.arG])
            t.op("sync", (lambda i: lambda EE: EE.dma_start(
                out=acc[:, 1024:2048], in_=self.x_tm[i * 128:(i + 1) * 128, :]))(i),
                reads=[self.x_tm], writes=[self.arF], dma=True)
            t.op("sync", (lambda i: lambda EE: EE.dma_start(
                out=acc[:, 3072:3584].bitcast(BF16)[:, 0:1024], in_=self.s_dram[i * 128:(i + 1) * 128, :]))(i),
                reads=[self.s_dram], writes=[self.arF], dma=True)
            t.op("vector", (lambda i: lambda EE: EE.tensor_tensor(
                scr[:, 0:1024], scr[:, 1024:2048], acc[:, 3072:3584].bitcast(BF16)[:, 0:1024], A.mult))(i),
                reads=[self.arG, self.arF], writes=[self.arG])
            t.op("vector", (lambda i: lambda EE: EE.tensor_tensor(
                acc[:, 2048:3072], acc[:, 1024:2048], scr[:, 0:1024], A.add))(i),
                reads=[self.arF, self.arG], writes=[self.arF])
            t.op("sync", (lambda i: lambda EE: EE.dma_start(
                out=self.x1_dram[i * 128:(i + 1) * 128, :], in_=acc[:, 2048:3072]))(i),
                reads=[self.arF], writes=[self.x1_dram], dma=True)
        self.tap("x1", self.x1_dram, [self.x1_dram])

    def emit_ln1_proj(self):
        t = self.t
        msum, sqs = self.sm1, self.sm2
        scr = self.arG
        xtile = self.arF
        for i in range(TT):
            t.op("sync", (lambda i: lambda EE: EE.dma_start(
                out=xtile[:, 0:1024], in_=self.x1_dram[i * 128:(i + 1) * 128, :]))(i),
                reads=[self.x1_dram], writes=[self.arF], dma=True)
            t.op("scalar", (lambda i: lambda EE: EE.activation(
                out=scr[:, 0:1024], in_=xtile[:, 0:1024], func=AF.Identity, accum_out=msum[:, i:i + 1]))(i),
                reads=[self.arF], writes=[self.arG, msum])
            t.op("scalar", (lambda i: lambda EE: EE.activation(
                out=scr[:, 1024:2048], in_=xtile[:, 0:1024], func=AF.Square, accum_out=sqs[:, i:i + 1]))(i),
                reads=[self.arF], writes=[self.arG, sqs])
        mmean, rstd = self.mmean, self.rstd
        t.op("vector", lambda EE: EE.tensor_scalar(mmean[:], msum[:], 1.0 / D, None, A.mult),
             reads=[msum], writes=[mmean])
        t.op("vector", lambda EE: EE.tensor_tensor(self.zcol[:], mmean[:], mmean[:], A.mult),
             reads=[mmean], writes=[self.zcol])
        t.op("vector", lambda EE: EE.scalar_tensor_tensor(sqs[:], sqs[:], 1.0 / D, self.zcol[:], A.mult, A.subtract),
             reads=[sqs, self.zcol], writes=[sqs])
        self.rsqrt_newton(sqs[:], rstd[:], LN_EPS, sqs, rstd, TT)
        stg = self.arW.ap()
        for i in range(TT):
            t.op("sync", (lambda i: lambda EE: EE.dma_start(
                out=xtile[:, 0:1024], in_=self.x1_dram[i * 128:(i + 1) * 128, :]))(i),
                reads=[self.x1_dram], writes=[self.arF], dma=True)
            t.op("vector", (lambda i: lambda EE: EE.tensor_scalar(
                xtile[:, 1024:2048], xtile[:, 0:1024], mmean[:, i:i + 1], rstd[:, i:i + 1],
                A.subtract, A.mult))(i),
                reads=[self.arF, mmean, rstd], writes=[self.arF])
            t.op("sync", (lambda i: lambda EE: EE.dma_start(
                out=self.xn1f_dram[i * 128:(i + 1) * 128, :], in_=xtile[:, 1024:2048]))(i),
                reads=[self.arF], writes=[self.xn1f_dram], dma=True)
            t.op("vector", (lambda i: lambda EE: EE.tensor_copy(
                stg[:, 0:1024], xtile[:, 1024:2048]))(i),
                reads=[self.arF], writes=[self.arW])
            t.op("sync", (lambda i: lambda EE: EE.dma_start(
                out=self.xn1b_dram[i * 128:(i + 1) * 128, :], in_=stg[:, 0:1024]))(i),
                reads=[self.arW], writes=[self.xn1b_dram], dma=True)
        xn1_fm = self.arB.ap().rearrange("p (c n) -> p c n", n=NT)
        self.xn1_fm = xn1_fm
        for c in range(DC):
            t.op("scalar", (lambda c: lambda EE: EE.dma_start_transpose(
                xn1_fm[:, c, :], self.xn1b_dram[:, c * 128:(c + 1) * 128]))(c),
                reads=[self.xn1b_dram], writes=[self.arB], dma=True)
        q_tm = self.arC.ap().rearrange("p (i d) -> p i d", d=1024)
        k_tm = self.arD.ap().rearrange("p (i d) -> p i d", d=1024)
        v_tm = self.arA.ap().rearrange("p (i d) -> p i d", d=1024)
        wpb = self.arW.ap().rearrange("p (c n) -> p c n", n=1024)
        ps = self.ps
        beta_tm = self.small
        for wnm, dst, dstar in (("wq", q_tm, self.arC), ("wk", k_tm, self.arD), ("wv", v_tm, self.arA)):
            wap = getattr(self, wnm)
            for c in range(DC):
                t.op("gpsimd", (lambda c, wap: lambda EE: EE.dma_start(
                    out=wpb[:, c, :], in_=wap[c * 128:(c + 1) * 128, :]))(c, wap),
                    reads=[wap, self.arW], writes=[self.arW], dma=True)
            for i in range(TT):
                for n in range(2):
                    p = ps[n]
                    sl = slice(n * 512, (n + 1) * 512)
                    for c in range(DC):
                        t.op("tensor", (lambda i, c, sl, p: lambda EE: EE.matmul(
                            p[:], xn1_fm[:, c, i * 128:(i + 1) * 128], wpb[:, c, sl],
                            start=(c == 0), stop=(c == DC - 1)))(i, c, sl, p),
                            reads=[self.arB, self.arW], writes=[p])
                    t.op("vector", (lambda i, sl, p, dst, dstar: lambda EE: EE.tensor_copy(
                        dst[:, i, sl], p[:]))(i, sl, p, dst, dstar),
                        reads=[p], writes=[dstar])
        wbb = self.smh.ap().bitcast(BF16)
        for c in range(DC):
            t.op("gpsimd", (lambda c: lambda EE: EE.dma_start(
                out=wbb[:, c * NH:(c + 1) * NH], in_=self.wbeta[c * 128:(c + 1) * 128, :]))(c),
                reads=[self.wbeta, self.smh], writes=[self.smh], dma=True)
        p5 = self.ps[5]
        for i in range(TT):
            for c in range(DC):
                t.op("tensor", (lambda i, c: lambda EE: EE.matmul(
                    p5[:, 0:NH], xn1_fm[:, c, i * 128:(i + 1) * 128], wbb[:, c * NH:(c + 1) * NH],
                    start=(c == 0), stop=(c == DC - 1)))(i, c),
                    reads=[self.arB, self.smh], writes=[p5])
            t.op("scalar", (lambda i: lambda EE: EE.activation(
                out=beta_tm[:, 1536 + i * NH:1536 + (i + 1) * NH], in_=p5[:, 0:NH], func=AF.Sigmoid))(i),
                reads=[p5], writes=[self.small])
        for dst, dstar in ((q_tm, self.arC), (k_tm, self.arD)):
            for i in range(TT):
                for h in range(NH):
                    t.op("scalar", (lambda i, h, dst, dstar: lambda EE: EE.activation(
                        out=self.arG[:, 0:DK], in_=dst[:, i, h * DK:(h + 1) * DK], func=AF.Square,
                        accum_out=self.arG[:, 1024 + h:1024 + h + 1]))(i, h, dst, dstar),
                        reads=[dstar], writes=[self.arG])
                v4 = self.arG[:, 1024:1024 + NH]
                o4 = self.arG[:, 1028:1028 + NH]
                t.op("vector", (lambda v4, o4: lambda EE: EE.reciprocal(o4, v4))(v4, o4),
                     reads=[self.arG], writes=[self.arG])
                t.op("scalar", (lambda o4: lambda EE: EE.activation(out=o4, in_=o4, func=AF.Sqrt))(o4),
                     reads=[self.arG], writes=[self.arG])
                t.op("vector", (lambda v4, o4: lambda EE: EE.tensor_tensor(
                    self.arG[:, 1032:1036], o4, o4, A.mult))(v4, o4),
                    reads=[self.arG], writes=[self.arG])
                t.op("vector", (lambda v4: lambda EE: EE.tensor_tensor(
                    self.arG[:, 1032:1036], self.arG[:, 1032:1036], v4, A.mult))(v4),
                    reads=[self.arG], writes=[self.arG])
                t.op("vector", lambda EE: EE.tensor_scalar(
                    self.arG[:, 1032:1036], self.arG[:, 1032:1036], -0.5, 1.5, A.mult, A.add),
                    reads=[self.arG], writes=[self.arG])
                t.op("vector", (lambda o4: lambda EE: EE.tensor_tensor(
                    o4, o4, self.arG[:, 1032:1036], A.mult))(o4),
                    reads=[self.arG], writes=[self.arG])
                t.op("vector", (lambda i, dst, dstar, o4: lambda EE: EE.tensor_tensor(
                    dst[:, i, :].rearrange("p (h d) -> p h d", d=DK),
                    dst[:, i, :].rearrange("p (h d) -> p h d", d=DK),
                    o4.unsqueeze(2).broadcast_to([128, NH, DK]), A.mult))(i, dst, dstar, o4),
                    reads=[self.arG, dstar], writes=[dstar])
        for i in range(TT):
            t.op("sync", (lambda i: lambda EE: EE.dma_start(
                out=self.qkvb_in[i * 128:(i + 1) * 128, 0:D], in_=q_tm[:, i, :]))(i),
                reads=[self.arC], writes=[self.qkvb_in], dma=True)
            t.op("sync", (lambda i: lambda EE: EE.dma_start(
                out=self.qkvb_in[i * 128:(i + 1) * 128, D:2 * D], in_=k_tm[:, i, :]))(i),
                reads=[self.arD], writes=[self.qkvb_in], dma=True)
            t.op("sync", (lambda i: lambda EE: EE.dma_start(
                out=self.qkvb_in[i * 128:(i + 1) * 128, 2 * D:3 * D], in_=v_tm[:, i, :]))(i),
                reads=[self.arA], writes=[self.qkvb_in], dma=True)
            t.op("vector", (lambda i: lambda EE: EE.tensor_copy(
                self.arW.ap()[:, 7000:7000 + NH],
                beta_tm[:, 1536 + i * NH:1536 + (i + 1) * NH]))(i),
                reads=[self.small], writes=[self.arW])
            t.op("sync", (lambda i: lambda EE: EE.dma_start(
                out=self.bb_in[i * 128:(i + 1) * 128, 0:2],
                in_=self.arW.ap()[:, 7000:7002]))(i),
                reads=[self.arW], writes=[self.bb_in], dma=True)
            t.op("sync", (lambda i: lambda EE: EE.dma_start(
                out=self.bb_in[i * 128:(i + 1) * 128, 8:10],
                in_=self.arW.ap()[:, 7002:7004]))(i),
                reads=[self.arW], writes=[self.bb_in], dma=True)
        t.op("gpsimd", lambda EE: EE.collective_compute(
            "AllGather", A.bypass, replica_groups=[[0, 1], [2, 3], [4, 5], [6, 7]],
            ins=[self.qkvb_in.opt()], outs=[self.qkvb_out.opt()]),
            reads=[self.qkvb_in], writes=[self.qkvb_out])
        t.op("gpsimd", lambda EE: EE.collective_compute(
            "AllGather", A.bypass, replica_groups=[[0, 1], [2, 3], [4, 5], [6, 7]],
            ins=[self.bb_in.opt()], outs=[self.bb_out.opt()]),
            reads=[self.bb_in], writes=[self.bb_out])
        qidx = t.sbuf("qidx", [128, 96], U32)
        bidx = t.sbuf("bidx", [128, 32], U32)
        odx = t.sbuf("odx", [128, 32], U32)
        self.qidx, self.bidx, self.odx = qidx, bidx, odx
        t.op("sync", lambda EE: EE.dma_start(out=qidx[:], in_=self.qkvidx), reads=[self.qkvidx], writes=[qidx], dma=True)
        t.op("sync", lambda EE: EE.dma_start(out=bidx[:], in_=self.betaidx), reads=[self.betaidx], writes=[bidx], dma=True)
        t.op("sync", lambda EE: EE.dma_start(out=odx[:], in_=self.oidx), reads=[self.oidx], writes=[odx], dma=True)
        view6 = self.qkvb_out.rearrange("r (b w) -> (r b) w", w=512)
        viewB = self.bb_out.rearrange("r (b w) -> (r b) w", w=8)
        stgb = self.arW.ap()
        for ch in range(NCH):
            for tt in range(3):
                t.op("gpsimd", (lambda ch, tt: lambda EE: EE.indirect_dma_start(
                    stgb[:, 6144 + tt * 512:6144 + (tt + 1) * 512], None, view6,
                    IndirectOffsetOnAxis(qidx[:, ch * 3 + tt:ch * 3 + tt + 1], 0)))(ch, tt),
                    reads=[self.qkvb_out, qidx, self.arW], writes=[self.arW], dma=True)
            t.op("sync", (lambda ch: lambda EE: EE.dma_start(
                out=self.qkv_my[ch * 128:(ch + 1) * 128, :], in_=stgb[:, 6144:6144 + 1536]))(ch),
                reads=[self.arW], writes=[self.qkv_my], dma=True)
            t.op("gpsimd", (lambda ch: lambda EE: EE.indirect_dma_start(
                stgb[:, 7680:7688], None, viewB,
                IndirectOffsetOnAxis(bidx[:, ch:ch + 1], 0)))(ch),
                reads=[self.bb_out, bidx, self.arW], writes=[self.arW], dma=True)
            t.op("sync", (lambda ch: lambda EE: EE.dma_start(
                out=self.beta_my[ch * 128:(ch + 1) * 128, :], in_=stgb[:, 7680:7688]))(ch),
                reads=[self.arW], writes=[self.beta_my], dma=True)
        self.tap("qkvmy", self.qkv_my, [self.qkv_my])
        self.tap("betamy", self.beta_my, [self.beta_my])

    @staticmethod
    def _rr(slot_gens):
        its = [iter(g) for g in slot_gens]
        alive = [True] * len(its)
        while any(alive):
            for i, it in enumerate(its):
                if alive[i]:
                    try:
                        next(it)
                    except StopIteration:
                        alive[i] = False

    def _delta_chunkA(self, ridx, s):
        t = self.t
        ps = self.ps
        aA = self.arA.ap()
        scr = self.arG
        hh, ch = ridx // NCH, ridx % NCH
        r0 = ch * 128
        if s < 3:
            aA = self.arA.ap()
            b = s * 3072
            pA, pB = ps[2 * s], ps[2 * s + 1]
        else:
            aA = self.arB.ap()
            b = 0
            pA, pB = self.pw.ap()[:, 0:512], self.pw.ap()[:, 512:1024]
        ktm = aA[:, b + 0:b + 256]; vtm = aA[:, b + 256:b + 512]
        Kb = aA[:, b + 512:b + 768]; Vb = aA[:, b + 768:b + 1024]
        qfm = aA[:, b + 1024:b + 1280]; kfm = aA[:, b + 1280:b + 1536]
        Am = aA[:, b + 1536:b + 1664]; ATm = aA[:, b + 1664:b + 1792]
        negMT = aA[:, b + 1792:b + 1920]; Xm = aA[:, b + 1920:b + 2048]
        XTm = aA[:, b + 2048:b + 2176]; Z2 = aA[:, b + 2176:b + 2304]
        WTm = aA[:, b + 2432:b + 2688]
        Um = aA[:, b + 2688:b + 2944]; AqkTm = aA[:, b + 2944:b + 3072]
        bcol = scr[:, 1024 + ch * 8 + hh:1024 + ch * 8 + hh + 1]
        pT = (pB.ap() if callable(getattr(pB, 'ap', None)) else pB).bitcast(BF16)[:, 0:128]
        t.op("sync", (lambda hh, r0, ktm: lambda EE: EE.dma_start(
            out=ktm, in_=self.qkv_my[r0:r0 + 128, 512 + hh * DK:512 + (hh + 1) * DK]))(hh, r0, ktm),
            reads=[self.qkv_my[r0:r0 + 128, 512 + hh * DK:512 + (hh + 1) * DK]], writes=[ktm], dma=True)
        t.op("sync", (lambda hh, r0, vtm: lambda EE: EE.dma_start(
            out=vtm, in_=self.qkv_my[r0:r0 + 128, 1024 + hh * DK:1024 + (hh + 1) * DK]))(hh, r0, vtm),
            reads=[self.qkv_my[r0:r0 + 128, 1024 + hh * DK:1024 + (hh + 1) * DK]], writes=[vtm], dma=True)
        yield
        for c2 in range(2):
            t.op("scalar", (lambda hh, r0, c2, qfm: lambda EE: EE.dma_start_transpose(
                qfm[:, c2 * 128:(c2 + 1) * 128],
                self.qkv_my[r0:r0 + 128, hh * DK + c2 * 128:hh * DK + (c2 + 1) * 128]))(hh, r0, c2, qfm),
                reads=[self.qkv_my[r0:r0 + 128, hh * DK + c2 * 128:hh * DK + (c2 + 1) * 128]],
                writes=[qfm[:, c2 * 128:(c2 + 1) * 128]], dma=True)
            t.op("scalar", (lambda hh, r0, c2, kfm: lambda EE: EE.dma_start_transpose(
                kfm[:, c2 * 128:(c2 + 1) * 128],
                self.qkv_my[r0:r0 + 128, 512 + hh * DK + c2 * 128:512 + hh * DK + (c2 + 1) * 128]))(hh, r0, c2, kfm),
                reads=[self.qkv_my[r0:r0 + 128, 512 + hh * DK + c2 * 128:512 + hh * DK + (c2 + 1) * 128]],
                writes=[kfm[:, c2 * 128:(c2 + 1) * 128]], dma=True)
            yield
        t.op("vector", (lambda bcol, ktm, Kb: lambda EE: EE.tensor_scalar(
            Kb, ktm, bcol, None, A.mult))(bcol, ktm, Kb),
            reads=[ktm, bcol], writes=[Kb])
        t.op("vector", (lambda bcol, vtm, Vb: lambda EE: EE.tensor_scalar(
            Vb, vtm, bcol, None, A.mult))(bcol, vtm, Vb),
            reads=[vtm, bcol], writes=[Vb])
        yield
        for c2 in range(2):
            t.op("tensor", (lambda c2, kfm, pA: lambda EE: EE.matmul(
                pA[:, 0:128], kfm[:, c2 * 128:(c2 + 1) * 128], kfm[:, c2 * 128:(c2 + 1) * 128],
                start=(c2 == 0), stop=(c2 == 1)))(c2, kfm, pA),
                reads=[kfm[:, c2 * 128:(c2 + 1) * 128]], writes=[pA[:, 0:128]])
        yield
        t.op("vector", (lambda bcol, pA, Am: lambda EE: EE.scalar_tensor_tensor(
            Am, pA[:, 0:128], bcol, self.c_trilxb[:], A.mult, A.mult))(bcol, pA, Am),
            reads=[pA[:, 0:128], bcol, self.c_trilxb], writes=[Am])
        yield
        t.op("tensor", (lambda Am, pT: lambda EE: EE.transpose(pT, Am, self.c_identb[:]))(Am, pT),
             reads=[Am, self.c_identb], writes=[pT])
        yield
        t.op("vector", (lambda ATm, pT: lambda EE: EE.tensor_copy(ATm, pT))(ATm, pT),
             reads=[pT], writes=[ATm])
        yield
        t.op("vector", (lambda ATm, negMT: lambda EE: EE.scalar_tensor_tensor(
            negMT, ATm, -1.0, self.c_identb[:], A.mult, A.subtract))(ATm, negMT),
            reads=[ATm, self.c_identb], writes=[negMT])
        t.op("vector", (lambda Am, Xm: lambda EE: EE.tensor_tensor(
            Xm, self.c_identb[:], Am, A.subtract))(Am, Xm),
            reads=[Am, self.c_identb], writes=[Xm])
        t.op("vector", (lambda ATm, XTm: lambda EE: EE.tensor_tensor(
            XTm, self.c_identb[:], ATm, A.subtract))(ATm, XTm),
            reads=[ATm, self.c_identb], writes=[XTm])
        yield
        for it in range(NEWTON):
            last = it == NEWTON - 1
            t.op("tensor", (lambda negMT, Xm, pA: lambda EE: EE.matmul(
                pA[:, 0:128], negMT, Xm, start=True, stop=False))(negMT, Xm, pA),
                reads=[negMT, Xm], writes=[pA[:, 0:128]])
            t.op("tensor", (lambda pA: lambda EE: EE.matmul(
                pA[:, 0:128], self.c_identb[:], self.c_ident2b[:], start=False, stop=True))(pA),
                reads=[self.c_identb, self.c_ident2b], writes=[pA[:, 0:128]])
            yield
            t.op("vector", (lambda Z2, pA: lambda EE: EE.tensor_copy(Z2, pA[:, 0:128]))(Z2, pA),
                 reads=[pA[:, 0:128]], writes=[Z2])
            yield
            t.op("tensor", (lambda XTm, Z2, pA: lambda EE: EE.matmul(
                pA[:, 0:128], XTm, Z2, start=True, stop=True))(XTm, Z2, pA),
                reads=[XTm, Z2], writes=[pA[:, 0:128]])
            if not last:
                t.op("tensor", (lambda Z2, XTm, pB: lambda EE: EE.matmul(
                    pB[:, 0:128], Z2, XTm, start=True, stop=True))(Z2, XTm, pB),
                    reads=[Z2, XTm], writes=[pB[:, 0:128]])
                yield
                t.op("scalar", (lambda XTm, pB: lambda EE: EE.activation(
                    out=XTm, in_=pB[:, 0:128], func=AF.Copy))(XTm, pB),
                    reads=[pB[:, 0:128]], writes=[XTm])
            else:
                yield
            t.op("vector", (lambda Xm, pA: lambda EE: EE.tensor_copy(Xm, pA[:, 0:128]))(Xm, pA),
                 reads=[pA[:, 0:128]], writes=[Xm])
            yield
        t.op("tensor", (lambda Xm, pT: lambda EE: EE.transpose(pT, Xm, self.c_identb[:]))(Xm, pT),
             reads=[Xm, self.c_identb], writes=[pT])
        yield
        t.op("vector", (lambda XTm, pT: lambda EE: EE.tensor_copy(XTm, pT))(XTm, pT),
             reads=[pT], writes=[XTm])
        yield
        for c2 in range(2):
            t.op("tensor", (lambda c2, Kb, XTm, pA: lambda EE: EE.matmul(
                pA[:, 0:128], Kb[:, c2 * 128:(c2 + 1) * 128], XTm,
                start=True, stop=True))(c2, Kb, XTm, pA),
                reads=[Kb[:, c2 * 128:(c2 + 1) * 128], XTm], writes=[pA[:, 0:128]])
            yield
            t.op("scalar", (lambda c2, WTm, pA: lambda EE: EE.activation(
                out=WTm[:, c2 * 128:(c2 + 1) * 128], in_=pA[:, 0:128], func=AF.Copy))(c2, WTm, pA),
                reads=[pA[:, 0:128]], writes=[WTm[:, c2 * 128:(c2 + 1) * 128]])
            yield
        t.op("tensor", (lambda XTm, Vb, pA: lambda EE: EE.matmul(
            pA[:, 0:256], XTm, Vb, start=True, stop=True))(XTm, Vb, pA),
            reads=[XTm, Vb], writes=[pA[:, 0:256]])
        yield
        t.op("vector", (lambda Um, pA: lambda EE: EE.tensor_copy(Um, pA[:, 0:256]))(Um, pA),
             reads=[pA[:, 0:256]], writes=[Um])
        yield
        for c2 in range(2):
            t.op("tensor", (lambda c2, kfm, qfm, pB: lambda EE: EE.matmul(
                pB[:, 0:128], kfm[:, c2 * 128:(c2 + 1) * 128], qfm[:, c2 * 128:(c2 + 1) * 128],
                start=(c2 == 0), stop=(c2 == 1)))(c2, kfm, qfm, pB),
                reads=[kfm[:, c2 * 128:(c2 + 1) * 128], qfm[:, c2 * 128:(c2 + 1) * 128]],
                writes=[pB[:, 0:128]])
        yield
        t.op("vector", (lambda AqkTm, pB: lambda EE: EE.tensor_tensor(
            AqkTm, pB[:, 0:128], self.c_triuib[:], A.mult))(AqkTm, pB),
            reads=[pB[:, 0:128], self.c_triuib], writes=[AqkTm])
        yield
        for dc0, sap, wlen in ((0, WTm, 256), (256, Um, 256), (512, AqkTm, 128),
                               (640, qfm, 256), (896, ktm, 256)):
            t.op("gpsimd", (lambda ridx, dc0, sap, wlen: lambda EE: EE.dma_start(
                out=self.dsc[ridx * 128:(ridx + 1) * 128, dc0:dc0 + wlen],
                in_=sap))(ridx, dc0, sap, wlen),
                reads=[sap], writes=[self.dsc[ridx * 128:(ridx + 1) * 128, dc0:dc0 + wlen]], dma=True)
        yield

    def _delta_chunkB(self, ch, hh):
        t = self.t
        ps = self.ps
        aA = self.arA.ap()
        o_all = self.arC.ap().rearrange("p (h c d) -> p h c d", h=2, d=DK)
        S_f32 = self.arF.ap().rearrange("p (k n) -> p k n", n=DK)
        S_bf = self.arD.ap().rearrange("p (k n) -> p k n", n=DK)
        ridx = hh * NCH + ch
        rb = 9216 + (hh * 2 + ch % 2) * 1408
        rec = aA[:, rb:rb + 1152]
        Uc = aA[:, rb + 1152:rb + 1408]
        t.op("gpsimd", (lambda ridx, rec: lambda EE: EE.dma_start(
            out=rec, in_=self.dsc[ridx * 128:(ridx + 1) * 128, 0:1152]))(ridx, rec),
            reads=[self.dsc[ridx * 128:(ridx + 1) * 128, 0:1152]], writes=[rec], dma=True)
        yield
        WT0 = rec[:, 0:128]
        WT1 = rec[:, 128:256]
        U_ = rec[:, 256:512]
        AqkT = rec[:, 512:640]
        qfm = rec[:, 640:896]
        ktm = rec[:, 896:1152]
        pW, pO = ps[hh * 3], ps[hh * 3 + 1]
        pS0 = ps[2] if hh == 0 else self.pw.ap()[:, 0:256]
        pS1 = ps[5] if hh == 0 else self.pw.ap()[:, 512:768]
        for c2 in range(2):
            t.op("tensor", (lambda hh, c2, WT0, WT1, pW: lambda EE: EE.matmul(
                pW[:, 0:256], (WT0 if c2 == 0 else WT1), S_bf[:, hh * 2 + c2, :],
                start=(c2 == 0), stop=(c2 == 1)))(hh, c2, WT0, WT1, pW),
                reads=[(WT0 if c2 == 0 else WT1), S_bf[:, hh * 2 + c2, :]], writes=[pW[:, 0:256]])
        yield
        t.op("vector", (lambda U_, Uc, pW: lambda EE: EE.tensor_tensor(
            Uc, U_, pW[:, 0:256], A.subtract))(U_, Uc, pW),
            reads=[U_, pW[:, 0:256]], writes=[Uc])
        yield
        for c2 in range(2):
            t.op("tensor", (lambda hh, c2, qfm, pO: lambda EE: EE.matmul(
                pO[:, 0:256], qfm[:, c2 * 128:(c2 + 1) * 128], S_bf[:, hh * 2 + c2, :],
                start=(c2 == 0), stop=False))(hh, c2, qfm, pO),
                reads=[qfm[:, c2 * 128:(c2 + 1) * 128], S_bf[:, hh * 2 + c2, :]], writes=[pO[:, 0:256]])
        t.op("tensor", (lambda AqkT, Uc, pO: lambda EE: EE.matmul(
            pO[:, 0:256], AqkT, Uc, start=False, stop=True))(AqkT, Uc, pO),
            reads=[AqkT, Uc], writes=[pO[:, 0:256]])
        yield
        t.op("vector", (lambda hh, ch, pO: lambda EE: EE.tensor_copy(
            o_all[:, hh, ch, :], pO[:, 0:256]))(hh, ch, pO),
            reads=[pO[:, 0:256]], writes=[o_all[:, hh, ch, :]])
        for c2 in range(2):
            pSc = pS0 if c2 == 0 else pS1
            t.op("tensor", (lambda c2, ktm, Uc, pSc: lambda EE: EE.matmul(
                pSc[:, 0:256], ktm[:, c2 * 128:(c2 + 1) * 128], Uc,
                start=True, stop=True))(c2, ktm, Uc, pSc),
                reads=[ktm[:, c2 * 128:(c2 + 1) * 128], Uc], writes=[pSc[:, 0:256]])
        yield
        for c2 in range(2):
            pSc = pS0 if c2 == 0 else pS1
            t.op("vector", (lambda hh, c2, pSc: lambda EE: EE.tensor_tensor(
                S_f32[:, hh * 2 + c2, :], S_f32[:, hh * 2 + c2, :], pSc[:, 0:256], A.add))(hh, c2, pSc),
                reads=[pSc[:, 0:256], S_f32[:, hh * 2 + c2, :]], writes=[S_f32[:, hh * 2 + c2, :]])
            t.op("vector", (lambda hh, c2: lambda EE: EE.tensor_copy(
                S_bf[:, hh * 2 + c2, :], S_f32[:, hh * 2 + c2, :]))(hh, c2),
                reads=[S_f32[:, hh * 2 + c2, :]], writes=[S_bf[:, hh * 2 + c2, :]])
        yield

    def emit_delta(self):
        t = self.t
        scr = self.arG
        S_f32 = self.arF.ap().rearrange("p (k n) -> p k n", n=DK)
        S_bf = self.arD.ap().rearrange("p (k n) -> p k n", n=DK)
        for ch in range(NCH):
            bb = scr[:, 1280 + ch * 4:1280 + ch * 4 + 4].bitcast(BF16)[:, 0:8]
            bf = scr[:, 1024 + ch * 8:1024 + ch * 8 + 8]
            t.op("gpsimd", (lambda ch, bb: lambda EE: EE.dma_start(
                out=bb, in_=self.beta_my[ch * 128:(ch + 1) * 128, 0:8]))(ch, bb),
                reads=[self.beta_my[ch * 128:(ch + 1) * 128, 0:8]], writes=[bb], dma=True)
            t.op("vector", (lambda bb, bf: lambda EE: EE.tensor_copy(bf, bb))(bb, bf),
                 reads=[bb], writes=[bf])
        NS = 4

        def slotA(s):
            for ridx in range(s, 2 * NCH, NS):
                for _ in self._delta_chunkA(ridx, s):
                    yield

        self._rr([slotA(s) for s in range(NS)] + [self.xq_gen(0), self.xq_gen(1)])
        for hh in range(2):
            t.op("vector", (lambda hh: lambda EE: EE.memset(S_f32[:, hh * 2:(hh + 1) * 2, :], 0.0))(hh),
                 writes=[S_f32[:, hh * 2:(hh + 1) * 2, :]])
            t.op("vector", (lambda hh: lambda EE: EE.memset(S_bf[:, hh * 2:(hh + 1) * 2, :], 0.0))(hh),
                 writes=[S_bf[:, hh * 2:(hh + 1) * 2, :]])

        def headB(hh):
            for ch in range(NCH):
                for _ in self._delta_chunkB(ch, hh):
                    yield

        self._rr([headB(0), headB(1)])

    def s1_gen(self):
        t = self.t
        ps = self.ps
        o_all = self.arC.ap().rearrange("p (h c d) -> p h c d", h=2, d=DK)
        sm = self.small
        for hh in range(2):
            for ch in range(NCH):
                t.op("scalar", (lambda hh, ch: lambda EE: EE.activation(
                    out=sm[:, 0:DK], in_=o_all[:, hh, ch, :], func=AF.Square,
                    accum_out=sm[:, 256 + ch:256 + ch + 1]))(hh, ch),
                    reads=[o_all[:, hh, ch, :]], writes=[sm[:, 0:DK], sm[:, 256 + ch:256 + ch + 1]])
                yield
            v32 = sm[:, 256:256 + NCH]
            o32 = sm[:, 288:288 + NCH]
            tmp = sm[:, 320:320 + NCH]
            t.op("vector", (lambda v32: lambda EE: EE.tensor_scalar(
                v32, v32, 1.0 / DK, None, A.mult))(v32), reads=[v32], writes=[v32])
            t.op("vector", (lambda v32: lambda EE: EE.tensor_scalar(
                v32, v32, RMS_EPS, None, A.add))(v32), reads=[v32], writes=[v32])
            t.op("vector", (lambda v32, o32: lambda EE: EE.reciprocal(o32, v32))(v32, o32),
                 reads=[v32], writes=[o32])
            t.op("scalar", (lambda o32: lambda EE: EE.activation(out=o32, in_=o32, func=AF.Sqrt))(o32),
                 reads=[o32], writes=[o32])
            yield
            t.op("vector", (lambda tmp, o32: lambda EE: EE.tensor_tensor(
                tmp, o32, o32, A.mult))(tmp, o32), reads=[o32], writes=[tmp])
            t.op("vector", (lambda tmp, v32: lambda EE: EE.tensor_tensor(
                tmp, tmp, v32, A.mult))(tmp, v32), reads=[tmp, v32], writes=[tmp])
            t.op("vector", (lambda tmp: lambda EE: EE.tensor_scalar(
                tmp, tmp, -0.5, 1.5, A.mult, A.add))(tmp), reads=[tmp], writes=[tmp])
            t.op("vector", (lambda o32, tmp: lambda EE: EE.tensor_tensor(
                o32, o32, tmp, A.mult))(o32, tmp), reads=[o32, tmp], writes=[o32])
            yield
            t.op("vector", (lambda hh, o32: lambda EE: EE.tensor_tensor(
                o_all[:, hh], o_all[:, hh],
                o32.unsqueeze(2).broadcast_to([128, NCH, DK]), A.mult))(hh, o32),
                reads=[o32, o_all[:, hh]], writes=[o_all[:, hh]])
            t.op("vector", (lambda hh: lambda EE: EE.tensor_tensor(
                o_all[:, hh], o_all[:, hh],
                self.c_gnb[:, 0:DK].rearrange("p (o d) -> p o d", o=1).broadcast_to([128, NCH, DK]),
                A.mult))(hh),
                reads=[self.c_gnb, o_all[:, hh]], writes=[o_all[:, hh]])
            yield
            for ch in range(NCH):
                t.op("sync", (lambda hh, ch: lambda EE: EE.dma_start(
                    out=self.o_in[ch * 128:(ch + 1) * 128, hh * DK:(hh + 1) * DK],
                    in_=o_all[:, hh, ch, :]))(hh, ch),
                    reads=[o_all[:, hh, ch, :]],
                    writes=[self.o_in[ch * 128:(ch + 1) * 128, hh * DK:(hh + 1) * DK]], dma=True)
                if ch % 8 == 7:
                    yield
        t.op("gpsimd", lambda EE: EE.collective_compute(
            "AllGather", A.bypass, replica_groups=[[0, 1], [2, 3], [4, 5], [6, 7]],
            ins=[self.o_in.opt()], outs=[self.o_out.opt()]),
            reads=[self.o_in], writes=[self.o_out])
        yield
        stg = self.arW.ap()
        for hf in range(2):
            for ch in range(TT):
                t.op("gpsimd", (lambda hf, ch: lambda EE: EE.indirect_dma_start(
                    stg[:, 6144:6656], None, self.o_out,
                    IndirectOffsetOnAxis(self.odx[:, hf * 16 + ch:hf * 16 + ch + 1], 0)))(hf, ch),
                    reads=[self.o_out, self.odx, stg[:, 6144:6656]], writes=[stg[:, 6144:6656]], dma=True)
                t.op("sync", (lambda hf, ch: lambda EE: EE.dma_start(
                    out=self.o_my[ch * 128:(ch + 1) * 128, hf * 512:(hf + 1) * 512],
                    in_=stg[:, 6144:6656]))(hf, ch),
                    reads=[stg[:, 6144:6656]],
                    writes=[self.o_my[ch * 128:(ch + 1) * 128, hf * 512:(hf + 1) * 512]], dma=True)
                yield
        o_fm = self.arA.ap().rearrange("p (c n) -> p c n", n=NT)
        for c in range(DC):
            t.op("scalar", (lambda c: lambda EE: EE.dma_start_transpose(
                o_fm[:, c, :], self.o_my[:, c * 128:(c + 1) * 128]))(c),
                reads=[self.o_my], writes=[o_fm[:, c, :]], dma=True)
            yield
        wob = self.arD.ap().rearrange("p (c n) -> p c n", n=1024)
        for c in range(DC):
            t.op("gpsimd", (lambda c: lambda EE: EE.dma_start(
                out=wob[:, c, :], in_=self.wo[c * 128:(c + 1) * 128, :]))(c),
                reads=[self.wo], writes=[wob[:, c, :]], dma=True)
        yield
        sstg = sm[:, 512:1024].bitcast(BF16)
        for i in range(TT):
            for n in range(2):
                p = ps[4 + n]
                sl = slice(n * 512, (n + 1) * 512)
                for c in range(DC):
                    t.op("tensor", (lambda i, c, sl, p: lambda EE: EE.matmul(
                        p[:], o_fm[:, c, i * 128:(i + 1) * 128], wob[:, c, sl],
                        start=(c == 0), stop=(c == DC - 1)))(i, c, sl, p),
                        reads=[o_fm[:, c, i * 128:(i + 1) * 128], wob[:, c, sl]], writes=[p])
                yield
                t.op("vector", (lambda n, p: lambda EE: EE.tensor_copy(
                    sstg[:, n * 512:(n + 1) * 512], p[:]))(n, p),
                    reads=[p], writes=[sstg[:, n * 512:(n + 1) * 512]])
                yield
            t.op("sync", (lambda i: lambda EE: EE.dma_start(
                out=self.s1_dram[i * 128:(i + 1) * 128, :], in_=sstg[:, 0:1024]))(i),
                reads=[sstg[:, 0:1024]], writes=[self.s1_dram[i * 128:(i + 1) * 128, :]], dma=True)
            yield

    def quant_gen(self, xin, xqout, w, qarena, qbase, cs, sb):
        t = self.t
        g = qarena.ap() if hasattr(qarena, "ap") else qarena
        nb = w // 16
        am = self.small[:, sb:sb + nb]
        rs = self.small[:, sb + nb:sb + 2 * nb]
        t.op("vector", (lambda xin, am: lambda EE: EE.tensor_reduce(
            am, xin.rearrange("p (b s) -> p b s", s=16), AX.X, A.max,
            apply_absolute_value=True))(xin, am), reads=[xin], writes=[am])
        t.op("vector", (lambda am: lambda EE: EE.tensor_scalar(
            am, am, 1.0 / 6.0, 1e-12, A.mult, A.max))(am), reads=[am], writes=[am])
        t.op("vector", (lambda am, rs: lambda EE: EE.reciprocal(rs, am))(am, rs),
             reads=[am], writes=[rs])
        yield
        c0 = 0
        while c0 < w:
            cw = min(cs, w - c0)
            cb = cw // 16
            b1 = g[:, qbase:qbase + cw]
            b2 = g[:, qbase + cs:qbase + cs + cw]
            b3 = g[:, qbase + 2 * cs:qbase + 2 * cs + cw]
            b4 = g[:, qbase + 3 * cs:qbase + 3 * cs + cw]
            xs = xin[:, c0:c0 + cw]
            rss = rs[:, c0 // 16:c0 // 16 + cb]
            ams = am[:, c0 // 16:c0 // 16 + cb]
            t.op("vector", (lambda xs, rss, b1, cb, cw: lambda EE: EE.tensor_tensor(
                b1.rearrange("p (b s) -> p b s", s=16), xs.rearrange("p (b s) -> p b s", s=16),
                rss.unsqueeze(2).broadcast_to([128, cb, 16]), A.mult))(xs, rss, b1, cb, cw),
                reads=[xs, rss], writes=[b1])
            t.op("vector", (lambda b1: lambda EE: EE.tensor_scalar(
                b1.bitcast(I32), b1.bitcast(I32), self.cmask1[:, 0:1], None, A.bitwise_and))(b1),
                reads=[b1, self.cmask1], writes=[b1])
            yield
            t.op("vector", (lambda b1, b2: lambda EE: EE.tensor_scalar(
                b2, b1, 1.0, None, A.max))(b1, b2), reads=[b1], writes=[b2])
            t.op("vector", (lambda b2, b3: lambda EE: EE.tensor_scalar(
                b3.bitcast(I32), b2.bitcast(I32), self.cmask2[:, 0:1], None, A.bitwise_and))(b2, b3),
                reads=[b2, self.cmask2], writes=[b3])
            yield
            t.op("vector", (lambda b3: lambda EE: EE.tensor_scalar(
                b3, b3.bitcast(I32), 2097152.0, None, A.is_gt))(b3), reads=[b3], writes=[b3])
            t.op("vector", (lambda b2, b4: lambda EE: EE.tensor_scalar(
                b4.bitcast(I32), b2.bitcast(I32), self.cmask3[:, 0:1], None, A.bitwise_and))(b2, b4),
                reads=[b2, self.cmask3], writes=[b4])
            yield
            t.op("vector", (lambda b3, b4: lambda EE: EE.scalar_tensor_tensor(
                b4, b3, 4194304.0, b4.bitcast(I32), A.mult, A.add))(b3, b4),
                reads=[b3, b4], writes=[b4])
            t.op("vector", (lambda b3, b4: lambda EE: EE.tensor_copy(b3.bitcast(I32), b4))(b3, b4),
                 reads=[b4], writes=[b3])
            yield
            t.op("scalar", (lambda b1, b2: lambda EE: EE.activation(
                out=b2, in_=b1, func=AF.Identity, scale=self.c_actc[:, 0:1], bias=self.c_actc[:, 1:2]))(b1, b2),
                reads=[b1, self.c_actc], writes=[b2])
            t.op("vector", (lambda b2: lambda EE: EE.tensor_scalar(
                b2, b2, 0.5, None, A.subtract))(b2), reads=[b2], writes=[b2])
            yield
            t.op("scalar", (lambda b1, b4: lambda EE: EE.activation(
                out=b4, in_=b1, func=AF.Identity, scale=self.c_actc[:, 0:1], bias=self.c_actc[:, 2:3]))(b1, b4),
                reads=[b1, self.c_actc], writes=[b4])
            yield
            t.op("vector", (lambda b3, b4: lambda EE: EE.tensor_tensor(
                b4, b4, b3, A.min))(b3, b4), reads=[b3, b4], writes=[b4])
            t.op("scalar", (lambda b4: lambda EE: EE.activation(
                out=b4, in_=b4, func=AF.Relu, bias=self.c_actc[:, 3:4]))(b4),
                reads=[b4, self.c_actc], writes=[b4])
            yield
            t.op("vector", (lambda b2, b4: lambda EE: EE.tensor_tensor(
                b4, b2, b4, A.min))(b2, b4), reads=[b2, b4], writes=[b4])
            t.op("scalar", (lambda b4: lambda EE: EE.activation(
                out=b4, in_=b4, func=AF.Relu, bias=self.c_actc[:, 4:5]))(b4),
                reads=[b4, self.c_actc], writes=[b4])
            yield
            t.op("scalar", (lambda xs, b1: lambda EE: EE.activation(
                out=b1, in_=xs, func=AF.Sign))(xs, b1), reads=[xs], writes=[b1])
            t.op("vector", (lambda b1, b4: lambda EE: EE.tensor_tensor(
                b4, b4, b1, A.mult))(b1, b4), reads=[b1, b4], writes=[b4])
            yield
            t.op("vector", (lambda b4, ams, cb, c0, cw: lambda EE: EE.tensor_tensor(
                xqout[:, c0:c0 + cw].rearrange("p (b s) -> p b s", s=16),
                b4.rearrange("p (b s) -> p b s", s=16),
                ams.unsqueeze(2).broadcast_to([128, cb, 16]), A.mult))(b4, ams, cb, c0, cw),
                reads=[b4, ams], writes=[xqout[:, c0:c0 + cw]])
            yield
            c0 += cw

    def emit_wquant(self):
        t = self.t
        self.cmask1 = t.sbuf("cmask1", [128, 1], I32)
        self.cmask2 = t.sbuf("cmask2", [128, 1], I32)
        self.cmask3 = t.sbuf("cmask3", [128, 1], I32)
        t.op("vector", lambda EE: EE.memset(self.cmask1[:], 0x7FFFFFFF), writes=[self.cmask1])
        t.op("vector", lambda EE: EE.memset(self.cmask2[:], 0x3FFFFF), writes=[self.cmask2])
        t.op("vector", lambda EE: EE.memset(self.cmask3[:], -4194304), writes=[self.cmask3])
        jobs = []
        for src_ap, dsts, nrow, ncol in ((self.wg_nat, [(self.wgq_nat, 0, 128)], FI, 128),
                                         (self.wu_nat, [(self.wuq_nat, 0, 128)], FI, 128),
                                         (self.wd_nat, [(self.wdq_a, 0, 128), (self.wdq_b, 128, 128),
                                                        (self.wdq_c, 256, 96)], D, FI // 8)):
            for r0 in range(0, nrow, 128):
                jobs.append((src_ap, r0, ncol, dsts))

        def wq_stream(k):
            arF = self.arF.ap()
            arW = self.arW.ap()
            for j in range(k, len(jobs), 2):
                src_ap, r0, ncol, dsts = jobs[j]
                wtile = arF[:, 2048 + k * 384:2048 + k * 384 + ncol]
                wq = arW[:, 7296 + k * 384:7296 + k * 384 + ncol]
                t.op("scalar", (lambda src_ap, r0, ncol, wtile: lambda EE: EE.dma_start(
                    out=wtile, in_=src_ap[r0:r0 + 128, :]))(src_ap, r0, ncol, wtile),
                    reads=[src_ap[r0:r0 + 128, :]], writes=[wtile], dma=True)
                yield
                if k == 0:
                    qa, qb = self.arG, 0
                else:
                    qa, qb = self.arF, 2816
                for _ in self.quant_gen(wtile, wq, ncol, qa, qb, 256, 1600 + k * 64):
                    yield
                for dst_ap, cb, cn in dsts:
                    t.op("scalar", (lambda dst_ap, r0, cb, cn, wq: lambda EE: EE.dma_start(
                        out=dst_ap[r0:r0 + 128, 0:cn], in_=wq[:, cb:cb + cn]))(dst_ap, r0, cb, cn, wq),
                        reads=[wq[:, cb:cb + cn]], writes=[dst_ap[r0:r0 + 128, 0:cn]], dma=True)
                yield

        self._rr([wq_stream(0), wq_stream(1)])
        t.op("vector", lambda EE: EE.memset(self.arG[:, 0:32], 0.0), writes=[self.arG.ap()[:, 0:32]])
        for r0 in range(0, D, 128):
            t.op("scalar", (lambda r0: lambda EE: EE.dma_start(
                out=self.wdq_c[r0:r0 + 128, 96:128], in_=self.arG.ap()[:, 0:32].bitcast(BF16)[:, 0:32]))(r0),
                reads=[self.arG.ap()[:, 0:32]], writes=[self.wdq_c[r0:r0 + 128, 96:128]], dma=True)
        stg = self.arW.ap()
        t.op("scalar", lambda EE: EE.dma_start_transpose(stg[:, 0:FI], self.wgq_nat),
             reads=[self.wgq_nat], writes=[stg[:, 0:FI]], dma=True)
        t.op("scalar", lambda EE: EE.dma_start(out=self.wgu_in[:, 0:FI], in_=stg[:, 0:FI]),
             reads=[stg[:, 0:FI]], writes=[self.wgu_in], dma=True)
        t.op("scalar", lambda EE: EE.dma_start_transpose(stg[:, 0:FI], self.wuq_nat),
             reads=[self.wgu_in, self.wuq_nat], writes=[stg[:, 0:FI]], dma=True)
        t.op("scalar", lambda EE: EE.dma_start(out=self.wgu_in[:, FI:2 * FI], in_=stg[:, 0:FI]),
             reads=[stg[:, 0:FI]], writes=[self.wgu_in], dma=True)
        for srcw, rr0, nn in ((self.wdq_a, 0, 128), (self.wdq_b, 128, 128), (self.wdq_c, 256, 96)):
            t.op("scalar", (lambda srcw: lambda EE: EE.dma_start_transpose(
                stg[0:128, 0:D], srcw))(srcw),
                reads=[srcw, self.wd_in], writes=[stg[:, 0:D]], dma=True)
            t.op("scalar", (lambda rr0, nn: lambda EE: EE.dma_start(
                out=self.wd_in[rr0:rr0 + nn, :], in_=stg[0:nn, 0:D]))(rr0, nn),
                reads=[stg[:, 0:D]], writes=[self.wd_in[rr0:rr0 + nn, :]], dma=True)
        t.op("gpsimd", lambda EE: EE.collective_compute(
            "AllGather", A.bypass, replica_groups=[list(range(NCORE))],
            ins=[self.wgu_in.opt()], outs=[self.wgu_out.opt()]),
            reads=[self.wgu_in], writes=[self.wgu_out])
        t.op("gpsimd", lambda EE: EE.collective_compute(
            "AllGather", A.bypass, replica_groups=[list(range(NCORE))],
            ins=[self.wd_in.opt()], outs=[self.wd_out.opt()]),
            reads=[self.wd_in], writes=[self.wd_out])

    def xq_gen(self, k=0):
        t = self.t
        arF = self.arF.ap()
        arW = self.arW.ap()
        for i in range(k, TT, 2):
            xtile = arF[:, 2048:3072] if k == 0 else arF[:, 1024:2048]
            xq = arW[:, 5120:6144] if k == 0 else arW[:, 4096:5120]
            t.op("scalar", (lambda i, xtile: lambda EE: EE.dma_start(
                out=xtile, in_=self.xn1f_dram[i * 128:(i + 1) * 128, :]))(i, xtile),
                reads=[self.xn1f_dram[i * 128:(i + 1) * 128, :]], writes=[xtile], dma=True)
            yield
            for _ in self.quant_gen(xtile, xq, 1024, self.arG, 0, 256, 1728) if k == 0 else \
                     self.quant_gen(xtile, xq, 1024, self.arF, 3072, 256, 1472):
                yield
            t.op("scalar", (lambda i, xq: lambda EE: EE.dma_start(
                out=self.xq_dram[i * 128:(i + 1) * 128, :], in_=xq))(i, xq),
                reads=[xq], writes=[self.xq_dram[i * 128:(i + 1) * 128, :]], dma=True)
            yield

    def gu_tiles(self, k, o0, ow):
        t = self.t
        ps = self.ps
        arF = self.arF.ap()
        arW = self.arW.ap()
        xq_fm = self.arB.ap().rearrange("p (c n) -> p c n", n=NT)
        wE = self.arE.ap().rearrange("p (c n) -> p c n", n=512)
        pG, pU = ps[2 * k], ps[2 * k + 1]
        for i in range(k, TT, 2):
            for c in range(DC):
                t.op("tensor", (lambda i, c, ow, pG: lambda EE: EE.matmul(
                    pG[:, 0:ow], xq_fm[:, c, i * 128:(i + 1) * 128], wE[:, c, 0:ow],
                    start=(c == 0), stop=(c == DC - 1)))(i, c, ow, pG),
                    reads=[xq_fm[:, c, i * 128:(i + 1) * 128], wE[:, c, 0:ow]], writes=[pG])
            yield
            for c in range(DC):
                t.op("tensor", (lambda i, c, ow, pU: lambda EE: EE.matmul(
                    pU[:, 0:ow], xq_fm[:, c, i * 128:(i + 1) * 128], wE[:, 8 + c, 0:ow],
                    start=(c == 0), stop=(c == DC - 1)))(i, c, ow, pU),
                    reads=[xq_fm[:, c, i * 128:(i + 1) * 128], wE[:, 8 + c, 0:ow]], writes=[pU])
            yield
            sg = arF[:, 3072 + k * 512:3072 + k * 512 + ow]
            hb = arF[:, 2048 + k * 512:2048 + k * 512 + ow]
            hqs = arW[:, 6656 + k * 512:6656 + k * 512 + ow]
            t.op("scalar", (lambda sg, pG, ow: lambda EE: EE.activation(
                out=sg, in_=pG[:, 0:ow], func=AF.Silu))(sg, pG, ow),
                reads=[pG], writes=[sg])
            yield
            t.op("vector", (lambda hb, sg, pU, ow: lambda EE: EE.tensor_tensor(
                hb, sg, pU[:, 0:ow], A.mult))(hb, sg, pU, ow),
                reads=[sg, pU], writes=[hb])
            yield
            for _ in self.quant_gen(hb, hqs, ow, self.arG, k * 1024, 256, 1856 + k * 64):
                yield
            t.op("sync", (lambda i, o0, ow, hqs: lambda EE: EE.dma_start(
                out=self.hq_dram[i * 128:(i + 1) * 128, o0:o0 + ow], in_=hqs))(i, o0, ow, hqs),
                reads=[hqs], writes=[self.hq_dram[i * 128:(i + 1) * 128, o0:o0 + ow]], dma=True)
            yield

    def emit_s1_gu(self):
        t = self.t
        xq_fm = self.arB.ap().rearrange("p (c n) -> p c n", n=NT)
        wE = self.arE.ap().rearrange("p (c n) -> p c n", n=512)
        for c in range(DC):
            t.op("scalar", (lambda c: lambda EE: EE.dma_start_transpose(
                xq_fm[:, c, :], self.xq_dram[:, c * 128:(c + 1) * 128]))(c),
                reads=[self.xq_dram], writes=[xq_fm[:, c, :]], dma=True)
        s1i = iter(self.s1_gen())
        s1_alive = [True]

        def step_s1():
            if s1_alive[0]:
                try:
                    next(s1i)
                except StopIteration:
                    s1_alive[0] = False

        OCS = [(0, 512), (512, 512), (1024, 512), (1536, 512), (2048, 512), (2560, 256)]
        for oci, (o0, ow) in enumerate(OCS):
            for c in range(DC):
                t.op("sync", (lambda c, o0, ow: lambda EE: EE.dma_start(
                    out=wE[:, c, 0:ow], in_=self.wgu_out[c * 128:(c + 1) * 128, o0:o0 + ow]))(c, o0, ow),
                    reads=[self.wgu_out], writes=[wE[:, c, 0:ow]], dma=True)
                t.op("sync", (lambda c, o0, ow: lambda EE: EE.dma_start(
                    out=wE[:, 8 + c, 0:ow], in_=self.wgu_out[c * 128:(c + 1) * 128, FI + o0:FI + o0 + ow]))(c, o0, ow),
                    reads=[self.wgu_out], writes=[wE[:, 8 + c, 0:ow]], dma=True)
            its = [iter(self.gu_tiles(0, o0, ow)), iter(self.gu_tiles(1, o0, ow))]
            alive = [True, True]
            while any(alive):
                step_s1()
                for i2, it in enumerate(its):
                    if alive[i2]:
                        try:
                            next(it)
                        except StopIteration:
                            alive[i2] = False
        while s1_alive[0]:
            step_s1()

    def emit_l1_down(self):
        t = self.t
        ps = self.ps
        KCH = FI // 128
        arC = self.arC.ap()
        arD = self.arD.ap()
        arF = self.arF.ap()
        arG = self.arG.ap()
        for tg in range(4):
            def pslot(t4, n):
                if t4 < 3:
                    return self.ps[t4 * 2 + n], self.ps[t4 * 2 + n][:]
                return self.pw, self.pw[:, n * 512:(n + 1) * 512]
            for kc in range(KCH):
                hqf = arC[:, (kc % 2) * 512:(kc % 2) * 512 + 512]
                wdb = arD[:, 8192 + (kc % 2) * 1024:8192 + (kc % 2) * 1024 + 1024]
                t.op("scalar", (lambda tg, kc, hqf: lambda EE: EE.dma_start_transpose(
                    hqf, self.hq_dram[tg * 512:(tg + 1) * 512, kc * 128:(kc + 1) * 128]))(tg, kc, hqf),
                    reads=[self.hq_dram[tg * 512:(tg + 1) * 512, kc * 128:(kc + 1) * 128]],
                    writes=[hqf], dma=True)
                t.op("gpsimd", (lambda kc, wdb: lambda EE: EE.dma_start(
                    out=wdb, in_=self.wd_out[kc * 128:(kc + 1) * 128, :]))(kc, wdb),
                    reads=[self.wd_out], writes=[wdb], dma=True)
                for t4 in range(4):
                    for n in range(2):
                        ph, pap = pslot(t4, n)
                        t.op("tensor", (lambda kc, t4, n, pap, hqf, wdb: lambda EE: EE.matmul(
                            pap, hqf[:, t4 * 128:(t4 + 1) * 128], wdb[:, n * 512:(n + 1) * 512],
                            start=(kc == 0), stop=(kc == KCH - 1)))(kc, t4, n, pap, hqf, wdb),
                            reads=[hqf[:, t4 * 128:(t4 + 1) * 128], wdb[:, n * 512:(n + 1) * 512]],
                            writes=[ph])
            for t4 in range(4):
                i = tg * 4 + t4
                x1s = arF[:, 0:1024]
                s1s = arF[:, 1024:1536]
                ys = arF[:, 1536:2560]
                gs = arG[:, 0:1024]
                t.op("sync", (lambda i, x1s: lambda EE: EE.dma_start(
                    out=x1s, in_=self.x1_dram[i * 128:(i + 1) * 128, :]))(i, x1s),
                    reads=[self.x1_dram[i * 128:(i + 1) * 128, :]], writes=[x1s], dma=True)
                t.op("sync", (lambda i, s1s: lambda EE: EE.dma_start(
                    out=s1s.bitcast(BF16), in_=self.s1_dram[i * 128:(i + 1) * 128, :]))(i, s1s),
                    reads=[self.s1_dram[i * 128:(i + 1) * 128, :]], writes=[s1s], dma=True)
                for n in range(2):
                    ph, pap = pslot(t4, n)
                    t.op("scalar", (lambda n, pap, gs: lambda EE: EE.activation(
                        out=gs[:, n * 512:(n + 1) * 512], in_=pap, func=AF.Silu))(n, pap, gs),
                        reads=[ph], writes=[gs[:, n * 512:(n + 1) * 512]])
                t.op("vector", (lambda gs, s1s: lambda EE: EE.tensor_tensor(
                    gs, gs, s1s.bitcast(BF16), A.mult))(gs, s1s),
                    reads=[gs, s1s], writes=[gs])
                t.op("vector", (lambda ys, x1s, gs: lambda EE: EE.tensor_tensor(
                    ys, x1s, gs, A.add))(ys, x1s, gs),
                    reads=[x1s, gs], writes=[ys])
                t.op("sync", (lambda i, ys: lambda EE: EE.dma_start(
                    out=self.y[i * 128:(i + 1) * 128, :], in_=ys))(i, ys),
                    reads=[ys], writes=[self.y[i * 128:(i + 1) * 128, :]], dma=True)


def make_consts():
    import ml_dtypes
    ii = np.eye(128, dtype=np.float32)
    io = np.arange(128, dtype=np.float32)
    return {
        "identf": ii,
        "identb": ii.astype(ml_dtypes.bfloat16),
        "ident2b": (2 * ii).astype(ml_dtypes.bfloat16),
        "strictU": (io[:, None] < io[None, :]).astype(np.float32),
        "trilxb": (io[:, None] > io[None, :]).astype(ml_dtypes.bfloat16),
        "triuib": (io[:, None] <= io[None, :]).astype(ml_dtypes.bfloat16),
        "iotap": io[:, None].copy(),
        "iotae": np.broadcast_to(np.arange(E, dtype=np.float32)[None, :], (128, E)).copy(),
        "iotag": np.broadcast_to(np.arange(8, dtype=np.float32)[None, :], (128, 8)).copy(),
        "onesc": np.ones((128, 1), np.float32),
        "actc": np.broadcast_to(np.array([1e30, -2.5e29, -7.5e29, -0.5, 0.5, 0, 0, 0], np.float32)[None, :], (128, 8)).copy(),
        "onescb": np.ones((128, 1), ml_dtypes.bfloat16),
        "ones1": np.ones((1, 128), np.float32),
    }


def stage_inputs(inputs):
    x = np.ascontiguousarray(np.asarray(inputs["L0_attn_x"], np.float32).reshape(T, D))
    w1, w2, w3 = inputs["L0_ffn_w1"], inputs["L0_ffn_w2"], inputs["L0_ffn_w3"]
    consts = make_consts()
    maps = []
    pidx = np.arange(128, dtype=np.uint32)
    for c in range(NCORE):
        x_tm = np.ascontiguousarray(x[c * NT:(c + 1) * NT])
        myh = (0, 1) if c % 2 == 0 else (2, 3)
        oth = (2, 3) if c % 2 == 0 else (0, 1)
        headperm = np.array([*myh, *oth])
        colperm = np.concatenate([np.arange(h * DK, (h + 1) * DK) for h in headperm])
        myhalf = c % 2
        qkvidx = np.zeros((128, 96), np.uint32)
        betaidx = np.zeros((128, 32), np.uint32)
        oidx = np.zeros((128, 32), np.uint32)
        for ch in range(NCH):
            half = ch // 16
            for tt in range(3):
                blk = tt * 2 + (0 if half == myhalf else 1)
                qkvidx[:, ch * 3 + tt] = (ch * 128 + pidx) * 6 + blk
            betaidx[:, ch] = (ch * 128 + pidx) * 2 + (0 if half == myhalf else 1)
        for hf in range(2):
            for ch in range(TT):
                oidx[:, hf * 16 + ch] = hf * L + myhalf * NT + ch * 128 + pidx
        m = {
            "x_tm": x_tm,
            "w0": np.asarray(inputs["L0_attn_W"], np.float32),
            "gwt": np.ascontiguousarray(np.asarray(inputs["L0_ffn_gate_weight"], np.float32).T),
            "ebb": np.ascontiguousarray(np.broadcast_to(
                np.asarray(inputs["L0_ffn_e_bias"], np.float32)[None, :], (128, E))),
            "w1t": np.ascontiguousarray(np.asarray(w1, np.float32)[c * EPC:(c + 1) * EPC].transpose(0, 2, 1)),
            "w3t": np.ascontiguousarray(np.asarray(w3, np.float32)[c * EPC:(c + 1) * EPC].transpose(0, 2, 1)),
            "w2t": np.ascontiguousarray(np.asarray(w2, np.float32)[c * EPC:(c + 1) * EPC].transpose(0, 2, 1)),
            "sgt": np.ascontiguousarray(np.asarray(inputs["L0_ffn_shared_gate"], np.float32).T),
            "sut": np.ascontiguousarray(np.asarray(inputs["L0_ffn_shared_up"], np.float32).T),
            "sdt": np.ascontiguousarray(np.asarray(inputs["L0_ffn_shared_down"], np.float32).T),
            "wq": np.ascontiguousarray(np.asarray(inputs["L1_attn_Wq"], np.float32)[:, colperm]),
            "wk": np.ascontiguousarray(np.asarray(inputs["L1_attn_Wk"], np.float32)[:, colperm]),
            "wv": np.ascontiguousarray(np.asarray(inputs["L1_attn_Wv"], np.float32)[:, colperm]),
            "wbeta": np.ascontiguousarray(np.asarray(inputs["L1_attn_Wbeta"], np.float32)[:, headperm]),
            "wo": np.asarray(inputs["L1_attn_Wo"], np.float32),
            "gnb": np.ascontiguousarray(np.broadcast_to(
                np.tile(np.asarray(inputs["L1_attn_g_norm"], np.float32), NH)[None, :], (128, D))),
            "wg_nat": np.ascontiguousarray(np.asarray(inputs["L1_ffn_w_gate"], np.float32)[:, c * 128:(c + 1) * 128]),
            "wu_nat": np.ascontiguousarray(np.asarray(inputs["L1_ffn_w_up"], np.float32)[:, c * 128:(c + 1) * 128]),
            "wd_nat": np.ascontiguousarray(np.asarray(inputs["L1_ffn_w_down"], np.float32)[:, c * (FI // 8):(c + 1) * (FI // 8)]),
            "prevm": (np.arange(NCORE) < c).astype(np.float32)[:, None],
            "qkvidx": qkvidx,
            "betaidx": betaidx,
            "oidx": oidx,
            "tokoff": np.full((128, 1), c * NT, np.float32),
        }
        m.update(consts)
        maps.append(m)
    return maps


_KB_CACHE = {}


def get_kb(taps=()):
    key = tuple(sorted(dict(taps).items()))
    if key not in _KB_CACHE:
        _KB_CACHE[key] = KB(dict(taps))
    return _KB_CACHE[key]


def kernel(**inputs):
    maps = stage_inputs(inputs)
    kb = get_kb()
    res = run_bass_kernel_spmd(kb.nc, maps, core_ids=list(range(NCORE)))
    out = np.concatenate([res.results[c]["y"] for c in range(NCORE)], axis=0)
    return out.reshape(B, L, D)


# version 25
# speedup vs baseline: 1.0586x; 1.0586x over previous
import numpy as np
import concourse.bass as bass
import concourse.mybir as mybir
from concourse.bass_utils import run_bass_kernel_spmd
from concourse.bass_core_rust import IndirectOffsetOnAxis

F32 = mybir.dt.float32
BF16 = mybir.dt.bfloat16
I32 = mybir.dt.int32
I16 = mybir.dt.int16
U32 = mybir.dt.uint32
A = mybir.AluOpType
AF = mybir.ActivationFunctionType
AX = mybir.AxisListType

NCORE = 8
B, L, D = 4, 4096, 1024
T = B * L
NT = T // NCORE
TT = NT // 128
DC = D // 128
E, KTOP, CAP = 64, 8, 1024
EPC = E // NCORE
FME = 512
FI = 2816
NH, DK = 4, 256
NCH = L // 128
LN_EPS, ATT_EPS, RMS_EPS = 1e-5, 1e-6, 1e-5
NEWTON = 6
TRASH = E * CAP

ENGINES = ("tensor", "vector", "scalar", "gpsimd", "sync")
DMA_ENGINES = ("sync", "scalar", "gpsimd", "vector", "tensor")


class Trk:
    def __init__(self, nc):
        self.nc = nc
        self._ctx = []
        self.sem = {}
        self.cnt = {}
        for e in ENGINES:
            self.sem[e] = self._enter(nc.semaphore(f"s_{e}"))
            self.cnt[e] = 0
        self.DMA_POOL = 12
        self.dma_slot = {}
        for e in DMA_ENGINES:
            for i in range(self.DMA_POOL):
                self.sem[f"dma_{e}{i}"] = self._enter(nc.semaphore(f"d_{e}{i}"))
                self.cnt[f"dma_{e}{i}"] = 0
            self.dma_slot[e] = 0
        self.water = {e: {} for e in ENGINES}
        self.writer = {}
        self.readers = {}
        self.prog = {e: [] for e in ENGINES}
        self.n_inst = 0
        import os
        self.no_self_wait = os.environ.get("KB_NOSELFWAIT", "0") == "1"

    def sbuf(self, name, shape, dtype):
        return self._enter(self.nc.sbuf_tensor(name, shape, dtype))

    def psum(self, name, shape, dtype=F32):
        return self._enter(self.nc.psum_tensor(name, shape, dtype))

    def _enter(self, cm):
        v = cm.__enter__()
        self._ctx.append(cm)
        return v

    @staticmethod
    def _span(x):
        if not hasattr(x, "tensor"):
            try:
                es = mybir.dt.size(x.dtype)
                import math as _m
                tot = es
                for d in x.shape[1:]:
                    tot *= d
                return (x.name, 0, max(tot, es * int(np.prod(x.shape))))
            except Exception:
                return (x.name, 0, 1 << 60)
        tensor = x.tensor
        name = tensor.name
        try:
            es = mybir.dt.size(x.dtype)
            rowlen = 1
            for d in tensor.shape[1:]:
                rowlen *= d
            from concourse.bass_primitives import MemorySpace as _MS
            flat = x.space != _MS.SBUF and x.space != _MS.PSUM
            dims = list(x.ap)
            if flat:
                lo = x.offset
                ext = 0
                for st, ct in dims:
                    ext += abs(int(st)) * (int(ct) - 1)
                return (name, lo * es, (lo + ext + 1) * es)
            lo = x.offset % rowlen if rowlen else 0
            ext = 0
            for st, ct in dims[1:]:
                ext += abs(int(st)) * (int(ct) - 1)
            return (name, lo * es, (lo + ext + 1) * es)
        except Exception:
            return (name, 0, 1 << 60)

    def _wait(self, eng, dep):
        sname, val = dep
        if val <= 0 or self.water[eng].get(sname, 0) >= val:
            return
        if self.no_self_wait and sname == eng:
            self.water[eng][sname] = val
            return
        self.water[eng][sname] = val
        sem = self.sem[sname]
        self.prog[eng].append(lambda EE, sem=sem, val=val: EE.wait_ge(sem, val))

    def op(self, eng, fn, reads=(), writes=(), dma=False):
        rsp = [self._span(x) for x in reads]
        wsp = [self._span(x) for x in writes]
        for name, lo, hi in rsp:
            for wlo, whi, dep in self.writer.get(name, ()):
                if wlo < hi and lo < whi:
                    self._wait(eng, dep)
        for name, lo, hi in wsp:
            for wlo, whi, dep in self.writer.get(name, ()):
                if wlo < hi and lo < whi:
                    self._wait(eng, dep)
            for rlo, rhi, dep in self.readers.get(name, ()):
                if rlo < hi and lo < rhi:
                    self._wait(eng, dep)
        if dma:
            i = self.dma_slot[eng]
            self.dma_slot[eng] = (i + 1) % self.DMA_POOL
            sname = f"dma_{eng}{i}"
            self._wait(eng, (sname, self.cnt[sname]))
            self.cnt[sname] += 16
            val = self.cnt[sname]
            inc = 16
        else:
            sname = eng
            self.cnt[sname] += 1
            val = self.cnt[sname]
            inc = 1
        sem = self.sem[sname]

        def run(EE, fn=fn, sem=sem, inc=inc):
            fn(EE).then_inc(sem, inc)

        self.prog[eng].append(run)
        dep = (sname, val)
        for name, lo, hi in rsp:
            self.readers.setdefault(name, []).append((lo, hi, dep))
        for name, lo, hi in wsp:
            def upd(entries):
                out = []
                for elo, ehi, edep in entries:
                    if elo >= hi or ehi <= lo:
                        out.append((elo, ehi, edep))
                        continue
                    if elo < lo:
                        out.append((elo, lo, edep))
                    if ehi > hi:
                        out.append((hi, ehi, edep))
                return out
            self.writer[name] = upd(self.writer.get(name, []))
            self.writer[name].append((lo, hi, dep))
            self.readers[name] = upd(self.readers.get(name, []))
        self.n_inst += 1
        return dep

    def wait_all_dma(self, eng):
        def w(EE):
            for i in range(self.DMA_POOL):
                sn = f"dma_{eng}{i}"
                if self.cnt[sn] > 0:
                    EE.wait_ge(self.sem[sn], self.cnt[sn])

        self.prog[eng].append(w)

    def finalize(self, block):
        for e in ENGINES:
            seq = self.prog[e]
            if not seq:
                continue

            def make(seq):
                def body(EE):
                    for f in seq:
                        f(EE)

                return body

            getattr(block, e)(make(seq))

    def close(self):
        for cm in reversed(self._ctx):
            cm.__exit__(None, None, None)


class KB:
    def __init__(self, debug_taps=()):
        self.nc = bass.Bass("TRN2", target_bir_lowering=False, debug=False)
        self.debug_taps = dict(debug_taps)
        self.taps = {}
        self.declare_io()
        self.t = Trk(self.nc)
        self.alloc()
        with self.nc.Block() as block:
            self.emit()
            for e in DMA_ENGINES:
                self.t.wait_all_dma(e)
            self.t.finalize(block)
        self.t.close()

    def declare_io(self):
        nc = self.nc

        def din(name, shape, dtype=F32):
            ap = nc.dram_tensor(name, shape, dtype, kind="ExternalInput").ap()
            setattr(self, name, ap)

        din("x_tm", [NT, D])
        din("w0", [D, 3 * D])
        din("gwt", [D, E])
        din("ebb", [128, E])
        din("w1t", [EPC, D, FME])
        din("w3t", [EPC, D, FME])
        din("w2t", [EPC, FME, D])
        din("sgt", [D, D])
        din("sut", [D, D])
        din("sdt", [D, D])
        din("wq", [D, D])
        din("wk", [D, D])
        din("wv", [D, D])
        din("wbeta", [D, NH])
        din("wo", [D, D])
        din("gnb", [128, D])
        din("wg_nat", [FI, 128])
        din("wu_nat", [FI, 128])
        din("wd_nat", [D, FI // 8])
        din("identf", [128, 128])
        din("identb", [128, 128], BF16)
        din("ident2b", [128, 128], BF16)
        din("strictU", [128, 128])
        din("trilxb", [128, 128], BF16)
        din("triuib", [128, 128], BF16)
        din("iotap", [128, 1])
        din("iotae", [128, E])
        din("iotag", [128, 8])
        din("onesc", [128, 1])
        din("onescb", [128, 1], BF16)
        din("ones1", [1, 128])
        din("prevm", [NCORE, 1])
        din("tokoff", [128, 1])
        din("actc", [128, 8])
        din("qkvidx", [128, 96], U32)
        din("betaidx", [128, 32], U32)
        din("oidx", [128, 32], U32)

        self.y = nc.dram_tensor("y", [NT, D], F32, kind="ExternalOutput").ap()

        def dint(name, shape, dtype=F32, space="Local"):
            ap = nc.dram_tensor(name, shape, dtype, kind="Internal", addr_space=space).ap()
            setattr(self, name, ap)

        dint("agx_in", [NT, D], BF16)
        dint("agx_out", [T, D], BF16, "Shared")
        dint("kv_in", [D + 1, D])
        dint("kv_out", [D + 1, D])
        dint("cnt_in", [1, E])
        dint("cnt_out", [NCORE, E])
        dint("tab_in", [E * CAP + 128, 2])
        dint("tab_out", [E * CAP + 128, 2], F32, "Shared")
        dint("ye_in", [EPC * CAP, D], BF16)
        dint("ye_out", [E * CAP + 128, D], BF16, "Shared")
        dint("x1_dram", [NT, D])
        dint("s_dram", [NT, D], BF16)
        dint("s1_dram", [NT, D], BF16)
        dint("xn1f_dram", [NT, D])
        dint("xn1b_dram", [NT, D], BF16)
        dint("qkvb_in", [NT, 3 * D], BF16)
        dint("qkvb_out", [2 * NT, 3 * D], BF16)
        dint("bb_in", [NT, 16], BF16)
        dint("bb_out", [2 * NT, 16], BF16)
        dint("o_in", [L, 2 * DK], BF16)
        dint("o_out", [2 * L, 2 * DK], BF16)
        dint("wgq_nat", [FI, 128], BF16)
        dint("wuq_nat", [FI, 128], BF16)
        dint("wdq_a", [D, 128], BF16)
        dint("wdq_b", [D, 128], BF16)
        dint("wdq_c", [D, 128], BF16)
        dint("wgu_in", [128, 2 * FI], BF16)
        dint("wgu_out", [D, 2 * FI], BF16, "Shared")
        dint("wd_in", [FI // 8, D], BF16)
        dint("wd_out", [FI, D], BF16, "Shared")
        dint("xq_dram", [NT, D], BF16)
        dint("xe_nat", [CAP, D], BF16)
        dint("xe_nat2", [CAP, D], BF16)
        dint("qkv_my", [L, 3 * 512], BF16)
        dint("beta_my", [L, 8], BF16)
        dint("dsc", [64 * 128, 1280], BF16)
        dint("o_my", [NT, D], BF16)
        dint("hq_dram", [NT, FI], BF16)

        for name, (shape, dtype) in self.debug_taps.items():
            self.taps[name] = nc.dram_tensor("tap_" + name, shape, dtype, kind="ExternalOutput").ap()

    def tap(self, name, src_ap, reads):
        if name in self.taps:
            dst = self.taps[name]
            self.t.op("sync", lambda EE: EE.dma_start(out=dst, in_=src_ap),
                      reads=reads, writes=[dst], dma=True)

    def tap_tiles(self, name, tiled_ap, reads, rows=128):
        if name in self.taps:
            dst = self.taps[name]
            ntile = dst.shape[0] // rows
            for i in range(ntile):
                self.t.op("sync", (lambda i: lambda EE: EE.dma_start(
                    out=dst[i * rows:(i + 1) * rows, :], in_=tiled_ap[:, i, :]))(i),
                    reads=reads, writes=[dst], dma=True)

    def alloc(self):
        t = self.t
        self.ps = [t.psum(f"ps{i}", [128, 512]) for i in range(6)]
        self.pw = t.psum("pw", [128, 1024])
        for nm in ("identf", "identb", "ident2b", "strictU", "trilxb", "triuib",
                   "iotap", "iotae", "iotag", "onesc", "onescb", "ones1", "prevm", "ebb",
                   "tokoff", "actc", "gnb"):
            ap = getattr(self, nm)
            setattr(self, "c_" + nm, t.sbuf("c_" + nm, list(ap.shape), ap.dtype))
        self.arA = t.sbuf("arA", [128, 16384], BF16)
        self.arB = t.sbuf("arB", [128, 16384], BF16)
        self.arC = t.sbuf("arC", [128, 16384], BF16)
        self.arD = t.sbuf("arD", [128, 16384], BF16)
        self.arW = t.sbuf("arW", [128, 8192], BF16)
        self.arE = t.sbuf("arE", [128, 8192], BF16)
        self.arF = t.sbuf("arF", [128, 4096], F32)
        self.arG = t.sbuf("arG", [128, 2048], F32)
        self.mmean = t.sbuf("mmean", [128, TT], F32)
        self.rstd = t.sbuf("rstd", [128, TT], F32)
        self.sm1 = t.sbuf("sm1", [128, TT], F32)
        self.sm2 = t.sbuf("sm2", [128, TT], F32)
        self.zcol = t.sbuf("zcol", [128, TT], F32)
        self.small = t.sbuf("small", [128, 2048], F32)
        self.smi = t.sbuf("smi", [128, 128], U32)
        self.smh = t.sbuf("smh", [128, 64], I16)

    def rsqrt_newton(self, v, out, eps, vt, ot, n):
        t = self.t
        tmp = self.sm2 if ot is not self.sm2 and vt is not self.sm2 else self.sm1
        tt = tmp
        t.op("vector", lambda EE: EE.tensor_scalar(v, v, eps, None, A.add), reads=[vt], writes=[vt])
        t.op("vector", lambda EE: EE.reciprocal(out, v), reads=[vt], writes=[ot])
        t.op("scalar", lambda EE: EE.activation(out=out, in_=out, func=AF.Sqrt), reads=[ot], writes=[ot])
        t.op("vector", lambda EE: EE.tensor_tensor(tt[:, 0:n], out, out, A.mult), reads=[ot], writes=[tt])
        t.op("vector", lambda EE: EE.tensor_tensor(tt[:, 0:n], tt[:, 0:n], v, A.mult), reads=[tt, vt], writes=[tt])
        t.op("vector", lambda EE: EE.tensor_scalar(tt[:, 0:n], tt[:, 0:n], -0.5, 1.5, A.mult, A.add), reads=[tt], writes=[tt])
        t.op("vector", lambda EE: EE.tensor_tensor(out, out, tt[:, 0:n], A.mult), reads=[tt, ot], writes=[ot])

    PHASES = ("consts", "ln0", "l0_attn", "routing", "moe", "l0_epilogue",
              "ln1_proj", "delta", "s1", "l1_ffn")

    def emit(self):
        import os
        upto = int(os.environ.get("KB_PHASES", "99"))
        self.emit_consts()
        self.emit_ln0()
        if upto >= 2:
            self.emit_l0_attn()
        if upto >= 3:
            self.emit_routing()
        if upto >= 4:
            self.emit_moe()
        if upto >= 5:
            self.emit_l0_epilogue()
        if upto >= 6:
            self.emit_ln1_proj()
        if upto >= 7:
            self.emit_wquant()
            self.emit_delta()
        if upto >= 8:
            self.emit_s1_gu()
        self.emit_l1_down()

    def emit_consts(self):
        t = self.t
        for nm in ("identf", "identb", "ident2b", "strictU", "trilxb", "triuib",
                   "iotap", "iotae", "iotag", "onesc", "onescb", "ones1", "prevm", "ebb",
                   "tokoff", "actc", "gnb"):
            ap = getattr(self, nm)
            sb = getattr(self, "c_" + nm)
            t.op("sync", (lambda sb, ap: lambda EE: EE.dma_start(out=sb[:], in_=ap))(sb, ap),
                 reads=[ap], writes=[sb], dma=True)

    def emit_ln0(self):
        t = self.t
        xtile = self.arF
        msum, sqs = self.sm1, self.sm2
        scr = self.arG
        xview = xtile.ap().rearrange("p (k d) -> p k d", d=1024)
        for i in range(TT):
            xa = xview[:, i % 4, :]
            t.op("sync", (lambda i, xa: lambda EE: EE.dma_start(
                out=xa, in_=self.x_tm[i * 128:(i + 1) * 128, :]))(i, xa),
                reads=[self.x_tm], writes=[self.arF], dma=True)
            t.op("scalar", (lambda i, xa: lambda EE: EE.activation(
                out=scr[:, 0:1024], in_=xa, func=AF.Identity, accum_out=msum[:, i:i + 1]))(i, xa),
                reads=[self.arF], writes=[self.arG, msum])
            t.op("scalar", (lambda i, xa: lambda EE: EE.activation(
                out=scr[:, 1024:2048], in_=xa, func=AF.Square, accum_out=sqs[:, i:i + 1]))(i, xa),
                reads=[self.arF], writes=[self.arG, sqs])
        mmean, rstd = self.mmean, self.rstd
        t.op("vector", lambda EE: EE.tensor_scalar(mmean[:], msum[:], 1.0 / D, None, A.mult),
             reads=[msum], writes=[mmean])
        t.op("vector", lambda EE: EE.tensor_tensor(self.zcol[:], mmean[:], mmean[:], A.mult),
             reads=[mmean], writes=[self.zcol])
        t.op("vector", lambda EE: EE.scalar_tensor_tensor(sqs[:], sqs[:], 1.0 / D, self.zcol[:], A.mult, A.subtract),
             reads=[sqs, self.zcol], writes=[sqs])
        self.rsqrt_newton(sqs[:], rstd[:], LN_EPS, sqs, rstd, TT)
        xn_tm = self.arD.ap().rearrange("p (i d) -> p i d", d=1024)
        self.xn_tm = xn_tm
        for i in range(TT):
            xa = xview[:, i % 4, :]
            t.op("sync", (lambda i, xa: lambda EE: EE.dma_start(
                out=xa, in_=self.x_tm[i * 128:(i + 1) * 128, :]))(i, xa),
                reads=[self.x_tm], writes=[self.arF], dma=True)
            t.op("vector", (lambda i, xa: lambda EE: EE.tensor_scalar(
                xn_tm[:, i, :], xa, mmean[:, i:i + 1], rstd[:, i:i + 1], A.subtract, A.mult))(i, xa),
                reads=[self.arF, mmean, rstd], writes=[self.arD])
            t.op("gpsimd", (lambda i: lambda EE: EE.dma_start(
                out=self.agx_in[i * 128:(i + 1) * 128, :], in_=xn_tm[:, i, :]))(i),
                reads=[self.arD], writes=[self.agx_in], dma=True)
        t.op("gpsimd", lambda EE: EE.collective_compute(
            "AllGather", A.bypass, replica_groups=[list(range(NCORE))],
            ins=[self.agx_in.opt()], outs=[self.agx_out.opt()]),
            reads=[self.agx_in], writes=[self.agx_out])
        xn_fm = self.arB.ap().rearrange("p (c n) -> p c n", n=NT)
        self.xn_fm = xn_fm
        for c in range(DC):
            t.op("scalar", (lambda c: lambda EE: EE.dma_start_transpose(
                xn_fm[:, c, :], self.agx_in[:, c * 128:(c + 1) * 128]))(c),
                reads=[self.agx_in], writes=[self.arB], dma=True)
        self.tap_tiles("xn_tm", xn_tm, [self.arD])

    def emit_l0_attn(self):
        t = self.t
        xn_fm = self.xn_fm
        qp_fm = self.arA.ap().rearrange("p (c n) -> p c n", n=NT)
        kp_tm = self.arC.ap().rearrange("p (i d) -> p i d", d=1024)
        v_tm = self.arD.ap().rearrange("p (i d) -> p i d", d=1024)
        wbuf = self.arW.ap().rearrange("p (c n) -> p c n", n=1024)
        ps = self.ps
        scr = self.arG
        for c in range(DC):
            t.op("gpsimd", (lambda c: lambda EE: EE.dma_start(
                out=wbuf[:, c, :], in_=self.w0[c * 128:(c + 1) * 128, 0:1024]))(c),
                reads=[self.w0], writes=[self.arW], dma=True)
        for qc in range(DC):
            for n in range(4):
                p = ps[n % 2]
                sl = slice(n * 512, (n + 1) * 512)
                sa = scr[:, (n % 2) * 512:(n % 2) * 512 + 512]
                sb = scr[:, 1024 + (n % 2) * 512:1024 + (n % 2) * 512 + 512]
                for c in range(DC):
                    t.op("tensor", (lambda qc, c, sl, p: lambda EE: EE.matmul(
                        p[:], wbuf[:, c, qc * 128:(qc + 1) * 128], xn_fm[:, c, sl],
                        start=(c == 0), stop=(c == DC - 1)))(qc, c, sl, p),
                        reads=[wbuf[:, c, qc * 128:(qc + 1) * 128], xn_fm[:, c, sl]], writes=[p])
                t.op("vector", (lambda p, sa: lambda EE: EE.tensor_scalar(
                    sa, p[:], 0.0, None, A.min))(p, sa), reads=[p], writes=[sa])
                t.op("scalar", (lambda sa, sb: lambda EE: EE.activation(out=sb, in_=sa, func=AF.Exp))(sa, sb),
                     reads=[sa], writes=[sb])
                t.op("vector", (lambda qc, sl, p, sb: lambda EE: EE.scalar_tensor_tensor(
                    qp_fm[:, qc, sl], p[:], 0.0, sb, A.max, A.add))(qc, sl, p, sb),
                    reads=[p, sb], writes=[qp_fm[:, qc, sl]])
        for c in range(DC):
            t.op("gpsimd", (lambda c: lambda EE: EE.dma_start(
                out=wbuf[:, c, :], in_=self.w0[c * 128:(c + 1) * 128, 1024:2048]))(c),
                reads=[self.w0, self.arW], writes=[self.arW], dma=True)
        for i in range(TT):
            for n in range(2):
                p = ps[2 + n]
                sl = slice(n * 512, (n + 1) * 512)
                sa = scr[:, n * 512:n * 512 + 512]
                sb = scr[:, 1024 + n * 512:1024 + n * 512 + 512]
                for c in range(DC):
                    t.op("tensor", (lambda i, c, sl, p: lambda EE: EE.matmul(
                        p[:], xn_fm[:, c, i * 128:(i + 1) * 128], wbuf[:, c, sl],
                        start=(c == 0), stop=(c == DC - 1)))(i, c, sl, p),
                        reads=[xn_fm[:, c, i * 128:(i + 1) * 128], wbuf[:, c, sl]], writes=[p])
                t.op("vector", (lambda p, sa: lambda EE: EE.tensor_scalar(
                    sa, p[:], 0.0, None, A.min))(p, sa), reads=[p], writes=[sa])
                t.op("scalar", (lambda sa, sb: lambda EE: EE.activation(out=sb, in_=sa, func=AF.Exp))(sa, sb),
                     reads=[sa], writes=[sb])
                t.op("vector", (lambda i, n, p, sb: lambda EE: EE.scalar_tensor_tensor(
                    kp_tm[:, i, n * 512:(n + 1) * 512], p[:], 0.0, sb, A.max, A.add))(i, n, p, sb),
                    reads=[p, sb], writes=[kp_tm[:, i, n * 512:(n + 1) * 512]])
        for c in range(DC):
            t.op("gpsimd", (lambda c: lambda EE: EE.dma_start(
                out=wbuf[:, c, :], in_=self.w0[c * 128:(c + 1) * 128, 2048:3072]))(c),
                reads=[self.w0, self.arW], writes=[self.arW], dma=True)
        for i in range(TT):
            for n in range(2):
                p = ps[2 + n]
                sl = slice(n * 512, (n + 1) * 512)
                for c in range(DC):
                    t.op("tensor", (lambda i, c, sl, p: lambda EE: EE.matmul(
                        p[:], xn_fm[:, c, i * 128:(i + 1) * 128], wbuf[:, c, sl],
                        start=(c == 0), stop=(c == DC - 1)))(i, c, sl, p),
                        reads=[xn_fm[:, c, i * 128:(i + 1) * 128], wbuf[:, c, sl]], writes=[p])
                t.op("vector", (lambda i, n, p: lambda EE: EE.tensor_copy(
                    v_tm[:, i, n * 512:(n + 1) * 512], p[:]))(i, n, p),
                    reads=[p], writes=[v_tm[:, i, n * 512:(n + 1) * 512]])
        pw = self.pw
        p5 = ps[5]
        for kc in range(DC):
            for nh in range(2):
                sl = slice(nh * 512, (nh + 1) * 512)
                for i in range(TT):
                    t.op("tensor", (lambda kc, i, sl: lambda EE: EE.matmul(
                        pw[:, sl], kp_tm[:, i, kc * 128:(kc + 1) * 128], v_tm[:, i, sl],
                        start=(i == 0), stop=(i == TT - 1)))(kc, i, sl),
                        reads=[self.arC, self.arD], writes=[pw])
            for i in range(TT):
                t.op("tensor", (lambda kc, i: lambda EE: EE.matmul(
                    p5[:, 0:1], kp_tm[:, i, kc * 128:(kc + 1) * 128], self.c_onescb[:],
                    start=(i == 0), stop=(i == TT - 1)))(kc, i),
                    reads=[kp_tm[:, i, kc * 128:(kc + 1) * 128], self.c_onescb], writes=[p5])
            sa = scr[:, (kc % 2) * 1024:(kc % 2) * 1024 + 1024]
            sb = scr[:, 2046 + (kc % 2):2047 + (kc % 2)]
            t.op("vector", (lambda kc, sa: lambda EE: EE.tensor_copy(sa, pw[:]))(kc, sa),
                 reads=[pw], writes=[sa])
            t.op("vector", (lambda kc, sb: lambda EE: EE.tensor_copy(sb, p5[:, 0:1]))(kc, sb),
                 reads=[p5], writes=[sb])
            t.op("sync", (lambda kc, sa: lambda EE: EE.dma_start(
                out=self.kv_in[kc * 128:(kc + 1) * 128, :], in_=sa))(kc, sa),
                reads=[sa], writes=[self.kv_in[kc * 128:(kc + 1) * 128, :]], dma=True)
            t.op("sync", (lambda kc, sb: lambda EE: EE.dma_start(
                out=self.kv_in[D:D + 1, kc * 128:(kc + 1) * 128].rearrange("one p -> p one"),
                in_=sb))(kc, sb),
                reads=[sb], writes=[self.kv_in[D:D + 1, kc * 128:(kc + 1) * 128]], dma=True)
        t.op("gpsimd", lambda EE: EE.collective_compute(
            "AllReduce", A.add, replica_groups=[[0, 1], [2, 3], [4, 5], [6, 7]],
            ins=[self.kv_in.opt()], outs=[self.kv_out.opt()]),
            reads=[self.kv_in], writes=[self.kv_out])
        self.tap("kvout", self.kv_out, [self.kv_out])
        self.tap("kvin", self.kv_in, [self.kv_in])
        kvb = self.arC.ap().rearrange("p (c n) -> p c n", n=1024)
        ksc = self.sm1
        for kc in range(DC):
            t.op("gpsimd", (lambda kc: lambda EE: EE.dma_start(
                out=kvb[:, kc, :], in_=self.kv_out[kc * 128:(kc + 1) * 128, :]))(kc),
                reads=[self.kv_out, self.arC], writes=[self.arC], dma=True)
            t.op("gpsimd", (lambda kc: lambda EE: EE.dma_start(
                out=ksc[:, kc:kc + 1],
                in_=self.kv_out[D:D + 1, kc * 128:(kc + 1) * 128].rearrange("one p -> p one")))(kc),
                reads=[self.kv_out], writes=[ksc], dma=True)
        ksb = self.smh.ap().bitcast(BF16)
        t.op("vector", lambda EE: EE.tensor_copy(ksb[:, 0:DC], ksc[:, 0:DC]),
             reads=[ksc], writes=[self.smh])
        zrow = self.arG
        for n in range(4):
            for c in range(DC):
                t.op("tensor", (lambda n, c: lambda EE: EE.matmul(
                    p5[0:1, 0:512], ksb[:, c:c + 1], qp_fm[:, c, n * 512:(n + 1) * 512],
                    start=(c == 0), stop=(c == DC - 1)))(n, c),
                    reads=[self.smh, self.arA], writes=[p5])
            t.op("vector", (lambda n: lambda EE: EE.tensor_copy(zrow[0:1, n * 512:(n + 1) * 512], p5[0:1, 0:512]))(n),
                 reads=[p5], writes=[self.arG])
        zcol = self.zcol
        for i in range(TT):
            t.op("tensor", (lambda i: lambda EE: EE.transpose(
                self.ps[4][:, 0:1], zrow[0:1, i * 128:(i + 1) * 128], self.c_identf[0:1, 0:1]))(i),
                reads=[self.arG, self.c_identf], writes=[self.ps[4]])
            t.op("vector", (lambda i: lambda EE: EE.tensor_copy(zcol[:, i:i + 1], self.ps[4][:, 0:1]))(i),
                 reads=[self.ps[4]], writes=[zcol])
        t.op("vector", lambda EE: EE.tensor_scalar(zcol[:], zcol[:], ATT_EPS, None, A.add),
             reads=[zcol], writes=[zcol])
        t.op("vector", lambda EE: EE.reciprocal(zcol[:], zcol[:]), reads=[zcol], writes=[zcol])
        sstage = self.arW.ap().bitcast(BF16)
        for i in range(TT):
            sst = sstage[:, 6144 + (i % 2) * 1024:6144 + (i % 2) * 1024 + 1024]
            for n in range(2):
                p = ps[n]
                sl = slice(n * 512, (n + 1) * 512)
                for c in range(DC):
                    t.op("tensor", (lambda i, c, sl, p: lambda EE: EE.matmul(
                        p[:], qp_fm[:, c, i * 128:(i + 1) * 128], kvb[:, c, sl],
                        start=(c == 0), stop=(c == DC - 1)))(i, c, sl, p),
                        reads=[qp_fm[:, c, i * 128:(i + 1) * 128], kvb[:, c, sl]], writes=[p])
                t.op("vector", (lambda i, sl, p, sst: lambda EE: EE.tensor_scalar(
                    sst[:, sl], p[:], zcol[:, i:i + 1], None, A.mult))(i, sl, p, sst),
                    reads=[p, zcol], writes=[sst[:, sl]])
            t.op("sync", (lambda i, sst: lambda EE: EE.dma_start(
                out=self.s_dram[i * 128:(i + 1) * 128, :], in_=sst))(i, sst),
                reads=[sst], writes=[self.s_dram[i * 128:(i + 1) * 128, :]], dma=True)
        self.tap("s0", self.s_dram, [self.s_dram])

    def emit_routing(self):
        t = self.t
        aF = self.arA.ap().bitcast(F32)
        scores = aF.rearrange("p (i e) -> p i e", e=E)
        sc0 = 0
        ss0 = 1024
        pf0 = 2048
        gwsb = self.arF
        for c in range(DC):
            t.op("sync", (lambda c: lambda EE: EE.dma_start(
                out=gwsb[:, c * 64:(c + 1) * 64], in_=self.gwt[c * 128:(c + 1) * 128, :]))(c),
                reads=[self.gwt], writes=[self.arF], dma=True)
        p5 = self.ps[5]
        for c in range(DC):
            t.op("tensor", (lambda c: lambda EE: EE.matmul(
                p5[0:1, 0:64], self.c_onesc[:], gwsb[:, c * 64:(c + 1) * 64],
                start=(c == 0), stop=(c == DC - 1)))(c),
                reads=[self.c_onesc, self.arF], writes=[p5])
        gwcol = self.arG
        t.op("vector", lambda EE: EE.tensor_copy(gwcol[0:1, 0:E], p5[0:1, 0:64]), reads=[p5], writes=[self.arG])
        t.op("tensor", lambda EE: EE.matmul(p5[:, 0:64], self.c_ones1[:], gwcol[0:1, 0:E], start=True, stop=True),
             reads=[self.c_ones1, self.arG], writes=[p5])
        gwcolb = self.arG
        t.op("vector", lambda EE: EE.tensor_copy(gwcolb[:, 64:64 + E], p5[:, 0:64]), reads=[p5], writes=[self.arG])
        mr = self.sm2
        t.op("vector", lambda EE: EE.tensor_tensor(mr[:], self.mmean[:], self.rstd[:], A.mult),
             reads=[self.mmean, self.rstd], writes=[mr])
        xfm0 = 1024
        p0 = self.ps[0]
        small = self.small
        for i in range(TT):
            t.op("sync", (lambda i: lambda EE: EE.dma_start(
                out=self.arF[:, 1024:2048], in_=self.x_tm[i * 128:(i + 1) * 128, :]))(i),
                reads=[self.x_tm], writes=[self.arF], dma=True)
            for c in range(DC):
                t.op("tensor", (lambda c: lambda EE: EE.transpose(
                    self.ps[4][:, 0:128], self.arF[:, 1024 + c * 128:1024 + (c + 1) * 128], self.c_identf[:]))(c),
                    reads=[self.arF, self.c_identf], writes=[self.ps[4]])
                t.op("vector", (lambda c: lambda EE: EE.tensor_copy(
                    self.arG[:, xfm0 + c * 128:xfm0 + (c + 1) * 128], self.ps[4][:, 0:128]))(c),
                    reads=[self.ps[4]], writes=[self.arG])
            for c in range(DC):
                t.op("tensor", (lambda i, c: lambda EE: EE.matmul(
                    p0[:, 0:64], self.arG[:, xfm0 + c * 128:xfm0 + (c + 1) * 128],
                    gwsb[:, c * 64:(c + 1) * 64],
                    start=(c == 0), stop=(c == DC - 1)))(i, c),
                    reads=[self.arG, self.arF], writes=[p0])
            t.op("vector", (lambda i: lambda EE: EE.tensor_scalar(
                small[:, 1024:1024 + E], gwcolb[:, 64:64 + E], mr[:, i:i + 1], None, A.mult))(i),
                reads=[self.arG, mr], writes=[small])
            t.op("vector", (lambda i: lambda EE: EE.scalar_tensor_tensor(
                small[:, 1088:1088 + E], p0[:, 0:64], self.rstd[:, i:i + 1], small[:, 1024:1024 + E],
                A.mult, A.subtract))(i),
                reads=[p0, self.rstd, small], writes=[small])
            t.op("scalar", (lambda i: lambda EE: EE.activation(
                out=aF[:, sc0 + i * E:sc0 + (i + 1) * E], in_=small[:, 1088:1088 + E], func=AF.Sigmoid))(i),
                reads=[small], writes=[self.arA])
            sfc = small[:, 1152:1152 + E]
            t.op("vector", (lambda i, sfc: lambda EE: EE.tensor_tensor(
                sfc, aF[:, sc0 + i * E:sc0 + (i + 1) * E], self.c_ebb[:], A.add))(i, sfc),
                reads=[self.arA, self.c_ebb], writes=[small])
            g3 = sfc.rearrange("p (g e) -> p g e", e=8)
            l1m = small[:, 1216:1248].rearrange("p (g e) -> p g e", e=4)
            l1s = small[:, 1248:1280].rearrange("p (g e) -> p g e", e=4)
            t.op("vector", (lambda g3, l1m: lambda EE: EE.tensor_tensor(l1m, g3[:, :, 0::2], g3[:, :, 1::2], A.max))(g3, l1m),
                 reads=[small], writes=[small])
            t.op("vector", (lambda g3, l1s: lambda EE: EE.tensor_tensor(l1s, g3[:, :, 0::2], g3[:, :, 1::2], A.add))(g3, l1s),
                 reads=[small], writes=[small])
            l2m = small[:, 1280:1296].rearrange("p (g e) -> p g e", e=2)
            l2s = small[:, 1296:1312].rearrange("p (g e) -> p g e", e=2)
            l2c = small[:, 1312:1328].rearrange("p (g e) -> p g e", e=2)
            t.op("vector", (lambda l1m, l2m: lambda EE: EE.tensor_tensor(l2m, l1m[:, :, 0::2], l1m[:, :, 1::2], A.max))(l1m, l2m),
                 reads=[small], writes=[small])
            t.op("vector", (lambda l1m, l2c: lambda EE: EE.tensor_tensor(l2c, l1m[:, :, 0::2], l1m[:, :, 1::2], A.add))(l1m, l2c),
                 reads=[small], writes=[small])
            t.op("vector", (lambda l1s, l2s: lambda EE: EE.tensor_tensor(l2s, l1s[:, :, 0::2], l1s[:, :, 1::2], A.max))(l1s, l2s),
                 reads=[small], writes=[small])
            t.op("vector", (lambda l2s, l2c: lambda EE: EE.tensor_tensor(l2s, l2s, l2c, A.max))(l2s, l2c),
                 reads=[small], writes=[small])
            gsum = small[:, 1328:1336]
            t.op("vector", (lambda l2m, gsum: lambda EE: EE.tensor_tensor(
                gsum, l2m[:, :, 0], l2m[:, :, 1], A.add))(l2m, gsum),
                reads=[small], writes=[small])
            t.op("vector", (lambda l2s, gsum: lambda EE: EE.tensor_tensor(
                gsum, gsum, l2s[:, :, 0], A.max))(l2s, gsum),
                reads=[small], writes=[small])
            t.op("vector", (lambda l2s, gsum: lambda EE: EE.tensor_tensor(
                gsum, gsum, l2s[:, :, 1], A.max))(l2s, gsum),
                reads=[small], writes=[small])
            gm8 = small[:, 1336:1344]
            gi8 = self.smi[:, 0:8]
            t.op("vector", (lambda gsum, gm8: lambda EE: EE.max(gm8, gsum))(gsum, gm8),
                 reads=[small], writes=[small])
            t.op("vector", (lambda gm8, gsum, gi8: lambda EE: EE.max_index(gi8, gm8, gsum))(gm8, gsum, gi8),
                 reads=[small], writes=[self.smi])
            gif = small[:, 1344:1352]
            t.op("vector", (lambda gi8, gif: lambda EE: EE.tensor_copy(gif, gi8))(gi8, gif),
                 reads=[self.smi], writes=[small])
            gmask = small[:, 1352:1360]
            for j in range(4):
                if j == 0:
                    t.op("vector", (lambda gif, gmask: lambda EE: EE.tensor_tensor(
                        gmask, self.c_iotag[:], gif[:, 0:1].broadcast_to([128, 8]), A.is_equal))(gif, gmask),
                        reads=[small, self.c_iotag], writes=[small])
                else:
                    t.op("vector", (lambda gif, gmask, j: lambda EE: EE.scalar_tensor_tensor(
                        gmask, self.c_iotag[:], 0.0, gif[:, j:j + 1].broadcast_to([128, 8]), A.add, A.is_equal,
                        accum_out=None))(gif, gmask, j),
                        reads=[small, self.c_iotag], writes=[small]) if False else None
                    oh1 = small[:, 1360:1368]
                    t.op("vector", (lambda gif, oh1, j: lambda EE: EE.tensor_tensor(
                        oh1, self.c_iotag[:], gif[:, j:j + 1].broadcast_to([128, 8]), A.is_equal))(gif, oh1, j),
                        reads=[small, self.c_iotag], writes=[small])
                    t.op("vector", (lambda gmask, oh1: lambda EE: EE.tensor_tensor(
                        gmask, gmask, oh1, A.add))(gmask, oh1),
                        reads=[small], writes=[small])
            tmp = small[:, 1368:1368 + E]
            t.op("vector", (lambda sfc, gmask, tmp: lambda EE: EE.tensor_tensor(
                tmp.rearrange("p (g e) -> p g e", e=8), sfc.rearrange("p (g e) -> p g e", e=8),
                gmask.rearrange("p (g o) -> p g o", o=1).broadcast_to([128, 8, 8]), A.mult))(sfc, gmask, tmp),
                reads=[small], writes=[small])
            t8 = small[:, 1432:1440]
            ti8 = self.smi[:, 8:16]
            t.op("vector", (lambda tmp, t8: lambda EE: EE.max(t8, tmp))(tmp, t8),
                 reads=[small], writes=[small])
            t.op("vector", (lambda t8, tmp, ti8: lambda EE: EE.max_index(ti8, t8, tmp))(t8, tmp, ti8),
                 reads=[small], writes=[self.smi])
            tidxf = small[:, 0:128].rearrange("p (i j) -> p i j", j=8)[:, i, :]
            t.op("vector", (lambda ti8, tidxf: lambda EE: EE.tensor_copy(tidxf, ti8))(ti8, tidxf),
                 reads=[self.smi], writes=[small])
            ssel = aF[:, ss0 + i * E:ss0 + (i + 1) * E]
            wsum = small[:, 640:648]
            topw = small[:, 128:256].rearrange("p (i j) -> p i j", j=8)[:, i, :]
            oh = small[:, 1440:1440 + E]
            scsl = aF[:, sc0 + i * E:sc0 + (i + 1) * E]
            for j in range(KTOP):
                t.op("vector", (lambda tidxf, oh, j: lambda EE: EE.tensor_tensor(
                    oh, self.c_iotae[:], tidxf[:, j:j + 1].broadcast_to([128, E]), A.is_equal))(tidxf, oh, j),
                    reads=[small, self.c_iotae], writes=[small])
                t.op("vector", (lambda oh, scsl, topw, j: lambda EE: EE.scalar_tensor_tensor(
                    small[:, 1504:1504 + E], scsl, 0.0, oh, A.add, A.mult,
                    accum_out=topw[:, j:j + 1]))(oh, scsl, topw, j),
                    reads=[small, self.arA], writes=[small])
                if j == 0:
                    t.op("vector", (lambda oh, ssel: lambda EE: EE.tensor_copy(ssel, oh))(oh, ssel),
                         reads=[small], writes=[self.arA])
                else:
                    t.op("vector", (lambda oh, ssel: lambda EE: EE.tensor_tensor(ssel, ssel, oh, A.add))(oh, ssel),
                         reads=[small, self.arA], writes=[self.arA])
            t.op("vector", (lambda topw, wsum: lambda EE: EE.tensor_reduce(
                wsum[:, 0:1], topw, AX.X, A.add))(topw, wsum),
                reads=[small], writes=[small])
            t.op("vector", (lambda wsum: lambda EE: EE.tensor_scalar(
                wsum[:, 0:1], wsum[:, 0:1], 1e-20, None, A.add))(wsum),
                reads=[small], writes=[small])
            t.op("vector", (lambda wsum: lambda EE: EE.reciprocal(wsum[:, 1:2], wsum[:, 0:1]))(wsum),
                 reads=[small], writes=[small])
            t.op("vector", (lambda wsum: lambda EE: EE.tensor_scalar(
                wsum[:, 1:2], wsum[:, 1:2], 2.5, None, A.mult))(wsum),
                reads=[small], writes=[small])
            t.op("vector", (lambda topw, wsum: lambda EE: EE.tensor_scalar(
                topw, topw, wsum[:, 1:2], None, A.mult))(topw, wsum),
                reads=[small], writes=[small])
        for i in range(TT):
            t.op("tensor", (lambda i: lambda EE: EE.matmul(
                p5[0:1, 0:64], self.c_onesc[:], aF[:, ss0 + i * E:ss0 + (i + 1) * E],
                start=(i == 0), stop=(i == TT - 1)))(i),
                reads=[self.c_onesc, self.arA], writes=[p5])
        t.op("vector", lambda EE: EE.tensor_copy(self.arG[0:1, 128:128 + E], p5[0:1, 0:64]),
             reads=[p5], writes=[self.arG])
        t.op("sync", lambda EE: EE.dma_start(out=self.cnt_in, in_=self.arG[0:1, 128:128 + E]),
             reads=[self.arG], writes=[self.cnt_in], dma=True)
        t.op("gpsimd", lambda EE: EE.collective_compute(
            "AllGather", A.bypass, replica_groups=[list(range(NCORE))],
            ins=[self.cnt_in.opt()], outs=[self.cnt_out.opt()]),
            reads=[self.cnt_in], writes=[self.cnt_out])
        cnts = self.arG
        t.op("sync", lambda EE: EE.dma_start(out=cnts[0:NCORE, 256:256 + E], in_=self.cnt_out),
             reads=[self.cnt_out], writes=[self.arG], dma=True)
        t.op("tensor", lambda EE: EE.matmul(p5[0:1, 0:64], self.c_prevm[:], cnts[0:NCORE, 256:256 + E],
                                            start=True, stop=True),
             reads=[self.c_prevm, self.arG], writes=[p5])
        t.op("vector", lambda EE: EE.tensor_copy(self.arG[0:1, 384:384 + E], p5[0:1, 0:64]),
             reads=[p5], writes=[self.arG])
        carry = self.arG[0:1, 384:384 + E]
        small = self.small
        for i in range(TT):
            t.op("tensor", (lambda i: lambda EE: EE.matmul(
                p0[:, 0:64], self.c_strictU[:], aF[:, ss0 + i * E:ss0 + (i + 1) * E],
                start=True, stop=False))(i),
                reads=[self.c_strictU, self.arA], writes=[p0])
            t.op("tensor", (lambda i: lambda EE: EE.matmul(
                p0[:, 0:64], self.c_ones1[:], carry, start=False, stop=True))(i),
                reads=[self.c_ones1, self.arG], writes=[p0])
            t.op("vector", (lambda i: lambda EE: EE.tensor_copy(
                aF[:, pf0 + i * E:pf0 + (i + 1) * E], p0[:, 0:64]))(i),
                reads=[p0], writes=[self.arA])
            t.op("tensor", (lambda i: lambda EE: EE.matmul(
                p5[0:1, 0:64], self.c_onesc[:], aF[:, ss0 + i * E:ss0 + (i + 1) * E],
                start=True, stop=True))(i),
                reads=[self.c_onesc, self.arA], writes=[p5])
            t.op("vector", (lambda i: lambda EE: EE.tensor_tensor(
                carry, carry, p5[0:1, 0:64], A.add))(i),
                reads=[p5, self.arG], writes=[self.arG])
        zer2 = self.arG[:, 512:514]
        t.op("vector", lambda EE: EE.memset(zer2, 0.0), writes=[self.arG])
        t.op("sync", lambda EE: EE.dma_start(
            out=self.tab_in.rearrange("(p k) two -> p (k two)", p=128), in_=self.arG[:, 512:514].broadcast_to([128, 1026]) if False else self.arG[:, 512:514]),
            reads=[self.arG], writes=[self.tab_in], dma=True) if False else None
        t.op("vector", lambda EE: EE.memset(self.arF[:, 0:1026], 0.0), writes=[self.arF])
        t.op("sync", lambda EE: EE.dma_start(
            out=self.tab_in.rearrange("(p k) two -> p (k two)", p=128), in_=self.arF[:, 0:1026]),
            reads=[self.arF], writes=[self.tab_in], dma=True)
        tidxa = small[:, 0:128].rearrange("p (i j) -> p i j", j=8)
        topwa = small[:, 128:256].rearrange("p (i j) -> p i j", j=8)
        posg = small[:, 256:384].rearrange("p (i j) -> p i j", j=8)
        kept = small[:, 384:512].rearrange("p (i j) -> p i j", j=8)
        scat = small[:, 512:640].rearrange("p (i j) -> p i j", j=8)
        trash = self.arG[:, 520:521]
        t.op("vector", lambda EE: EE.tensor_scalar(trash, self.c_iotap[:], float(TRASH), None, A.add),
             reads=[self.c_iotap], writes=[self.arG])
        oh = small[:, 1440:1440 + E]
        ex = small[:, 1504:1504 + E]
        rec = self.arG[:, 521:523]
        sidx = self.smi[:, 16:17]
        for i in range(TT):
            for j in range(KTOP):
                t.op("vector", (lambda i, j: lambda EE: EE.tensor_tensor(
                    oh, self.c_iotae[:], tidxa[:, i, j:j + 1].broadcast_to([128, E]), A.is_equal))(i, j),
                    reads=[small, self.c_iotae], writes=[small])
                t.op("vector", (lambda i, j: lambda EE: EE.scalar_tensor_tensor(
                    ex, aF[:, pf0 + i * E:pf0 + (i + 1) * E], 0.0, oh, A.add, A.mult,
                    accum_out=posg[:, i, j:j + 1]))(i, j),
                    reads=[small, self.arA], writes=[small])
                t.op("vector", (lambda i, j: lambda EE: EE.tensor_scalar(
                    kept[:, i, j:j + 1], posg[:, i, j:j + 1], float(CAP), None, A.is_lt))(i, j),
                    reads=[small], writes=[small])
                t.op("vector", (lambda i, j: lambda EE: EE.scalar_tensor_tensor(
                    scat[:, i, j:j + 1], tidxa[:, i, j:j + 1], float(CAP), posg[:, i, j:j + 1],
                    A.mult, A.add))(i, j),
                    reads=[small], writes=[small])
                t.op("vector", (lambda i, j: lambda EE: EE.scalar_tensor_tensor(
                    scat[:, i, j:j + 1], scat[:, i, j:j + 1], 1.0, trash, A.mult, A.subtract))(i, j),
                    reads=[small, self.arG], writes=[small])
                t.op("vector", (lambda i, j: lambda EE: EE.tensor_tensor(
                    scat[:, i, j:j + 1], scat[:, i, j:j + 1], kept[:, i, j:j + 1], A.mult))(i, j),
                    reads=[small], writes=[small])
                t.op("vector", (lambda i, j: lambda EE: EE.scalar_tensor_tensor(
                    scat[:, i, j:j + 1], scat[:, i, j:j + 1], 1.0, trash, A.mult, A.add))(i, j),
                    reads=[small, self.arG], writes=[small])
                t.op("vector", (lambda i, j: lambda EE: EE.tensor_scalar(
                    rec[:, 0:1], self.c_tokoff[:], float(i * 128 + 1), None, A.add))(i, j),
                    reads=[self.c_tokoff], writes=[self.arG])
                t.op("vector", (lambda i, j: lambda EE: EE.tensor_tensor(
                    rec[:, 0:1], rec[:, 0:1], self.c_iotap[:], A.add))(i, j),
                    reads=[self.arG, self.c_iotap], writes=[self.arG])
                t.op("vector", (lambda i, j: lambda EE: EE.tensor_copy(
                    rec[:, 1:2], topwa[:, i, j:j + 1]))(i, j),
                    reads=[small], writes=[self.arG])
                t.op("vector", (lambda i, j: lambda EE: EE.tensor_copy(
                    sidx, scat[:, i, j:j + 1]))(i, j),
                    reads=[small], writes=[self.smi])
                t.op("gpsimd", (lambda i, j: lambda EE: EE.indirect_dma_start(
                    self.tab_in, IndirectOffsetOnAxis(sidx, 0), rec, None))(i, j),
                    reads=[self.arG, self.smi, self.tab_in], writes=[self.tab_in], dma=True)
        t.op("gpsimd", lambda EE: EE.collective_compute(
            "AllReduce", A.add, replica_groups=[list(range(NCORE))],
            ins=[self.tab_in.opt()], outs=[self.tab_out.opt()]),
            reads=[self.tab_in], writes=[self.tab_out])
        self.tap("posg", small[:, 256:384], [small])
        self.tap("scat", small[:, 512:640], [small])
        self.tap("topw", small[:, 128:256], [small])
        self.tap("tidx", small[:, 0:128], [small])

    def moe_gen(self, g):
        t = self.t
        ps = self.ps
        ar = self.arA if g == 0 else self.arC
        xe = ar.ap().rearrange("p (c n) -> p c n", n=1024)
        h_fm = xe
        w2b = xe
        war = self.arW if g == 0 else self.arE
        web = war.ap().rearrange("p (c n) -> p c n", n=512)
        xen = self.xe_nat if g == 0 else self.xe_nat2
        arD = self.arD.ap()
        arF = self.arF.ap()
        arG = self.arG.ap()
        pM0, pM1, pW = ps[2 * g], ps[2 * g + 1], ps[4 + g]
        tabf = self.small[:, g * 16:g * 16 + 8]
        idxf = self.small[:, g * 16 + 8:g * 16 + 16]
        wrow = arF[0:1, g * 2048:g * 2048 + 1024]
        wb = arF[:, g * 2048 + 1024:g * 2048 + 2048]
        sg = arG[:, g * 512:g * 512 + 512]
        tmp = arG[:, 1024 + g * 512:1024 + g * 512 + 512]
        yestg = arD[:, 6144 + g * 1024:6144 + g * 1024 + 1024]
        for el in range(g, EPC, 2):
            for c in range(DC):
                t.op("gpsimd", (lambda el, c: lambda EE: EE.dma_start(
                    out=web[:, c, :], in_=self.w1t[el, c * 128:(c + 1) * 128, :]))(el, c),
                    reads=[self.w1t], writes=[web[:, c, :]], dma=True)
                t.op("gpsimd", (lambda el, c: lambda EE: EE.dma_start(
                    out=web[:, 8 + c, :], in_=self.w3t[el, c * 128:(c + 1) * 128, :]))(el, c),
                    reads=[self.w3t], writes=[web[:, 8 + c, :]], dma=True)
                yield
            for fc in range(4):
                t.op("gpsimd", (lambda el, fc: lambda EE: EE.dma_start(
                    out=w2b[:, 12 + fc, :], in_=self.w2t[el, fc * 128:(fc + 1) * 128, :]))(el, fc),
                    reads=[self.w2t], writes=[w2b[:, 12 + fc, :]], dma=True)
            yield
            t.op("sync", (lambda el, tabf: lambda EE: EE.dma_start(
                out=tabf,
                in_=self.tab_out[el * CAP:(el + 1) * CAP, 0:1].rearrange("(k p) one -> p k", p=128),
                allow_slow_non_contiguous=True))(el, tabf),
                reads=[self.tab_out[el * CAP:(el + 1) * CAP, 0:1]], writes=[tabf], dma=True)
            t.op("sync", (lambda el, wrow: lambda EE: EE.dma_start(
                out=wrow,
                in_=self.tab_out[el * CAP:(el + 1) * CAP, 1:2].rearrange("s one -> one s"),
                allow_slow_non_contiguous=True))(el, wrow),
                reads=[self.tab_out[el * CAP:(el + 1) * CAP, 1:2]], writes=[wrow], dma=True)
            yield
            t.op("vector", (lambda tabf, idxf: lambda EE: EE.tensor_scalar(
                idxf, tabf, 1.0, 1.0, A.max, A.subtract))(tabf, idxf),
                reads=[tabf], writes=[idxf])
            t.op("vector", (lambda g, idxf: lambda EE: EE.tensor_copy(
                self.smi[:, g * 8:g * 8 + 8], idxf))(g, idxf),
                reads=[idxf], writes=[self.smi[:, g * 8:g * 8 + 8]])
            yield
            for n in range(2):
                t.op("tensor", (lambda n, wrow, pW: lambda EE: EE.matmul(
                    pW[:], self.c_ones1[:], wrow[:, n * 512:(n + 1) * 512],
                    start=True, stop=True))(n, wrow, pW),
                    reads=[self.c_ones1, wrow[:, n * 512:(n + 1) * 512]], writes=[pW])
                yield
                t.op("vector", (lambda n, wb, pW: lambda EE: EE.tensor_copy(
                    wb[:, n * 512:(n + 1) * 512], pW[:]))(n, wb, pW),
                    reads=[pW], writes=[wb[:, n * 512:(n + 1) * 512]])
                yield
            for st in range(8):
                xg = arD[:, (g * 2 + st % 2) * 1024:(g * 2 + st % 2) * 1024 + 1024].bitcast(BF16)
                t.op("gpsimd", (lambda el, st, g, xg: lambda EE: EE.indirect_dma_start(
                    xg, None, self.agx_out,
                    IndirectOffsetOnAxis(self.smi[:, g * 8 + st:g * 8 + st + 1], 0)))(el, st, g, xg),
                    reads=[self.agx_out, self.smi[:, g * 8 + st:g * 8 + st + 1], xg], writes=[xg], dma=True)
                yield
                t.op("gpsimd", (lambda el, st, xg: lambda EE: EE.dma_start(
                    out=xen[st * 128:(st + 1) * 128, :], in_=xg))(el, st, xg),
                    reads=[xg], writes=[xen[st * 128:(st + 1) * 128, :]], dma=True)
                yield
            for c in range(DC):
                t.op("scalar", (lambda el, c: lambda EE: EE.dma_start_transpose(
                    xe[:, c, :], xen[:, c * 128:(c + 1) * 128]))(el, c),
                    reads=[xen[:, c * 128:(c + 1) * 128]], writes=[xe[:, c, :]], dma=True)
                yield
            for fc in range(4):
                for n in range(2):
                    sl = slice(n * 512, (n + 1) * 512)
                    for c in range(DC):
                        t.op("tensor", (lambda fc, c, sl, pM0: lambda EE: EE.matmul(
                            pM0[:], web[:, c, fc * 128:(fc + 1) * 128], xe[:, c, sl],
                            start=(c == 0), stop=(c == DC - 1)))(fc, c, sl, pM0),
                            reads=[web[:, c, fc * 128:(fc + 1) * 128], xe[:, c, sl]], writes=[pM0])
                    yield
                    for c in range(DC):
                        t.op("tensor", (lambda fc, c, sl, pM1: lambda EE: EE.matmul(
                            pM1[:], web[:, 8 + c, fc * 128:(fc + 1) * 128], xe[:, c, sl],
                            start=(c == 0), stop=(c == DC - 1)))(fc, c, sl, pM1),
                            reads=[web[:, 8 + c, fc * 128:(fc + 1) * 128], xe[:, c, sl]], writes=[pM1])
                    yield
                    t.op("scalar", (lambda sg, pM0: lambda EE: EE.activation(
                        out=sg, in_=pM0[:], func=AF.Silu))(sg, pM0),
                        reads=[pM0], writes=[sg])
                    yield
                    t.op("vector", (lambda tmp, sg, pM1: lambda EE: EE.tensor_tensor(
                        tmp, sg, pM1[:], A.mult))(tmp, sg, pM1),
                        reads=[sg, pM1], writes=[tmp])
                    yield
                    t.op("vector", (lambda fc, sl, tmp, wb: lambda EE: EE.tensor_tensor(
                        h_fm[:, 8 + fc, sl], tmp, wb[:, sl], A.mult))(fc, sl, tmp, wb),
                        reads=[tmp, wb[:, sl]], writes=[h_fm[:, 8 + fc, sl]])
                    yield
            for st in range(8):
                for n in range(2):
                    p = pM0 if n == 0 else pM1
                    sl = slice(n * 512, (n + 1) * 512)
                    for fc in range(4):
                        t.op("tensor", (lambda st, fc, sl, p: lambda EE: EE.matmul(
                            p[:], h_fm[:, 8 + fc, st * 128:(st + 1) * 128], w2b[:, 12 + fc, sl],
                            start=(fc == 0), stop=(fc == 3)))(st, fc, sl, p),
                            reads=[h_fm[:, 8 + fc, st * 128:(st + 1) * 128], w2b[:, 12 + fc, sl]],
                            writes=[p])
                    yield
                    t.op("vector", (lambda n, p, yestg: lambda EE: EE.tensor_copy(
                        yestg[:, n * 512:(n + 1) * 512], p[:]))(n, p, yestg),
                        reads=[p], writes=[yestg[:, n * 512:(n + 1) * 512]])
                    yield
                t.op("sync", (lambda el, st, yestg: lambda EE: EE.dma_start(
                    out=self.ye_in[el * CAP + st * 128:el * CAP + (st + 1) * 128, :],
                    in_=yestg))(el, st, yestg),
                    reads=[yestg],
                    writes=[self.ye_in[el * CAP + st * 128:el * CAP + (st + 1) * 128, :]], dma=True)
                yield

    def emit_moe(self):
        t = self.t
        self._rr([self.moe_gen(0), self.moe_gen(1)])
        t.op("vector", lambda EE: EE.memset(self.arD.ap()[:, 6144:7168], 0.0),
             writes=[self.arD.ap()[:, 6144:7168]])
        t.op("sync", lambda EE: EE.dma_start(
            out=self.ye_out[E * CAP:E * CAP + 128, :], in_=self.arD.ap()[:, 6144:7168].bitcast(BF16)[:, 0:1024]),
            reads=[self.arD.ap()[:, 6144:7168]], writes=[self.ye_out[E * CAP:E * CAP + 128, :]], dma=True)
        t.op("gpsimd", lambda EE: EE.collective_compute(
            "AllGather", A.bypass, replica_groups=[list(range(NCORE))],
            ins=[self.ye_in.opt()], outs=[self.ye_out[0:E * CAP, :].opt()]),
            reads=[self.ye_in], writes=[self.ye_out])

    def emit_l0_epilogue(self):
        t = self.t
        xn_fm = self.xn_fm
        gsh = self.arC.ap().rearrange("p (c n) -> p c n", n=NT)
        ush = self.arD.ap().rearrange("p (c n) -> p c n", n=NT)
        wsg = self.arW.ap().rearrange("p (c n) -> p c n", n=1024)
        ps = self.ps
        for wnm, dst in (("sgt", gsh), ("sut", ush)):
            wap = getattr(self, wnm)
            dstar = self.arC if dst is gsh else self.arD
            for c in range(DC):
                t.op("gpsimd", (lambda c, wap: lambda EE: EE.dma_start(
                    out=wsg[:, c, :], in_=wap[c * 128:(c + 1) * 128, :]))(c, wap),
                    reads=[wap, self.arW], writes=[self.arW], dma=True)
            for oc in range(DC):
                for n in range(4):
                    p = ps[n % 2]
                    sl = slice(n * 512, (n + 1) * 512)
                    for c in range(DC):
                        t.op("tensor", (lambda oc, c, sl, p: lambda EE: EE.matmul(
                            p[:], wsg[:, c, oc * 128:(oc + 1) * 128], xn_fm[:, c, sl],
                            start=(c == 0), stop=(c == DC - 1)))(oc, c, sl, p),
                            reads=[self.arW, self.arB], writes=[p])
                    if dst is gsh:
                        t.op("scalar", (lambda oc, sl, p, dst, dstar: lambda EE: EE.activation(
                            out=dst[:, oc, sl], in_=p[:], func=AF.Silu))(oc, sl, p, dst, dstar),
                            reads=[p], writes=[dstar])
                    else:
                        t.op("vector", (lambda oc, sl, p, dst, dstar: lambda EE: EE.tensor_copy(
                            dst[:, oc, sl], p[:]))(oc, sl, p, dst, dstar),
                            reads=[p], writes=[dstar])
        for c in range(DC):
            t.op("vector", (lambda c: lambda EE: EE.tensor_tensor(
                gsh[:, c, :], gsh[:, c, :], ush[:, c, :], A.mult))(c),
                reads=[self.arC, self.arD], writes=[self.arC])
        wsd = self.arW.ap().rearrange("p (c n) -> p c n", n=1024)
        for c in range(DC):
            t.op("gpsimd", (lambda c: lambda EE: EE.dma_start(
                out=wsd[:, c, :], in_=self.sdt[c * 128:(c + 1) * 128, :]))(c),
                reads=[self.sdt, self.arW], writes=[self.arW], dma=True)
        small = self.small
        scat = small[:, 512:640].rearrange("p (i j) -> p i j", j=8)
        arF = self.arF.ap()
        arG = self.arG.ap()
        arD = self.arD.ap()

        def epi_stream(k):
            acck = arF[:, k * 2048:k * 2048 + 1024]
            xk = arF[:, k * 2048 + 1024:k * 2048 + 2048]
            sk = arD[:, k * 1024:k * 1024 + 1024]
            gk = arG[:, k * 1024:k * 1024 + 1024]
            idxk = self.smi[:, 17 + k:18 + k]
            pa, pb = ps[2 + 2 * k], ps[3 + 2 * k]
            for i in range(k, TT, 2):
                t.op("sync", (lambda i, xk: lambda EE: EE.dma_start(
                    out=xk, in_=self.x_tm[i * 128:(i + 1) * 128, :]))(i, xk),
                    reads=[self.x_tm[i * 128:(i + 1) * 128, :]], writes=[xk], dma=True)
                t.op("sync", (lambda i, sk: lambda EE: EE.dma_start(
                    out=sk, in_=self.s_dram[i * 128:(i + 1) * 128, :]))(i, sk),
                    reads=[self.s_dram[i * 128:(i + 1) * 128, :]], writes=[sk], dma=True)
                yield
                for j in range(KTOP):
                    t.op("vector", (lambda i, j, idxk: lambda EE: EE.tensor_copy(
                        idxk, scat[:, i, j:j + 1]))(i, j, idxk),
                        reads=[small[:, 512 + i * 8 + j:513 + i * 8 + j]], writes=[idxk])
                    t.op("gpsimd", (lambda i, j, acck, idxk: lambda EE: EE.indirect_dma_start(
                        acck, None, self.ye_out,
                        IndirectOffsetOnAxis(idxk, 0),
                        compute_op=(A.bypass if j == 0 else A.add)))(i, j, acck, idxk),
                        reads=[self.ye_out, idxk] + ([acck] if j > 0 else []),
                        writes=[acck], dma=True)
                    yield
                for n in range(2):
                    p = pa if n == 0 else pb
                    sl = slice(n * 512, (n + 1) * 512)
                    for c in range(DC):
                        t.op("tensor", (lambda i, c, sl, p: lambda EE: EE.matmul(
                            p[:], gsh[:, c, i * 128:(i + 1) * 128], wsd[:, c, sl],
                            start=(c == 0), stop=(c == DC - 1)))(i, c, sl, p),
                            reads=[self.arC, self.arW], writes=[p])
                    yield
                    t.op("vector", (lambda n, sl, p, gk, acck: lambda EE: EE.tensor_tensor(
                        gk[:, sl], acck[:, sl], p[:], A.add))(n, sl, p, gk, acck),
                        reads=[acck[:, sl], p], writes=[gk[:, sl]])
                    yield
                t.op("scalar", (lambda gk: lambda EE: EE.activation(
                    out=gk, in_=gk, func=AF.Silu))(gk),
                    reads=[gk], writes=[gk])
                yield
                t.op("vector", (lambda gk, sk: lambda EE: EE.tensor_tensor(
                    gk, gk, sk.bitcast(BF16)[:, 0:1024] if False else sk, A.mult))(gk, sk),
                    reads=[gk, sk], writes=[gk])
                yield
                t.op("vector", (lambda xk, gk: lambda EE: EE.tensor_tensor(
                    xk, xk, gk, A.add))(xk, gk),
                    reads=[xk, gk], writes=[xk])
                yield
                t.op("sync", (lambda i, xk: lambda EE: EE.dma_start(
                    out=self.x1_dram[i * 128:(i + 1) * 128, :], in_=xk))(i, xk),
                    reads=[xk], writes=[self.x1_dram[i * 128:(i + 1) * 128, :]], dma=True)
                yield

        self._rr([epi_stream(0), epi_stream(1)])
        self.tap("x1", self.x1_dram, [self.x1_dram])

    def emit_ln1_proj(self):
        t = self.t
        msum, sqs = self.sm1, self.sm2
        scr = self.arG
        xtile = self.arF
        for i in range(TT):
            t.op("sync", (lambda i: lambda EE: EE.dma_start(
                out=xtile[:, 0:1024], in_=self.x1_dram[i * 128:(i + 1) * 128, :]))(i),
                reads=[self.x1_dram], writes=[self.arF], dma=True)
            t.op("scalar", (lambda i: lambda EE: EE.activation(
                out=scr[:, 0:1024], in_=xtile[:, 0:1024], func=AF.Identity, accum_out=msum[:, i:i + 1]))(i),
                reads=[self.arF], writes=[self.arG, msum])
            t.op("scalar", (lambda i: lambda EE: EE.activation(
                out=scr[:, 1024:2048], in_=xtile[:, 0:1024], func=AF.Square, accum_out=sqs[:, i:i + 1]))(i),
                reads=[self.arF], writes=[self.arG, sqs])
        mmean, rstd = self.mmean, self.rstd
        t.op("vector", lambda EE: EE.tensor_scalar(mmean[:], msum[:], 1.0 / D, None, A.mult),
             reads=[msum], writes=[mmean])
        t.op("vector", lambda EE: EE.tensor_tensor(self.zcol[:], mmean[:], mmean[:], A.mult),
             reads=[mmean], writes=[self.zcol])
        t.op("vector", lambda EE: EE.scalar_tensor_tensor(sqs[:], sqs[:], 1.0 / D, self.zcol[:], A.mult, A.subtract),
             reads=[sqs, self.zcol], writes=[sqs])
        self.rsqrt_newton(sqs[:], rstd[:], LN_EPS, sqs, rstd, TT)
        stg = self.arW.ap()
        for i in range(TT):
            t.op("sync", (lambda i: lambda EE: EE.dma_start(
                out=xtile[:, 0:1024], in_=self.x1_dram[i * 128:(i + 1) * 128, :]))(i),
                reads=[self.x1_dram], writes=[self.arF], dma=True)
            t.op("vector", (lambda i: lambda EE: EE.tensor_scalar(
                xtile[:, 1024:2048], xtile[:, 0:1024], mmean[:, i:i + 1], rstd[:, i:i + 1],
                A.subtract, A.mult))(i),
                reads=[self.arF, mmean, rstd], writes=[self.arF])
            t.op("sync", (lambda i: lambda EE: EE.dma_start(
                out=self.xn1f_dram[i * 128:(i + 1) * 128, :], in_=xtile[:, 1024:2048]))(i),
                reads=[self.arF], writes=[self.xn1f_dram], dma=True)
            t.op("vector", (lambda i: lambda EE: EE.tensor_copy(
                stg[:, 0:1024], xtile[:, 1024:2048]))(i),
                reads=[self.arF], writes=[self.arW])
            t.op("sync", (lambda i: lambda EE: EE.dma_start(
                out=self.xn1b_dram[i * 128:(i + 1) * 128, :], in_=stg[:, 0:1024]))(i),
                reads=[self.arW], writes=[self.xn1b_dram], dma=True)
        xn1_fm = self.arB.ap().rearrange("p (c n) -> p c n", n=NT)
        self.xn1_fm = xn1_fm
        for c in range(DC):
            t.op("scalar", (lambda c: lambda EE: EE.dma_start_transpose(
                xn1_fm[:, c, :], self.xn1b_dram[:, c * 128:(c + 1) * 128]))(c),
                reads=[self.xn1b_dram], writes=[self.arB], dma=True)
        q_tm = self.arC.ap().rearrange("p (i d) -> p i d", d=1024)
        k_tm = self.arD.ap().rearrange("p (i d) -> p i d", d=1024)
        v_tm = self.arA.ap().rearrange("p (i d) -> p i d", d=1024)
        wpb = self.arW.ap().rearrange("p (c n) -> p c n", n=1024)
        ps = self.ps
        beta_tm = self.small
        for wnm, dst, dstar in (("wq", q_tm, self.arC), ("wk", k_tm, self.arD), ("wv", v_tm, self.arA)):
            wap = getattr(self, wnm)
            for c in range(DC):
                t.op("gpsimd", (lambda c, wap: lambda EE: EE.dma_start(
                    out=wpb[:, c, :], in_=wap[c * 128:(c + 1) * 128, :]))(c, wap),
                    reads=[wap, self.arW], writes=[self.arW], dma=True)
            for i in range(TT):
                for n in range(2):
                    p = ps[n]
                    sl = slice(n * 512, (n + 1) * 512)
                    for c in range(DC):
                        t.op("tensor", (lambda i, c, sl, p: lambda EE: EE.matmul(
                            p[:], xn1_fm[:, c, i * 128:(i + 1) * 128], wpb[:, c, sl],
                            start=(c == 0), stop=(c == DC - 1)))(i, c, sl, p),
                            reads=[self.arB, self.arW], writes=[p])
                    t.op("vector", (lambda i, sl, p, dst, dstar: lambda EE: EE.tensor_copy(
                        dst[:, i, sl], p[:]))(i, sl, p, dst, dstar),
                        reads=[p], writes=[dstar])
        wbb = self.smh.ap().bitcast(BF16)
        for c in range(DC):
            t.op("gpsimd", (lambda c: lambda EE: EE.dma_start(
                out=wbb[:, c * NH:(c + 1) * NH], in_=self.wbeta[c * 128:(c + 1) * 128, :]))(c),
                reads=[self.wbeta, self.smh], writes=[self.smh], dma=True)
        p5 = self.ps[5]
        for i in range(TT):
            for c in range(DC):
                t.op("tensor", (lambda i, c: lambda EE: EE.matmul(
                    p5[:, 0:NH], xn1_fm[:, c, i * 128:(i + 1) * 128], wbb[:, c * NH:(c + 1) * NH],
                    start=(c == 0), stop=(c == DC - 1)))(i, c),
                    reads=[self.arB, self.smh], writes=[p5])
            t.op("scalar", (lambda i: lambda EE: EE.activation(
                out=beta_tm[:, 1536 + i * NH:1536 + (i + 1) * NH], in_=p5[:, 0:NH], func=AF.Sigmoid))(i),
                reads=[p5], writes=[self.small])
        for dst, dstar in ((q_tm, self.arC), (k_tm, self.arD)):
            for i in range(TT):
                for h in range(NH):
                    t.op("scalar", (lambda i, h, dst, dstar: lambda EE: EE.activation(
                        out=self.arG[:, 0:DK], in_=dst[:, i, h * DK:(h + 1) * DK], func=AF.Square,
                        accum_out=self.arG[:, 1024 + h:1024 + h + 1]))(i, h, dst, dstar),
                        reads=[dstar], writes=[self.arG])
                v4 = self.arG[:, 1024:1024 + NH]
                o4 = self.arG[:, 1028:1028 + NH]
                t.op("vector", (lambda v4, o4: lambda EE: EE.reciprocal(o4, v4))(v4, o4),
                     reads=[self.arG], writes=[self.arG])
                t.op("scalar", (lambda o4: lambda EE: EE.activation(out=o4, in_=o4, func=AF.Sqrt))(o4),
                     reads=[self.arG], writes=[self.arG])
                t.op("vector", (lambda v4, o4: lambda EE: EE.tensor_tensor(
                    self.arG[:, 1032:1036], o4, o4, A.mult))(v4, o4),
                    reads=[self.arG], writes=[self.arG])
                t.op("vector", (lambda v4: lambda EE: EE.tensor_tensor(
                    self.arG[:, 1032:1036], self.arG[:, 1032:1036], v4, A.mult))(v4),
                    reads=[self.arG], writes=[self.arG])
                t.op("vector", lambda EE: EE.tensor_scalar(
                    self.arG[:, 1032:1036], self.arG[:, 1032:1036], -0.5, 1.5, A.mult, A.add),
                    reads=[self.arG], writes=[self.arG])
                t.op("vector", (lambda o4: lambda EE: EE.tensor_tensor(
                    o4, o4, self.arG[:, 1032:1036], A.mult))(o4),
                    reads=[self.arG], writes=[self.arG])
                t.op("vector", (lambda i, dst, dstar, o4: lambda EE: EE.tensor_tensor(
                    dst[:, i, :].rearrange("p (h d) -> p h d", d=DK),
                    dst[:, i, :].rearrange("p (h d) -> p h d", d=DK),
                    o4.unsqueeze(2).broadcast_to([128, NH, DK]), A.mult))(i, dst, dstar, o4),
                    reads=[self.arG, dstar], writes=[dstar])
        for i in range(TT):
            t.op("sync", (lambda i: lambda EE: EE.dma_start(
                out=self.qkvb_in[i * 128:(i + 1) * 128, 0:D], in_=q_tm[:, i, :]))(i),
                reads=[self.arC], writes=[self.qkvb_in], dma=True)
            t.op("sync", (lambda i: lambda EE: EE.dma_start(
                out=self.qkvb_in[i * 128:(i + 1) * 128, D:2 * D], in_=k_tm[:, i, :]))(i),
                reads=[self.arD], writes=[self.qkvb_in], dma=True)
            t.op("sync", (lambda i: lambda EE: EE.dma_start(
                out=self.qkvb_in[i * 128:(i + 1) * 128, 2 * D:3 * D], in_=v_tm[:, i, :]))(i),
                reads=[self.arA], writes=[self.qkvb_in], dma=True)
            t.op("vector", (lambda i: lambda EE: EE.tensor_copy(
                self.arW.ap()[:, 7000:7000 + NH],
                beta_tm[:, 1536 + i * NH:1536 + (i + 1) * NH]))(i),
                reads=[self.small], writes=[self.arW])
            t.op("sync", (lambda i: lambda EE: EE.dma_start(
                out=self.bb_in[i * 128:(i + 1) * 128, 0:2],
                in_=self.arW.ap()[:, 7000:7002]))(i),
                reads=[self.arW], writes=[self.bb_in], dma=True)
            t.op("sync", (lambda i: lambda EE: EE.dma_start(
                out=self.bb_in[i * 128:(i + 1) * 128, 8:10],
                in_=self.arW.ap()[:, 7002:7004]))(i),
                reads=[self.arW], writes=[self.bb_in], dma=True)
        t.op("gpsimd", lambda EE: EE.collective_compute(
            "AllGather", A.bypass, replica_groups=[[0, 1], [2, 3], [4, 5], [6, 7]],
            ins=[self.qkvb_in.opt()], outs=[self.qkvb_out.opt()]),
            reads=[self.qkvb_in], writes=[self.qkvb_out])
        t.op("gpsimd", lambda EE: EE.collective_compute(
            "AllGather", A.bypass, replica_groups=[[0, 1], [2, 3], [4, 5], [6, 7]],
            ins=[self.bb_in.opt()], outs=[self.bb_out.opt()]),
            reads=[self.bb_in], writes=[self.bb_out])
        qidx = t.sbuf("qidx", [128, 96], U32)
        bidx = t.sbuf("bidx", [128, 32], U32)
        odx = t.sbuf("odx", [128, 32], U32)
        self.qidx, self.bidx, self.odx = qidx, bidx, odx
        t.op("sync", lambda EE: EE.dma_start(out=qidx[:], in_=self.qkvidx), reads=[self.qkvidx], writes=[qidx], dma=True)
        t.op("sync", lambda EE: EE.dma_start(out=bidx[:], in_=self.betaidx), reads=[self.betaidx], writes=[bidx], dma=True)
        t.op("sync", lambda EE: EE.dma_start(out=odx[:], in_=self.oidx), reads=[self.oidx], writes=[odx], dma=True)
        view6 = self.qkvb_out.rearrange("r (b w) -> (r b) w", w=512)
        viewB = self.bb_out.rearrange("r (b w) -> (r b) w", w=8)
        stgb = self.arW.ap()
        for ch in range(NCH):
            for tt in range(3):
                t.op("gpsimd", (lambda ch, tt: lambda EE: EE.indirect_dma_start(
                    stgb[:, 6144 + tt * 512:6144 + (tt + 1) * 512], None, view6,
                    IndirectOffsetOnAxis(qidx[:, ch * 3 + tt:ch * 3 + tt + 1], 0)))(ch, tt),
                    reads=[self.qkvb_out, qidx, self.arW], writes=[self.arW], dma=True)
            t.op("sync", (lambda ch: lambda EE: EE.dma_start(
                out=self.qkv_my[ch * 128:(ch + 1) * 128, :], in_=stgb[:, 6144:6144 + 1536]))(ch),
                reads=[self.arW], writes=[self.qkv_my], dma=True)
            t.op("gpsimd", (lambda ch: lambda EE: EE.indirect_dma_start(
                stgb[:, 7680:7688], None, viewB,
                IndirectOffsetOnAxis(bidx[:, ch:ch + 1], 0)))(ch),
                reads=[self.bb_out, bidx, self.arW], writes=[self.arW], dma=True)
            t.op("sync", (lambda ch: lambda EE: EE.dma_start(
                out=self.beta_my[ch * 128:(ch + 1) * 128, :], in_=stgb[:, 7680:7688]))(ch),
                reads=[self.arW], writes=[self.beta_my], dma=True)
        self.tap("qkvmy", self.qkv_my, [self.qkv_my])
        self.tap("betamy", self.beta_my, [self.beta_my])

    @staticmethod
    def _rr(slot_gens):
        its = [iter(g) for g in slot_gens]
        alive = [True] * len(its)
        while any(alive):
            for i, it in enumerate(its):
                if alive[i]:
                    try:
                        next(it)
                    except StopIteration:
                        alive[i] = False

    def _delta_chunkA(self, ridx, s):
        t = self.t
        ps = self.ps
        aA = self.arA.ap()
        scr = self.arG
        hh, ch = ridx // NCH, ridx % NCH
        r0 = ch * 128
        if s < 3:
            aA = self.arA.ap()
            b = s * 3072
            pA, pB = ps[2 * s], ps[2 * s + 1]
        else:
            aA = self.arB.ap()
            b = 0
            pA, pB = self.pw.ap()[:, 0:512], self.pw.ap()[:, 512:1024]
        ktm = aA[:, b + 0:b + 256]; vtm = aA[:, b + 256:b + 512]
        Kb = aA[:, b + 512:b + 768]; Vb = aA[:, b + 768:b + 1024]
        qfm = aA[:, b + 1024:b + 1280]; kfm = aA[:, b + 1280:b + 1536]
        Am = aA[:, b + 1536:b + 1664]; ATm = aA[:, b + 1664:b + 1792]
        negMT = aA[:, b + 1792:b + 1920]; Xm = aA[:, b + 1920:b + 2048]
        XTm = aA[:, b + 2048:b + 2176]; Z2 = aA[:, b + 2176:b + 2304]
        WTm = aA[:, b + 2432:b + 2688]
        Um = aA[:, b + 2688:b + 2944]; AqkTm = aA[:, b + 2944:b + 3072]
        bcol = scr[:, 1024 + ch * 8 + hh:1024 + ch * 8 + hh + 1]
        pT = (pB.ap() if callable(getattr(pB, 'ap', None)) else pB).bitcast(BF16)[:, 0:128]
        t.op("sync", (lambda hh, r0, ktm: lambda EE: EE.dma_start(
            out=ktm, in_=self.qkv_my[r0:r0 + 128, 512 + hh * DK:512 + (hh + 1) * DK]))(hh, r0, ktm),
            reads=[self.qkv_my[r0:r0 + 128, 512 + hh * DK:512 + (hh + 1) * DK]], writes=[ktm], dma=True)
        t.op("sync", (lambda hh, r0, vtm: lambda EE: EE.dma_start(
            out=vtm, in_=self.qkv_my[r0:r0 + 128, 1024 + hh * DK:1024 + (hh + 1) * DK]))(hh, r0, vtm),
            reads=[self.qkv_my[r0:r0 + 128, 1024 + hh * DK:1024 + (hh + 1) * DK]], writes=[vtm], dma=True)
        yield
        for c2 in range(2):
            t.op("scalar", (lambda hh, r0, c2, qfm: lambda EE: EE.dma_start_transpose(
                qfm[:, c2 * 128:(c2 + 1) * 128],
                self.qkv_my[r0:r0 + 128, hh * DK + c2 * 128:hh * DK + (c2 + 1) * 128]))(hh, r0, c2, qfm),
                reads=[self.qkv_my[r0:r0 + 128, hh * DK + c2 * 128:hh * DK + (c2 + 1) * 128]],
                writes=[qfm[:, c2 * 128:(c2 + 1) * 128]], dma=True)
            t.op("scalar", (lambda hh, r0, c2, kfm: lambda EE: EE.dma_start_transpose(
                kfm[:, c2 * 128:(c2 + 1) * 128],
                self.qkv_my[r0:r0 + 128, 512 + hh * DK + c2 * 128:512 + hh * DK + (c2 + 1) * 128]))(hh, r0, c2, kfm),
                reads=[self.qkv_my[r0:r0 + 128, 512 + hh * DK + c2 * 128:512 + hh * DK + (c2 + 1) * 128]],
                writes=[kfm[:, c2 * 128:(c2 + 1) * 128]], dma=True)
            yield
        t.op("vector", (lambda bcol, ktm, Kb: lambda EE: EE.tensor_scalar(
            Kb, ktm, bcol, None, A.mult))(bcol, ktm, Kb),
            reads=[ktm, bcol], writes=[Kb])
        t.op("vector", (lambda bcol, vtm, Vb: lambda EE: EE.tensor_scalar(
            Vb, vtm, bcol, None, A.mult))(bcol, vtm, Vb),
            reads=[vtm, bcol], writes=[Vb])
        yield
        for c2 in range(2):
            t.op("tensor", (lambda c2, kfm, pA: lambda EE: EE.matmul(
                pA[:, 0:128], kfm[:, c2 * 128:(c2 + 1) * 128], kfm[:, c2 * 128:(c2 + 1) * 128],
                start=(c2 == 0), stop=(c2 == 1)))(c2, kfm, pA),
                reads=[kfm[:, c2 * 128:(c2 + 1) * 128]], writes=[pA[:, 0:128]])
        yield
        t.op("vector", (lambda bcol, pA, Am: lambda EE: EE.scalar_tensor_tensor(
            Am, pA[:, 0:128], bcol, self.c_trilxb[:], A.mult, A.mult))(bcol, pA, Am),
            reads=[pA[:, 0:128], bcol, self.c_trilxb], writes=[Am])
        yield
        t.op("tensor", (lambda Am, pT: lambda EE: EE.transpose(pT, Am, self.c_identb[:]))(Am, pT),
             reads=[Am, self.c_identb], writes=[pT])
        yield
        t.op("vector", (lambda ATm, pT: lambda EE: EE.tensor_copy(ATm, pT))(ATm, pT),
             reads=[pT], writes=[ATm])
        yield
        t.op("vector", (lambda ATm, negMT: lambda EE: EE.scalar_tensor_tensor(
            negMT, ATm, -1.0, self.c_identb[:], A.mult, A.subtract))(ATm, negMT),
            reads=[ATm, self.c_identb], writes=[negMT])
        t.op("vector", (lambda Am, Xm: lambda EE: EE.tensor_tensor(
            Xm, self.c_identb[:], Am, A.subtract))(Am, Xm),
            reads=[Am, self.c_identb], writes=[Xm])
        t.op("vector", (lambda ATm, XTm: lambda EE: EE.tensor_tensor(
            XTm, self.c_identb[:], ATm, A.subtract))(ATm, XTm),
            reads=[ATm, self.c_identb], writes=[XTm])
        yield
        for it in range(NEWTON):
            last = it == NEWTON - 1
            t.op("tensor", (lambda negMT, Xm, pA: lambda EE: EE.matmul(
                pA[:, 0:128], negMT, Xm, start=True, stop=False))(negMT, Xm, pA),
                reads=[negMT, Xm], writes=[pA[:, 0:128]])
            t.op("tensor", (lambda pA: lambda EE: EE.matmul(
                pA[:, 0:128], self.c_identb[:], self.c_ident2b[:], start=False, stop=True))(pA),
                reads=[self.c_identb, self.c_ident2b], writes=[pA[:, 0:128]])
            yield
            t.op("vector", (lambda Z2, pA: lambda EE: EE.tensor_copy(Z2, pA[:, 0:128]))(Z2, pA),
                 reads=[pA[:, 0:128]], writes=[Z2])
            yield
            t.op("tensor", (lambda XTm, Z2, pA: lambda EE: EE.matmul(
                pA[:, 0:128], XTm, Z2, start=True, stop=True))(XTm, Z2, pA),
                reads=[XTm, Z2], writes=[pA[:, 0:128]])
            if not last:
                t.op("tensor", (lambda Z2, XTm, pB: lambda EE: EE.matmul(
                    pB[:, 0:128], Z2, XTm, start=True, stop=True))(Z2, XTm, pB),
                    reads=[Z2, XTm], writes=[pB[:, 0:128]])
                yield
                t.op("scalar", (lambda XTm, pB: lambda EE: EE.activation(
                    out=XTm, in_=pB[:, 0:128], func=AF.Copy))(XTm, pB),
                    reads=[pB[:, 0:128]], writes=[XTm])
            else:
                yield
            t.op("vector", (lambda Xm, pA: lambda EE: EE.tensor_copy(Xm, pA[:, 0:128]))(Xm, pA),
                 reads=[pA[:, 0:128]], writes=[Xm])
            yield
        t.op("tensor", (lambda Xm, pT: lambda EE: EE.transpose(pT, Xm, self.c_identb[:]))(Xm, pT),
             reads=[Xm, self.c_identb], writes=[pT])
        yield
        t.op("vector", (lambda XTm, pT: lambda EE: EE.tensor_copy(XTm, pT))(XTm, pT),
             reads=[pT], writes=[XTm])
        yield
        for c2 in range(2):
            t.op("tensor", (lambda c2, Kb, XTm, pA: lambda EE: EE.matmul(
                pA[:, 0:128], Kb[:, c2 * 128:(c2 + 1) * 128], XTm,
                start=True, stop=True))(c2, Kb, XTm, pA),
                reads=[Kb[:, c2 * 128:(c2 + 1) * 128], XTm], writes=[pA[:, 0:128]])
            yield
            t.op("scalar", (lambda c2, WTm, pA: lambda EE: EE.activation(
                out=WTm[:, c2 * 128:(c2 + 1) * 128], in_=pA[:, 0:128], func=AF.Copy))(c2, WTm, pA),
                reads=[pA[:, 0:128]], writes=[WTm[:, c2 * 128:(c2 + 1) * 128]])
            yield
        t.op("tensor", (lambda XTm, Vb, pA: lambda EE: EE.matmul(
            pA[:, 0:256], XTm, Vb, start=True, stop=True))(XTm, Vb, pA),
            reads=[XTm, Vb], writes=[pA[:, 0:256]])
        yield
        t.op("vector", (lambda Um, pA: lambda EE: EE.tensor_copy(Um, pA[:, 0:256]))(Um, pA),
             reads=[pA[:, 0:256]], writes=[Um])
        yield
        for c2 in range(2):
            t.op("tensor", (lambda c2, kfm, qfm, pB: lambda EE: EE.matmul(
                pB[:, 0:128], kfm[:, c2 * 128:(c2 + 1) * 128], qfm[:, c2 * 128:(c2 + 1) * 128],
                start=(c2 == 0), stop=(c2 == 1)))(c2, kfm, qfm, pB),
                reads=[kfm[:, c2 * 128:(c2 + 1) * 128], qfm[:, c2 * 128:(c2 + 1) * 128]],
                writes=[pB[:, 0:128]])
        yield
        t.op("vector", (lambda AqkTm, pB: lambda EE: EE.tensor_tensor(
            AqkTm, pB[:, 0:128], self.c_triuib[:], A.mult))(AqkTm, pB),
            reads=[pB[:, 0:128], self.c_triuib], writes=[AqkTm])
        yield
        for dc0, sap, wlen in ((0, WTm, 256), (256, Um, 256), (512, AqkTm, 128),
                               (640, qfm, 256), (896, ktm, 256)):
            t.op("gpsimd", (lambda ridx, dc0, sap, wlen: lambda EE: EE.dma_start(
                out=self.dsc[ridx * 128:(ridx + 1) * 128, dc0:dc0 + wlen],
                in_=sap))(ridx, dc0, sap, wlen),
                reads=[sap], writes=[self.dsc[ridx * 128:(ridx + 1) * 128, dc0:dc0 + wlen]], dma=True)
        yield

    def _delta_chunkB(self, ch, hh):
        t = self.t
        ps = self.ps
        aA = self.arA.ap()
        o_all = self.arC.ap().rearrange("p (h c d) -> p h c d", h=2, d=DK)
        S_f32 = self.arF.ap().rearrange("p (k n) -> p k n", n=DK)
        S_bf = self.arD.ap().rearrange("p (k n) -> p k n", n=DK)
        ridx = hh * NCH + ch
        rb = 9216 + (hh * 2 + ch % 2) * 1408
        rec = aA[:, rb:rb + 1152]
        Uc = aA[:, rb + 1152:rb + 1408]
        t.op("gpsimd", (lambda ridx, rec: lambda EE: EE.dma_start(
            out=rec, in_=self.dsc[ridx * 128:(ridx + 1) * 128, 0:1152]))(ridx, rec),
            reads=[self.dsc[ridx * 128:(ridx + 1) * 128, 0:1152]], writes=[rec], dma=True)
        yield
        WT0 = rec[:, 0:128]
        WT1 = rec[:, 128:256]
        U_ = rec[:, 256:512]
        AqkT = rec[:, 512:640]
        qfm = rec[:, 640:896]
        ktm = rec[:, 896:1152]
        pW, pO = ps[hh * 3], ps[hh * 3 + 1]
        pS0 = ps[2] if hh == 0 else self.pw.ap()[:, 0:256]
        pS1 = ps[5] if hh == 0 else self.pw.ap()[:, 512:768]
        for c2 in range(2):
            t.op("tensor", (lambda hh, c2, WT0, WT1, pW: lambda EE: EE.matmul(
                pW[:, 0:256], (WT0 if c2 == 0 else WT1), S_bf[:, hh * 2 + c2, :],
                start=(c2 == 0), stop=(c2 == 1)))(hh, c2, WT0, WT1, pW),
                reads=[(WT0 if c2 == 0 else WT1), S_bf[:, hh * 2 + c2, :]], writes=[pW[:, 0:256]])
        yield
        t.op("vector", (lambda U_, Uc, pW: lambda EE: EE.tensor_tensor(
            Uc, U_, pW[:, 0:256], A.subtract))(U_, Uc, pW),
            reads=[U_, pW[:, 0:256]], writes=[Uc])
        yield
        for c2 in range(2):
            t.op("tensor", (lambda hh, c2, qfm, pO: lambda EE: EE.matmul(
                pO[:, 0:256], qfm[:, c2 * 128:(c2 + 1) * 128], S_bf[:, hh * 2 + c2, :],
                start=(c2 == 0), stop=False))(hh, c2, qfm, pO),
                reads=[qfm[:, c2 * 128:(c2 + 1) * 128], S_bf[:, hh * 2 + c2, :]], writes=[pO[:, 0:256]])
        t.op("tensor", (lambda AqkT, Uc, pO: lambda EE: EE.matmul(
            pO[:, 0:256], AqkT, Uc, start=False, stop=True))(AqkT, Uc, pO),
            reads=[AqkT, Uc], writes=[pO[:, 0:256]])
        yield
        t.op("vector", (lambda hh, ch, pO: lambda EE: EE.tensor_copy(
            o_all[:, hh, ch, :], pO[:, 0:256]))(hh, ch, pO),
            reads=[pO[:, 0:256]], writes=[o_all[:, hh, ch, :]])
        for c2 in range(2):
            pSc = pS0 if c2 == 0 else pS1
            t.op("tensor", (lambda c2, ktm, Uc, pSc: lambda EE: EE.matmul(
                pSc[:, 0:256], ktm[:, c2 * 128:(c2 + 1) * 128], Uc,
                start=True, stop=True))(c2, ktm, Uc, pSc),
                reads=[ktm[:, c2 * 128:(c2 + 1) * 128], Uc], writes=[pSc[:, 0:256]])
        yield
        for c2 in range(2):
            pSc = pS0 if c2 == 0 else pS1
            t.op("vector", (lambda hh, c2, pSc: lambda EE: EE.tensor_tensor(
                S_f32[:, hh * 2 + c2, :], S_f32[:, hh * 2 + c2, :], pSc[:, 0:256], A.add))(hh, c2, pSc),
                reads=[pSc[:, 0:256], S_f32[:, hh * 2 + c2, :]], writes=[S_f32[:, hh * 2 + c2, :]])
            t.op("vector", (lambda hh, c2: lambda EE: EE.tensor_copy(
                S_bf[:, hh * 2 + c2, :], S_f32[:, hh * 2 + c2, :]))(hh, c2),
                reads=[S_f32[:, hh * 2 + c2, :]], writes=[S_bf[:, hh * 2 + c2, :]])
        yield

    def emit_delta(self):
        t = self.t
        scr = self.arG
        S_f32 = self.arF.ap().rearrange("p (k n) -> p k n", n=DK)
        S_bf = self.arD.ap().rearrange("p (k n) -> p k n", n=DK)
        for ch in range(NCH):
            bb = scr[:, 1280 + ch * 4:1280 + ch * 4 + 4].bitcast(BF16)[:, 0:8]
            bf = scr[:, 1024 + ch * 8:1024 + ch * 8 + 8]
            t.op("gpsimd", (lambda ch, bb: lambda EE: EE.dma_start(
                out=bb, in_=self.beta_my[ch * 128:(ch + 1) * 128, 0:8]))(ch, bb),
                reads=[self.beta_my[ch * 128:(ch + 1) * 128, 0:8]], writes=[bb], dma=True)
            t.op("vector", (lambda bb, bf: lambda EE: EE.tensor_copy(bf, bb))(bb, bf),
                 reads=[bb], writes=[bf])
        NS = 4

        def slotA(s):
            for ridx in range(s, 2 * NCH, NS):
                for _ in self._delta_chunkA(ridx, s):
                    yield

        self._rr([slotA(s) for s in range(NS)] + [self.xq_gen()])
        for hh in range(2):
            t.op("vector", (lambda hh: lambda EE: EE.memset(S_f32[:, hh * 2:(hh + 1) * 2, :], 0.0))(hh),
                 writes=[S_f32[:, hh * 2:(hh + 1) * 2, :]])
            t.op("vector", (lambda hh: lambda EE: EE.memset(S_bf[:, hh * 2:(hh + 1) * 2, :], 0.0))(hh),
                 writes=[S_bf[:, hh * 2:(hh + 1) * 2, :]])

        def headB(hh):
            for ch in range(NCH):
                for _ in self._delta_chunkB(ch, hh):
                    yield

        self._rr([headB(0), headB(1)])

    def s1_gen(self):
        t = self.t
        ps = self.ps
        o_all = self.arC.ap().rearrange("p (h c d) -> p h c d", h=2, d=DK)
        sm = self.small
        for hh in range(2):
            for ch in range(NCH):
                t.op("scalar", (lambda hh, ch: lambda EE: EE.activation(
                    out=sm[:, 0:DK], in_=o_all[:, hh, ch, :], func=AF.Square,
                    accum_out=sm[:, 256 + ch:256 + ch + 1]))(hh, ch),
                    reads=[o_all[:, hh, ch, :]], writes=[sm[:, 0:DK], sm[:, 256 + ch:256 + ch + 1]])
                yield
            v32 = sm[:, 256:256 + NCH]
            o32 = sm[:, 288:288 + NCH]
            tmp = sm[:, 320:320 + NCH]
            t.op("vector", (lambda v32: lambda EE: EE.tensor_scalar(
                v32, v32, 1.0 / DK, None, A.mult))(v32), reads=[v32], writes=[v32])
            t.op("vector", (lambda v32: lambda EE: EE.tensor_scalar(
                v32, v32, RMS_EPS, None, A.add))(v32), reads=[v32], writes=[v32])
            t.op("vector", (lambda v32, o32: lambda EE: EE.reciprocal(o32, v32))(v32, o32),
                 reads=[v32], writes=[o32])
            t.op("scalar", (lambda o32: lambda EE: EE.activation(out=o32, in_=o32, func=AF.Sqrt))(o32),
                 reads=[o32], writes=[o32])
            yield
            t.op("vector", (lambda tmp, o32: lambda EE: EE.tensor_tensor(
                tmp, o32, o32, A.mult))(tmp, o32), reads=[o32], writes=[tmp])
            t.op("vector", (lambda tmp, v32: lambda EE: EE.tensor_tensor(
                tmp, tmp, v32, A.mult))(tmp, v32), reads=[tmp, v32], writes=[tmp])
            t.op("vector", (lambda tmp: lambda EE: EE.tensor_scalar(
                tmp, tmp, -0.5, 1.5, A.mult, A.add))(tmp), reads=[tmp], writes=[tmp])
            t.op("vector", (lambda o32, tmp: lambda EE: EE.tensor_tensor(
                o32, o32, tmp, A.mult))(o32, tmp), reads=[o32, tmp], writes=[o32])
            yield
            t.op("vector", (lambda hh, o32: lambda EE: EE.tensor_tensor(
                o_all[:, hh], o_all[:, hh],
                o32.unsqueeze(2).broadcast_to([128, NCH, DK]), A.mult))(hh, o32),
                reads=[o32, o_all[:, hh]], writes=[o_all[:, hh]])
            t.op("vector", (lambda hh: lambda EE: EE.tensor_tensor(
                o_all[:, hh], o_all[:, hh],
                self.c_gnb[:, 0:DK].rearrange("p (o d) -> p o d", o=1).broadcast_to([128, NCH, DK]),
                A.mult))(hh),
                reads=[self.c_gnb, o_all[:, hh]], writes=[o_all[:, hh]])
            yield
            for ch in range(NCH):
                t.op("sync", (lambda hh, ch: lambda EE: EE.dma_start(
                    out=self.o_in[ch * 128:(ch + 1) * 128, hh * DK:(hh + 1) * DK],
                    in_=o_all[:, hh, ch, :]))(hh, ch),
                    reads=[o_all[:, hh, ch, :]],
                    writes=[self.o_in[ch * 128:(ch + 1) * 128, hh * DK:(hh + 1) * DK]], dma=True)
                if ch % 8 == 7:
                    yield
        t.op("gpsimd", lambda EE: EE.collective_compute(
            "AllGather", A.bypass, replica_groups=[[0, 1], [2, 3], [4, 5], [6, 7]],
            ins=[self.o_in.opt()], outs=[self.o_out.opt()]),
            reads=[self.o_in], writes=[self.o_out])
        yield
        stg = self.arW.ap()
        for hf in range(2):
            for ch in range(TT):
                t.op("gpsimd", (lambda hf, ch: lambda EE: EE.indirect_dma_start(
                    stg[:, 6144:6656], None, self.o_out,
                    IndirectOffsetOnAxis(self.odx[:, hf * 16 + ch:hf * 16 + ch + 1], 0)))(hf, ch),
                    reads=[self.o_out, self.odx, stg[:, 6144:6656]], writes=[stg[:, 6144:6656]], dma=True)
                t.op("sync", (lambda hf, ch: lambda EE: EE.dma_start(
                    out=self.o_my[ch * 128:(ch + 1) * 128, hf * 512:(hf + 1) * 512],
                    in_=stg[:, 6144:6656]))(hf, ch),
                    reads=[stg[:, 6144:6656]],
                    writes=[self.o_my[ch * 128:(ch + 1) * 128, hf * 512:(hf + 1) * 512]], dma=True)
                yield
        o_fm = self.arA.ap().rearrange("p (c n) -> p c n", n=NT)
        for c in range(DC):
            t.op("scalar", (lambda c: lambda EE: EE.dma_start_transpose(
                o_fm[:, c, :], self.o_my[:, c * 128:(c + 1) * 128]))(c),
                reads=[self.o_my], writes=[o_fm[:, c, :]], dma=True)
            yield
        wob = self.arD.ap().rearrange("p (c n) -> p c n", n=1024)
        for c in range(DC):
            t.op("gpsimd", (lambda c: lambda EE: EE.dma_start(
                out=wob[:, c, :], in_=self.wo[c * 128:(c + 1) * 128, :]))(c),
                reads=[self.wo], writes=[wob[:, c, :]], dma=True)
        yield
        sstg = sm[:, 512:1024].bitcast(BF16)
        for i in range(TT):
            for n in range(2):
                p = ps[4 + n]
                sl = slice(n * 512, (n + 1) * 512)
                for c in range(DC):
                    t.op("tensor", (lambda i, c, sl, p: lambda EE: EE.matmul(
                        p[:], o_fm[:, c, i * 128:(i + 1) * 128], wob[:, c, sl],
                        start=(c == 0), stop=(c == DC - 1)))(i, c, sl, p),
                        reads=[o_fm[:, c, i * 128:(i + 1) * 128], wob[:, c, sl]], writes=[p])
                yield
                t.op("vector", (lambda n, p: lambda EE: EE.tensor_copy(
                    sstg[:, n * 512:(n + 1) * 512], p[:]))(n, p),
                    reads=[p], writes=[sstg[:, n * 512:(n + 1) * 512]])
                yield
            t.op("sync", (lambda i: lambda EE: EE.dma_start(
                out=self.s1_dram[i * 128:(i + 1) * 128, :], in_=sstg[:, 0:1024]))(i),
                reads=[sstg[:, 0:1024]], writes=[self.s1_dram[i * 128:(i + 1) * 128, :]], dma=True)
            yield

    def quant_gen(self, xin, xqout, w, qarena, qbase, cs, sb):
        t = self.t
        g = qarena.ap() if hasattr(qarena, "ap") else qarena
        nb = w // 16
        am = self.small[:, sb:sb + nb]
        rs = self.small[:, sb + nb:sb + 2 * nb]
        t.op("vector", (lambda xin, am: lambda EE: EE.tensor_reduce(
            am, xin.rearrange("p (b s) -> p b s", s=16), AX.X, A.max,
            apply_absolute_value=True))(xin, am), reads=[xin], writes=[am])
        t.op("vector", (lambda am: lambda EE: EE.tensor_scalar(
            am, am, 1.0 / 6.0, 1e-12, A.mult, A.max))(am), reads=[am], writes=[am])
        t.op("vector", (lambda am, rs: lambda EE: EE.reciprocal(rs, am))(am, rs),
             reads=[am], writes=[rs])
        yield
        c0 = 0
        while c0 < w:
            cw = min(cs, w - c0)
            cb = cw // 16
            b1 = g[:, qbase:qbase + cw]
            b2 = g[:, qbase + cs:qbase + cs + cw]
            b3 = g[:, qbase + 2 * cs:qbase + 2 * cs + cw]
            b4 = g[:, qbase + 3 * cs:qbase + 3 * cs + cw]
            xs = xin[:, c0:c0 + cw]
            rss = rs[:, c0 // 16:c0 // 16 + cb]
            ams = am[:, c0 // 16:c0 // 16 + cb]
            t.op("vector", (lambda xs, rss, b1, cb, cw: lambda EE: EE.tensor_tensor(
                b1.rearrange("p (b s) -> p b s", s=16), xs.rearrange("p (b s) -> p b s", s=16),
                rss.unsqueeze(2).broadcast_to([128, cb, 16]), A.mult))(xs, rss, b1, cb, cw),
                reads=[xs, rss], writes=[b1])
            t.op("vector", (lambda b1: lambda EE: EE.tensor_scalar(
                b1.bitcast(I32), b1.bitcast(I32), self.cmask1[:, 0:1], None, A.bitwise_and))(b1),
                reads=[b1, self.cmask1], writes=[b1])
            yield
            t.op("vector", (lambda b1, b2: lambda EE: EE.tensor_scalar(
                b2, b1, 1.0, None, A.max))(b1, b2), reads=[b1], writes=[b2])
            t.op("vector", (lambda b2, b3: lambda EE: EE.tensor_scalar(
                b3.bitcast(I32), b2.bitcast(I32), self.cmask2[:, 0:1], None, A.bitwise_and))(b2, b3),
                reads=[b2, self.cmask2], writes=[b3])
            yield
            t.op("vector", (lambda b3: lambda EE: EE.tensor_scalar(
                b3, b3.bitcast(I32), 2097152.0, None, A.is_gt))(b3), reads=[b3], writes=[b3])
            t.op("vector", (lambda b2, b4: lambda EE: EE.tensor_scalar(
                b4.bitcast(I32), b2.bitcast(I32), self.cmask3[:, 0:1], None, A.bitwise_and))(b2, b4),
                reads=[b2, self.cmask3], writes=[b4])
            yield
            t.op("vector", (lambda b3, b4: lambda EE: EE.scalar_tensor_tensor(
                b4, b3, 4194304.0, b4.bitcast(I32), A.mult, A.add))(b3, b4),
                reads=[b3, b4], writes=[b4])
            t.op("vector", (lambda b3, b4: lambda EE: EE.tensor_copy(b3.bitcast(I32), b4))(b3, b4),
                 reads=[b4], writes=[b3])
            yield
            t.op("scalar", (lambda b1, b2: lambda EE: EE.activation(
                out=b2, in_=b1, func=AF.Identity, scale=self.c_actc[:, 0:1], bias=self.c_actc[:, 1:2]))(b1, b2),
                reads=[b1, self.c_actc], writes=[b2])
            t.op("vector", (lambda b2: lambda EE: EE.tensor_scalar(
                b2, b2, 0.5, None, A.subtract))(b2), reads=[b2], writes=[b2])
            yield
            t.op("scalar", (lambda b1, b4: lambda EE: EE.activation(
                out=b4, in_=b1, func=AF.Identity, scale=self.c_actc[:, 0:1], bias=self.c_actc[:, 2:3]))(b1, b4),
                reads=[b1, self.c_actc], writes=[b4])
            yield
            t.op("vector", (lambda b3, b4: lambda EE: EE.tensor_tensor(
                b4, b4, b3, A.min))(b3, b4), reads=[b3, b4], writes=[b4])
            t.op("scalar", (lambda b4: lambda EE: EE.activation(
                out=b4, in_=b4, func=AF.Relu, bias=self.c_actc[:, 3:4]))(b4),
                reads=[b4, self.c_actc], writes=[b4])
            yield
            t.op("vector", (lambda b2, b4: lambda EE: EE.tensor_tensor(
                b4, b2, b4, A.min))(b2, b4), reads=[b2, b4], writes=[b4])
            t.op("scalar", (lambda b4: lambda EE: EE.activation(
                out=b4, in_=b4, func=AF.Relu, bias=self.c_actc[:, 4:5]))(b4),
                reads=[b4, self.c_actc], writes=[b4])
            yield
            t.op("scalar", (lambda xs, b1: lambda EE: EE.activation(
                out=b1, in_=xs, func=AF.Sign))(xs, b1), reads=[xs], writes=[b1])
            t.op("vector", (lambda b1, b4: lambda EE: EE.tensor_tensor(
                b4, b4, b1, A.mult))(b1, b4), reads=[b1, b4], writes=[b4])
            yield
            t.op("vector", (lambda b4, ams, cb, c0, cw: lambda EE: EE.tensor_tensor(
                xqout[:, c0:c0 + cw].rearrange("p (b s) -> p b s", s=16),
                b4.rearrange("p (b s) -> p b s", s=16),
                ams.unsqueeze(2).broadcast_to([128, cb, 16]), A.mult))(b4, ams, cb, c0, cw),
                reads=[b4, ams], writes=[xqout[:, c0:c0 + cw]])
            yield
            c0 += cw

    def emit_wquant(self):
        t = self.t
        self.cmask1 = t.sbuf("cmask1", [128, 1], I32)
        self.cmask2 = t.sbuf("cmask2", [128, 1], I32)
        self.cmask3 = t.sbuf("cmask3", [128, 1], I32)
        t.op("vector", lambda EE: EE.memset(self.cmask1[:], 0x7FFFFFFF), writes=[self.cmask1])
        t.op("vector", lambda EE: EE.memset(self.cmask2[:], 0x3FFFFF), writes=[self.cmask2])
        t.op("vector", lambda EE: EE.memset(self.cmask3[:], -4194304), writes=[self.cmask3])
        jobs = []
        for src_ap, dsts, nrow, ncol in ((self.wg_nat, [(self.wgq_nat, 0, 128)], FI, 128),
                                         (self.wu_nat, [(self.wuq_nat, 0, 128)], FI, 128),
                                         (self.wd_nat, [(self.wdq_a, 0, 128), (self.wdq_b, 128, 128),
                                                        (self.wdq_c, 256, 96)], D, FI // 8)):
            for r0 in range(0, nrow, 128):
                jobs.append((src_ap, r0, ncol, dsts))

        def wq_stream(k):
            arF = self.arF.ap()
            arW = self.arW.ap()
            for j in range(k, len(jobs), 2):
                src_ap, r0, ncol, dsts = jobs[j]
                wtile = arF[:, 2048 + k * 384:2048 + k * 384 + ncol]
                wq = arW[:, 7296 + k * 384:7296 + k * 384 + ncol]
                t.op("scalar", (lambda src_ap, r0, ncol, wtile: lambda EE: EE.dma_start(
                    out=wtile, in_=src_ap[r0:r0 + 128, :]))(src_ap, r0, ncol, wtile),
                    reads=[src_ap[r0:r0 + 128, :]], writes=[wtile], dma=True)
                yield
                if k == 0:
                    qa, qb = self.arG, 0
                else:
                    qa, qb = self.arF, 2816
                for _ in self.quant_gen(wtile, wq, ncol, qa, qb, 256, 1600 + k * 64):
                    yield
                for dst_ap, cb, cn in dsts:
                    t.op("scalar", (lambda dst_ap, r0, cb, cn, wq: lambda EE: EE.dma_start(
                        out=dst_ap[r0:r0 + 128, 0:cn], in_=wq[:, cb:cb + cn]))(dst_ap, r0, cb, cn, wq),
                        reads=[wq[:, cb:cb + cn]], writes=[dst_ap[r0:r0 + 128, 0:cn]], dma=True)
                yield

        self._rr([wq_stream(0), wq_stream(1)])
        t.op("vector", lambda EE: EE.memset(self.arG[:, 0:32], 0.0), writes=[self.arG.ap()[:, 0:32]])
        for r0 in range(0, D, 128):
            t.op("scalar", (lambda r0: lambda EE: EE.dma_start(
                out=self.wdq_c[r0:r0 + 128, 96:128], in_=self.arG.ap()[:, 0:32].bitcast(BF16)[:, 0:32]))(r0),
                reads=[self.arG.ap()[:, 0:32]], writes=[self.wdq_c[r0:r0 + 128, 96:128]], dma=True)
        stg = self.arW.ap()
        t.op("scalar", lambda EE: EE.dma_start_transpose(stg[:, 0:FI], self.wgq_nat),
             reads=[self.wgq_nat], writes=[stg[:, 0:FI]], dma=True)
        t.op("scalar", lambda EE: EE.dma_start(out=self.wgu_in[:, 0:FI], in_=stg[:, 0:FI]),
             reads=[stg[:, 0:FI]], writes=[self.wgu_in], dma=True)
        t.op("scalar", lambda EE: EE.dma_start_transpose(stg[:, 0:FI], self.wuq_nat),
             reads=[self.wgu_in, self.wuq_nat], writes=[stg[:, 0:FI]], dma=True)
        t.op("scalar", lambda EE: EE.dma_start(out=self.wgu_in[:, FI:2 * FI], in_=stg[:, 0:FI]),
             reads=[stg[:, 0:FI]], writes=[self.wgu_in], dma=True)
        for srcw, rr0, nn in ((self.wdq_a, 0, 128), (self.wdq_b, 128, 128), (self.wdq_c, 256, 96)):
            t.op("scalar", (lambda srcw: lambda EE: EE.dma_start_transpose(
                stg[0:128, 0:D], srcw))(srcw),
                reads=[srcw, self.wd_in], writes=[stg[:, 0:D]], dma=True)
            t.op("scalar", (lambda rr0, nn: lambda EE: EE.dma_start(
                out=self.wd_in[rr0:rr0 + nn, :], in_=stg[0:nn, 0:D]))(rr0, nn),
                reads=[stg[:, 0:D]], writes=[self.wd_in[rr0:rr0 + nn, :]], dma=True)
        t.op("gpsimd", lambda EE: EE.collective_compute(
            "AllGather", A.bypass, replica_groups=[list(range(NCORE))],
            ins=[self.wgu_in.opt()], outs=[self.wgu_out.opt()]),
            reads=[self.wgu_in], writes=[self.wgu_out])
        t.op("gpsimd", lambda EE: EE.collective_compute(
            "AllGather", A.bypass, replica_groups=[list(range(NCORE))],
            ins=[self.wd_in.opt()], outs=[self.wd_out.opt()]),
            reads=[self.wd_in], writes=[self.wd_out])

    def xq_gen(self):
        t = self.t
        arF = self.arF.ap()
        arW = self.arW.ap()
        for i in range(TT):
            xtile = arF[:, 2048:3072]
            xq = arW[:, 5120:6144]
            t.op("scalar", (lambda i, xtile: lambda EE: EE.dma_start(
                out=xtile, in_=self.xn1f_dram[i * 128:(i + 1) * 128, :]))(i, xtile),
                reads=[self.xn1f_dram[i * 128:(i + 1) * 128, :]], writes=[xtile], dma=True)
            yield
            for _ in self.quant_gen(xtile, xq, 1024, self.arG, 0, 256, 1728):
                yield
            t.op("scalar", (lambda i, xq: lambda EE: EE.dma_start(
                out=self.xq_dram[i * 128:(i + 1) * 128, :], in_=xq))(i, xq),
                reads=[xq], writes=[self.xq_dram[i * 128:(i + 1) * 128, :]], dma=True)
            yield

    def gu_tiles(self, k, o0, ow):
        t = self.t
        ps = self.ps
        arF = self.arF.ap()
        arW = self.arW.ap()
        xq_fm = self.arB.ap().rearrange("p (c n) -> p c n", n=NT)
        wE = self.arE.ap().rearrange("p (c n) -> p c n", n=512)
        pG, pU = ps[2 * k], ps[2 * k + 1]
        for i in range(k, TT, 2):
            for c in range(DC):
                t.op("tensor", (lambda i, c, ow, pG: lambda EE: EE.matmul(
                    pG[:, 0:ow], xq_fm[:, c, i * 128:(i + 1) * 128], wE[:, c, 0:ow],
                    start=(c == 0), stop=(c == DC - 1)))(i, c, ow, pG),
                    reads=[xq_fm[:, c, i * 128:(i + 1) * 128], wE[:, c, 0:ow]], writes=[pG])
            yield
            for c in range(DC):
                t.op("tensor", (lambda i, c, ow, pU: lambda EE: EE.matmul(
                    pU[:, 0:ow], xq_fm[:, c, i * 128:(i + 1) * 128], wE[:, 8 + c, 0:ow],
                    start=(c == 0), stop=(c == DC - 1)))(i, c, ow, pU),
                    reads=[xq_fm[:, c, i * 128:(i + 1) * 128], wE[:, 8 + c, 0:ow]], writes=[pU])
            yield
            sg = arF[:, 3072 + k * 512:3072 + k * 512 + ow]
            hb = arF[:, 2048 + k * 512:2048 + k * 512 + ow]
            hqs = arW[:, 6656 + k * 512:6656 + k * 512 + ow]
            t.op("scalar", (lambda sg, pG, ow: lambda EE: EE.activation(
                out=sg, in_=pG[:, 0:ow], func=AF.Silu))(sg, pG, ow),
                reads=[pG], writes=[sg])
            yield
            t.op("vector", (lambda hb, sg, pU, ow: lambda EE: EE.tensor_tensor(
                hb, sg, pU[:, 0:ow], A.mult))(hb, sg, pU, ow),
                reads=[sg, pU], writes=[hb])
            yield
            for _ in self.quant_gen(hb, hqs, ow, self.arG, k * 1024, 256, 1856 + k * 64):
                yield
            t.op("sync", (lambda i, o0, ow, hqs: lambda EE: EE.dma_start(
                out=self.hq_dram[i * 128:(i + 1) * 128, o0:o0 + ow], in_=hqs))(i, o0, ow, hqs),
                reads=[hqs], writes=[self.hq_dram[i * 128:(i + 1) * 128, o0:o0 + ow]], dma=True)
            yield

    def emit_s1_gu(self):
        t = self.t
        xq_fm = self.arB.ap().rearrange("p (c n) -> p c n", n=NT)
        wE = self.arE.ap().rearrange("p (c n) -> p c n", n=512)
        for c in range(DC):
            t.op("scalar", (lambda c: lambda EE: EE.dma_start_transpose(
                xq_fm[:, c, :], self.xq_dram[:, c * 128:(c + 1) * 128]))(c),
                reads=[self.xq_dram], writes=[xq_fm[:, c, :]], dma=True)
        s1i = iter(self.s1_gen())
        s1_alive = [True]

        def step_s1():
            if s1_alive[0]:
                try:
                    next(s1i)
                except StopIteration:
                    s1_alive[0] = False

        OCS = [(0, 512), (512, 512), (1024, 512), (1536, 512), (2048, 512), (2560, 256)]
        for oci, (o0, ow) in enumerate(OCS):
            for c in range(DC):
                t.op("gpsimd", (lambda c, o0, ow: lambda EE: EE.dma_start(
                    out=wE[:, c, 0:ow], in_=self.wgu_out[c * 128:(c + 1) * 128, o0:o0 + ow]))(c, o0, ow),
                    reads=[self.wgu_out], writes=[wE[:, c, 0:ow]], dma=True)
                t.op("gpsimd", (lambda c, o0, ow: lambda EE: EE.dma_start(
                    out=wE[:, 8 + c, 0:ow], in_=self.wgu_out[c * 128:(c + 1) * 128, FI + o0:FI + o0 + ow]))(c, o0, ow),
                    reads=[self.wgu_out], writes=[wE[:, 8 + c, 0:ow]], dma=True)
            its = [iter(self.gu_tiles(0, o0, ow)), iter(self.gu_tiles(1, o0, ow))]
            alive = [True, True]
            while any(alive):
                step_s1()
                for i2, it in enumerate(its):
                    if alive[i2]:
                        try:
                            next(it)
                        except StopIteration:
                            alive[i2] = False
        while s1_alive[0]:
            step_s1()

    def emit_l1_down(self):
        t = self.t
        ps = self.ps
        KCH = FI // 128
        arC = self.arC.ap()
        arD = self.arD.ap()
        arF = self.arF.ap()
        arG = self.arG.ap()
        for tg in range(4):
            def pslot(t4, n):
                if t4 < 3:
                    return self.ps[t4 * 2 + n], self.ps[t4 * 2 + n][:]
                return self.pw, self.pw[:, n * 512:(n + 1) * 512]
            for kc in range(KCH):
                hqf = arC[:, (kc % 2) * 512:(kc % 2) * 512 + 512]
                wdb = arD[:, 8192 + (kc % 2) * 1024:8192 + (kc % 2) * 1024 + 1024]
                t.op("scalar", (lambda tg, kc, hqf: lambda EE: EE.dma_start_transpose(
                    hqf, self.hq_dram[tg * 512:(tg + 1) * 512, kc * 128:(kc + 1) * 128]))(tg, kc, hqf),
                    reads=[self.hq_dram[tg * 512:(tg + 1) * 512, kc * 128:(kc + 1) * 128]],
                    writes=[hqf], dma=True)
                t.op("gpsimd", (lambda kc, wdb: lambda EE: EE.dma_start(
                    out=wdb, in_=self.wd_out[kc * 128:(kc + 1) * 128, :]))(kc, wdb),
                    reads=[self.wd_out], writes=[wdb], dma=True)
                for t4 in range(4):
                    for n in range(2):
                        ph, pap = pslot(t4, n)
                        t.op("tensor", (lambda kc, t4, n, pap, hqf, wdb: lambda EE: EE.matmul(
                            pap, hqf[:, t4 * 128:(t4 + 1) * 128], wdb[:, n * 512:(n + 1) * 512],
                            start=(kc == 0), stop=(kc == KCH - 1)))(kc, t4, n, pap, hqf, wdb),
                            reads=[hqf[:, t4 * 128:(t4 + 1) * 128], wdb[:, n * 512:(n + 1) * 512]],
                            writes=[ph])
            for t4 in range(4):
                i = tg * 4 + t4
                x1s = arF[:, 0:1024]
                s1s = arF[:, 1024:1536]
                ys = arF[:, 1536:2560]
                gs = arG[:, 0:1024]
                t.op("sync", (lambda i, x1s: lambda EE: EE.dma_start(
                    out=x1s, in_=self.x1_dram[i * 128:(i + 1) * 128, :]))(i, x1s),
                    reads=[self.x1_dram[i * 128:(i + 1) * 128, :]], writes=[x1s], dma=True)
                t.op("sync", (lambda i, s1s: lambda EE: EE.dma_start(
                    out=s1s.bitcast(BF16), in_=self.s1_dram[i * 128:(i + 1) * 128, :]))(i, s1s),
                    reads=[self.s1_dram[i * 128:(i + 1) * 128, :]], writes=[s1s], dma=True)
                for n in range(2):
                    ph, pap = pslot(t4, n)
                    t.op("scalar", (lambda n, pap, gs: lambda EE: EE.activation(
                        out=gs[:, n * 512:(n + 1) * 512], in_=pap, func=AF.Silu))(n, pap, gs),
                        reads=[ph], writes=[gs[:, n * 512:(n + 1) * 512]])
                t.op("vector", (lambda gs, s1s: lambda EE: EE.tensor_tensor(
                    gs, gs, s1s.bitcast(BF16), A.mult))(gs, s1s),
                    reads=[gs, s1s], writes=[gs])
                t.op("vector", (lambda ys, x1s, gs: lambda EE: EE.tensor_tensor(
                    ys, x1s, gs, A.add))(ys, x1s, gs),
                    reads=[x1s, gs], writes=[ys])
                t.op("sync", (lambda i, ys: lambda EE: EE.dma_start(
                    out=self.y[i * 128:(i + 1) * 128, :], in_=ys))(i, ys),
                    reads=[ys], writes=[self.y[i * 128:(i + 1) * 128, :]], dma=True)


def make_consts():
    import ml_dtypes
    ii = np.eye(128, dtype=np.float32)
    io = np.arange(128, dtype=np.float32)
    return {
        "identf": ii,
        "identb": ii.astype(ml_dtypes.bfloat16),
        "ident2b": (2 * ii).astype(ml_dtypes.bfloat16),
        "strictU": (io[:, None] < io[None, :]).astype(np.float32),
        "trilxb": (io[:, None] > io[None, :]).astype(ml_dtypes.bfloat16),
        "triuib": (io[:, None] <= io[None, :]).astype(ml_dtypes.bfloat16),
        "iotap": io[:, None].copy(),
        "iotae": np.broadcast_to(np.arange(E, dtype=np.float32)[None, :], (128, E)).copy(),
        "iotag": np.broadcast_to(np.arange(8, dtype=np.float32)[None, :], (128, 8)).copy(),
        "onesc": np.ones((128, 1), np.float32),
        "actc": np.broadcast_to(np.array([1e30, -2.5e29, -7.5e29, -0.5, 0.5, 0, 0, 0], np.float32)[None, :], (128, 8)).copy(),
        "onescb": np.ones((128, 1), ml_dtypes.bfloat16),
        "ones1": np.ones((1, 128), np.float32),
    }


def stage_inputs(inputs):
    x = np.ascontiguousarray(np.asarray(inputs["L0_attn_x"], np.float32).reshape(T, D))
    w1, w2, w3 = inputs["L0_ffn_w1"], inputs["L0_ffn_w2"], inputs["L0_ffn_w3"]
    consts = make_consts()
    maps = []
    pidx = np.arange(128, dtype=np.uint32)
    for c in range(NCORE):
        x_tm = np.ascontiguousarray(x[c * NT:(c + 1) * NT])
        myh = (0, 1) if c % 2 == 0 else (2, 3)
        oth = (2, 3) if c % 2 == 0 else (0, 1)
        headperm = np.array([*myh, *oth])
        colperm = np.concatenate([np.arange(h * DK, (h + 1) * DK) for h in headperm])
        myhalf = c % 2
        qkvidx = np.zeros((128, 96), np.uint32)
        betaidx = np.zeros((128, 32), np.uint32)
        oidx = np.zeros((128, 32), np.uint32)
        for ch in range(NCH):
            half = ch // 16
            for tt in range(3):
                blk = tt * 2 + (0 if half == myhalf else 1)
                qkvidx[:, ch * 3 + tt] = (ch * 128 + pidx) * 6 + blk
            betaidx[:, ch] = (ch * 128 + pidx) * 2 + (0 if half == myhalf else 1)
        for hf in range(2):
            for ch in range(TT):
                oidx[:, hf * 16 + ch] = hf * L + myhalf * NT + ch * 128 + pidx
        m = {
            "x_tm": x_tm,
            "w0": np.asarray(inputs["L0_attn_W"], np.float32),
            "gwt": np.ascontiguousarray(np.asarray(inputs["L0_ffn_gate_weight"], np.float32).T),
            "ebb": np.ascontiguousarray(np.broadcast_to(
                np.asarray(inputs["L0_ffn_e_bias"], np.float32)[None, :], (128, E))),
            "w1t": np.ascontiguousarray(np.asarray(w1, np.float32)[c * EPC:(c + 1) * EPC].transpose(0, 2, 1)),
            "w3t": np.ascontiguousarray(np.asarray(w3, np.float32)[c * EPC:(c + 1) * EPC].transpose(0, 2, 1)),
            "w2t": np.ascontiguousarray(np.asarray(w2, np.float32)[c * EPC:(c + 1) * EPC].transpose(0, 2, 1)),
            "sgt": np.ascontiguousarray(np.asarray(inputs["L0_ffn_shared_gate"], np.float32).T),
            "sut": np.ascontiguousarray(np.asarray(inputs["L0_ffn_shared_up"], np.float32).T),
            "sdt": np.ascontiguousarray(np.asarray(inputs["L0_ffn_shared_down"], np.float32).T),
            "wq": np.ascontiguousarray(np.asarray(inputs["L1_attn_Wq"], np.float32)[:, colperm]),
            "wk": np.ascontiguousarray(np.asarray(inputs["L1_attn_Wk"], np.float32)[:, colperm]),
            "wv": np.ascontiguousarray(np.asarray(inputs["L1_attn_Wv"], np.float32)[:, colperm]),
            "wbeta": np.ascontiguousarray(np.asarray(inputs["L1_attn_Wbeta"], np.float32)[:, headperm]),
            "wo": np.asarray(inputs["L1_attn_Wo"], np.float32),
            "gnb": np.ascontiguousarray(np.broadcast_to(
                np.tile(np.asarray(inputs["L1_attn_g_norm"], np.float32), NH)[None, :], (128, D))),
            "wg_nat": np.ascontiguousarray(np.asarray(inputs["L1_ffn_w_gate"], np.float32)[:, c * 128:(c + 1) * 128]),
            "wu_nat": np.ascontiguousarray(np.asarray(inputs["L1_ffn_w_up"], np.float32)[:, c * 128:(c + 1) * 128]),
            "wd_nat": np.ascontiguousarray(np.asarray(inputs["L1_ffn_w_down"], np.float32)[:, c * (FI // 8):(c + 1) * (FI // 8)]),
            "prevm": (np.arange(NCORE) < c).astype(np.float32)[:, None],
            "qkvidx": qkvidx,
            "betaidx": betaidx,
            "oidx": oidx,
            "tokoff": np.full((128, 1), c * NT, np.float32),
        }
        m.update(consts)
        maps.append(m)
    return maps


_KB_CACHE = {}


def get_kb(taps=()):
    key = tuple(sorted(dict(taps).items()))
    if key not in _KB_CACHE:
        _KB_CACHE[key] = KB(dict(taps))
    return _KB_CACHE[key]


def kernel(**inputs):
    maps = stage_inputs(inputs)
    kb = get_kb()
    res = run_bass_kernel_spmd(kb.nc, maps, core_ids=list(range(NCORE)))
    out = np.concatenate([res.results[c]["y"] for c in range(NCORE)], axis=0)
    return out.reshape(B, L, D)
